# version 08
# baseline (speedup 1.0000x reference)
import numpy as np
import ml_dtypes
import concourse.bass as bass
import concourse.mybir as mybir
from concourse.bass_utils import run_bass_kernel_spmd
from contextlib import ExitStack

F32 = mybir.dt.float32
BF16 = mybir.dt.bfloat16
I32 = mybir.dt.int32
AF = mybir.ActivationFunctionType
ALU = mybir.AluOpType
AX = mybir.AxisListType

D = 4096
T = 2048
WU = 512
NWC = WU // 128
TPW = T + WU + 1
NFC = 32
NCH = 16
CLAMP = 80.0
NCORES = 8
ENGINES = ("sync", "vector", "scalar", "tensor", "gpsimd")

DEBUG_TAPS = []


class Prog:

    def __init__(self, nc):
        self.nc = nc
        self.q = {e: [] for e in ENGINES}
        self.all_waits = []

    def on(self, eng, fn, waits=(), inc=None, amt=1):
        if inc is not None:
            inc.n += amt
        n_after = inc.n if inc is not None else None
        self.q[eng].append((fn, tuple(waits), inc, amt))
        self.all_waits.extend(waits)
        return n_after

    def flush(self, block):
        for e in ENGINES:
            ops = self.q[e]
            if not ops:
                continue

            def body(eng, ops=ops):
                for (fn, waits, inc, amt) in ops:
                    for (s, v) in waits:
                        if v > 0:
                            eng.wait_ge(s.h, v)
                    ins = fn(eng)
                    if inc is not None and ins is not None:
                        ins.then_inc(inc.h, amt)
            getattr(block, e)(body)
        self.q = {e: [] for e in ENGINES}


def MM(t, out, lhsT, rhs, ob=0, kb=0, **kw):
    if 96 in (ob, kb):
        kw.setdefault("tile_position", (kb, ob))
    return t.matmul(out, lhsT, rhs, **kw)


def build_program():
    nc = bass.Bass("TRN2", target_bir_lowering=False)

    xp = nc.dram_tensor("xp", [TPW, D], F32, kind="ExternalInput").ap()
    WSLAB = {"k": 128, "r": 128, "v": 256, "g": 256, "o": 512}
    wT = {m: nc.dram_tensor(f"w{m}T", [D // WSLAB[m], 128, NFC, WSLAB[m]], F32,
                            kind="ExternalInput").ap()
          for m in ("k", "v", "r", "g", "o")}
    WQS = 64
    wfn = nc.dram_tensor("wfn", [D // WQS, 128, 32, WQS], F32,
                         kind="ExternalInput").ap()
    tmix = nc.dram_tensor("tmix", [4, D], F32, kind="ExternalInput").ap()
    tdec = nc.dram_tensor("tdec", [D], F32, kind="ExternalInput").ap()
    tfaa = nc.dram_tensor("tfaa", [D], F32, kind="ExternalInput").ap()
    gnwt = nc.dram_tensor("gnwt", [D], F32, kind="ExternalInput").ap()
    gnbt = nc.dram_tensor("gnbt", [D], F32, kind="ExternalInput").ap()
    fbi = nc.dram_tensor("fbi", [D], F32, kind="ExternalInput").ap()
    out = nc.dram_tensor("out", [T, D], F32, kind="ExternalOutput").ap()

    def internal(name, shape, dt):
        kind = "ExternalOutput" if name in DEBUG_TAPS else "Internal"
        return nc.dram_tensor(name, shape, dt, kind=kind).ap()

    xbf = internal("xbf", [D, TPW], BF16)
    kfh = internal("kfh", [NCH + NWC, D, 128], BF16)
    rfh = internal("rfh", [NCH, D, 128], BF16)
    vth = internal("vth", [T + WU, D], BF16)
    gth = internal("gth", [T, D], BF16)
    oh = internal("oh", [T, D], BF16)
    gxh = internal("gxh", [NCH, D, 128], BF16)
    arh = internal("arh", [T, D], F32)
    xlh = internal("xlh", [T, D], F32)
    xlbh = internal("xlbh", [NCH, D, 128], BF16)
    wqh = internal("wqh", [D // WQS, D, WQS], BF16)
    gxfd = internal("gxfd", [128, NFC * T], BF16) if "gxfd" in DEBUG_TAPS else None
    mmd = internal("mmd", [128, 512], F32) if "mmd" in DEBUG_TAPS else None
    ard = internal("ard", [T, D], F32) if "ard" in DEBUG_TAPS else None
    xld = internal("xld", [T, D], F32) if "xld" in DEBUG_TAPS else None
    xrd = internal("xrd", [128, 512], F32) if "xrd" in DEBUG_TAPS else None

    st = ExitStack()

    def sbuf(n, s, d=F32, stk=None):
        return (stk or st).enter_context(nc.sbuf_tensor(n, s, d))

    def psum_t(n, s, d=F32, stk=None):
        return (stk or st).enter_context(nc.psum_tensor(n, s, d))

    class Sem:
        def __init__(self, name):
            self.h = st.enter_context(nc.semaphore(name))
            self.n = 0

    with st:
        ident = sbuf("ident", [128, 128], BF16)
        Lall = sbuf("Lall", [128, NFC])
        tfall = sbuf("tfall", [128, NFC])
        eE = {k: sbuf(f"eE{k}", [128, NFC]) for k in (32, 64, 96, 128)}
        mixc = sbuf("mixc", [128, 4 * NFC])
        csc = sbuf("csc", [128, NFC])
        epsc = sbuf("epsc", [128, 1])
        nhc = sbuf("nhc", [128, 1])
        ii = sbuf("ii", [128, 128], I32)

        block = st.enter_context(nc.Block())
        P = Prog(nc)

        def bc(ap2d, span):
            return ap2d.rearrange("p (fc o) -> p fc o", o=1).broadcast_to(
                [128, NFC, span])

        s0 = Sem("s0")
        P.on("sync", lambda e: e.dma_start(
            out=Lall[:], in_=tdec.rearrange("(fc p) -> p fc", p=128),
            allow_slow_non_contiguous=True), inc=s0, amt=16)
        P.on("sync", lambda e: e.dma_start(
            out=tfall[:], in_=tfaa.rearrange("(fc p) -> p fc", p=128),
            allow_slow_non_contiguous=True), inc=s0, amt=16)
        P.on("sync", lambda e: e.dma_start(
            out=mixc[:].rearrange("p (m fc) -> p m fc", m=4),
            in_=tmix.rearrange("m (fc p) -> p m fc", p=128),
            allow_slow_non_contiguous=True), inc=s0, amt=16)
        P.on("gpsimd", lambda e: e.iota(ii[:], pattern=[[-1, 128]], base=0,
                                        channel_multiplier=1), inc=s0)
        P.on("vector", lambda e: e.memset(epsc[:], 1e-5), inc=s0)
        P.on("vector", lambda e: e.memset(nhc[:], -0.5), inc=s0)
        P.on("gpsimd", lambda e: e.tensor_single_scalar(
            ident[:], ii[:], 0, ALU.is_equal), waits=[(s0, s0.n)], inc=s0)
        P.on("scalar", lambda e: e.activation(csc[:], Lall[:], AF.Exp),
             waits=[(s0, 16)], inc=s0)
        P.on("vector", lambda e: e.tensor_scalar_mul(Lall[:], csc[:], -1.0),
             waits=[(s0, s0.n)], inc=s0)
        for k in (32, 64, 96, 128):
            P.on("vector", (lambda kk=k: lambda e: e.tensor_scalar_mul(
                csc[:], Lall[:], float(kk)))(), waits=[(s0, s0.n)], inc=s0)
            P.on("scalar", (lambda kk=k: lambda e: e.activation(
                eE[kk][:], csc[:], AF.Exp))(), waits=[(s0, s0.n)], inc=s0)
        s0_done = [(s0, s0.n)]

        with ExitStack() as stx:
            xt = [sbuf(f"xt{i}", [128, D], F32, stx) for i in range(2)]
            xb = sbuf("xb", [128, D], BF16, stx)
            xc = [sbuf(f"xc{i}", [128, D], BF16, stx) for i in range(2)]
            pX = [psum_t(f"pX{i}", [128, 1024], BF16, stx) for i in range(2)]
            sXin = [Sem("sXin0"), Sem("sXin1")]
            sXc = Sem("sXc")
            sXt = Sem("sXt")
            sXe = Sem("sXe")
            sXo = [Sem("sXo0"), Sem("sXo1")]
            NT = (TPW + 127) // 128

            QW = WQS
            qn = 32 * QW
            qw = [sbuf(f"qw{i}", [128, qn], F32, stx) for i in range(2)]
            tq = [sbuf(f"tq{i}", [128, qn], F32, stx) for i in range(2)]
            aq = [sbuf(f"aq{i}", [128, qn], F32, stx) for i in range(2)]
            b1 = [sbuf(f"b1{i}", [128, qn], F32, stx) for i in range(2)]
            b3 = [sbuf(f"b3{i}", [128, qn], F32, stx) for i in range(2)]
            sg = [sbuf(f"sg{i}", [128, qn], F32, stx) for i in range(2)]
            qo = [sbuf(f"qo{i}", [128, qn], BF16, stx) for i in range(2)]
            axq = [sbuf(f"axq{i}", [128, QW], F32, stx) for i in range(2)]
            eiq = [sbuf(f"eiq{i}", [128, QW], I32, stx) for i in range(2)]
            miq = [sbuf(f"miq{i}", [128, QW], I32, stx) for i in range(2)]
            scf = [sbuf(f"scf{i}", [128, QW], F32, stx) for i in range(2)]
            rsf = [sbuf(f"rsf{i}", [128, QW], F32, stx) for i in range(2)]
            sQin = [Sem("sQin0"), Sem("sQin1")]
            sQv = [Sem("sQv0"), Sem("sQv1")]
            sQp = [Sem("sQp0"), Sem("sQp1")]
            sQa = sQv
            sQo = [Sem("sQo0"), Sem("sQo1")]
            C1, C2, C3 = float(2 ** 22), float(2 ** 23), float(2 ** 24)
            NQS = D // QW

            def emit_x_tile(i):
                par = i % 2
                row0 = min(i * 128, TPW - 128)
                ldw = list(s0_done) if i < 2 else [(sXc, i - 1)]
                P.on("sync", (lambda par=par, row0=row0: lambda e: e.dma_start(
                    out=xt[par][:], in_=xp[row0:row0 + 128, :]))(),
                    waits=ldw, inc=sXin[par], amt=16)
                P.on("vector", (lambda par=par: lambda e: e.tensor_copy(
                    xb[:], xt[par][:]))(),
                    waits=[(sXin[par], 16 * (i // 2 + 1)), (sXt, 32 * i)],
                    inc=sXc)
                for fc in range(NFC):
                    g = 32 * i + fc
                    tp = fc % 2
                    tw = [(sXc, i + 1)]
                    if g >= 2:
                        tw.append((sXe, g - 1))
                    P.on("tensor", (lambda fc=fc, tp=tp: lambda e: e.transpose(
                        pX[tp][:, 0:128], xb[:, fc * 128:(fc + 1) * 128],
                        ident[:]))(), waits=tw, inc=sXt)
                    ew = [(sXt, g + 1)]
                    if fc == 0:
                        ew.append((sXo[par], 16 * (i // 2)))
                    P.on("scalar", (lambda fc=fc, tp=tp, par=par: lambda e: e.copy(
                        xc[par][:, fc * 128:(fc + 1) * 128], pX[tp][:, 0:128]))(),
                        waits=ew, inc=sXe)
                P.on("sync", (lambda par=par, row0=row0: lambda e: e.dma_start(
                    out=xbf[:, row0:row0 + 128].rearrange(
                        "(fc p) t -> p fc t", p=128),
                    in_=xc[par][:].rearrange("p (fc t) -> p fc t", t=128)))(),
                    waits=[(sXe, 32 * (i + 1))], inc=sXo[par], amt=16)

            Am_pool, Bm_act, Bm_dve, Cm_pool = {}, {}, {}, {}

            def emit_sq_A(s):
                par = s % 2
                sQ = sQv[par]
                qw3 = qw[par][:].rearrange("p (kr n) -> p kr n", kr=32)
                tq3 = tq[par][:].rearrange("p (kr n) -> p kr n", kr=32)
                rs_bc = rsf[par][:].rearrange("p (n o) -> p o n", o=1
                                              ).broadcast_to([128, 32, QW])
                ldw = [(sQp[par], Am_pool[s - 2])] if s >= 2 else list(s0_done)
                P.on("scalar", (lambda s=s, qw3=qw3: lambda e:
                                e.dma_start(out=qw3, in_=wfn[s]))(),
                    waits=ldw, inc=sQin[par], amt=16)
                P.on("vector", (lambda par=par: lambda e: e.reduce_max(
                    axq[par][:].rearrange("p (n o) -> p n o", o=1),
                    qw[par][:].rearrange("p (kr n) -> p n kr", n=QW),
                    AX.X, apply_absolute_value=True))(),
                    waits=[(sQin[par], 16 * (s // 2 + 1)), (sQv[par], sQv[par].n)],
                    inc=sQ)
                for fn in (
                    (lambda par=par: lambda e: e.tensor_scalar_max(
                        axq[par][:], axq[par][:], 1e-30))(),
                    (lambda par=par: lambda e: e.tensor_single_scalar(
                        eiq[par][:], axq[par][:].bitcast(I32), 23,
                        ALU.logical_shift_right))(),
                    (lambda par=par: lambda e: e.tensor_single_scalar(
                        miq[par][:], axq[par][:].bitcast(I32), 0x7FFFFF,
                        ALU.bitwise_and))(),
                    (lambda par=par: lambda e: e.tensor_single_scalar(
                        miq[par][:], miq[par][:], 0x400000, ALU.is_gt))(),
                    (lambda par=par: lambda e: e.tensor_tensor(
                        eiq[par][:], eiq[par][:], miq[par][:], ALU.add))(),
                    (lambda par=par: lambda e: e.tensor_single_scalar(
                        eiq[par][:], eiq[par][:], 2, ALU.subtract))(),
                ):
                    P.on("vector", fn, waits=[(sQ, sQ.n)], inc=sQ)
                scw = [(sQ, sQ.n)]
                if s >= 2:
                    scw.append((sQp[par], Cm_pool[s - 2]))
                P.on("vector", (lambda par=par: lambda e: e.tensor_single_scalar(
                    scf[par][:].bitcast(I32), eiq[par][:], 23,
                    ALU.logical_shift_left))(), waits=scw, inc=sQ)
                P.on("vector", (lambda par=par: lambda e: e.tensor_scalar(
                    miq[par][:], eiq[par][:], -1, 254, ALU.mult, ALU.add))(),
                    waits=[(sQ, sQ.n)], inc=sQ)
                P.on("vector", (lambda par=par: lambda e: e.tensor_single_scalar(
                    rsf[par][:].bitcast(I32), miq[par][:], 23,
                    ALU.logical_shift_left))(), waits=[(sQ, sQ.n)], inc=sQ)
                dve_scale = sQ.n
                tqw = [(sQv[par], dve_scale), (sQp[par], sQp[par].n)]
                if s >= 2:
                    tqw.append((sQv[par], dve_scale))
                    tqw.append((sQa[par], Bm_act[s - 2]))
                P.on("gpsimd", (lambda tq3=tq3, qw3=qw3, rs_bc=rs_bc: lambda e:
                                e.tensor_tensor(tq3, qw3, rs_bc, ALU.mult))(),
                    waits=tqw, inc=sQp[par])
                Am_pool[s] = sQp[par].n

            def emit_sq_B(s):
                par = s % 2
                sQ = sQv[par]
                abw = [(sQp[par], Am_pool[s]), (sQa[par], sQa[par].n)]
                if s >= 2:
                    abw.append((sQp[par], Cm_pool[s - 2]))
                P.on("scalar", (lambda par=par: lambda e: e.activation(
                    aq[par][:], tq[par][:], AF.Abs))(), waits=abw, inc=sQa[par])
                P.on("scalar", (lambda par=par: lambda e: e.activation(
                    sg[par][:], tq[par][:], AF.Sign))(),
                    waits=[(sQa[par], sQa[par].n)], inc=sQa[par])
                Bm_act[s] = sQa[par].n
                P.on("gpsimd", (lambda par=par: lambda e: e.tensor_scalar(
                    b1[par][:], aq[par][:], C1, C1, ALU.add, ALU.subtract))(),
                    waits=[(sQa[par], Bm_act[s] - 1), (sQp[par], sQp[par].n)],
                    inc=sQp[par])
                P.on("gpsimd", (lambda par=par: lambda e: e.tensor_scalar(
                    b3[par][:], aq[par][:], C3, C3, ALU.add, ALU.subtract))(),
                    waits=[(sQp[par], sQp[par].n)], inc=sQp[par])
                P.on("gpsimd", (lambda par=par: lambda e: e.tensor_scalar(
                    aq[par][:], aq[par][:], C2, C2, ALU.add, ALU.subtract))(),
                    waits=[(sQp[par], sQp[par].n)], inc=sQp[par])
                P.on("gpsimd", (lambda par=par: lambda e: e.tensor_scalar_min(
                    b1[par][:], b1[par][:], 2.0))(),
                    waits=[(sQp[par], sQp[par].n)], inc=sQp[par])
                P.on("gpsimd", (lambda par=par: lambda e: e.tensor_scalar(
                    aq[par][:], aq[par][:], 2.0, 0.0, ALU.subtract, ALU.max))(),
                    waits=[(sQp[par], sQp[par].n)], inc=sQp[par])
                P.on("gpsimd", (lambda par=par: lambda e: e.tensor_scalar(
                    b3[par][:], b3[par][:], 4.0, 0.0, ALU.subtract, ALU.max))(),
                    waits=[(sQp[par], sQp[par].n)], inc=sQp[par])
                Bm_dve[s] = sQp[par].n

            def emit_sq_C(s):
                par = s % 2
                sg3 = sg[par][:].rearrange("p (kr n) -> p kr n", kr=32)
                sc_bc = scf[par][:].rearrange("p (n o) -> p o n", o=1
                                              ).broadcast_to([128, 32, QW])
                P.on("gpsimd", (lambda par=par: lambda e: e.tensor_scalar_min(
                    aq[par][:], aq[par][:], 2.0))(),
                    waits=[(sQp[par], sQp[par].n)], inc=sQp[par])
                P.on("gpsimd", (lambda par=par: lambda e: e.tensor_add(
                    b1[par][:], b1[par][:], aq[par][:]))(),
                    waits=[(sQp[par], sQp[par].n)], inc=sQp[par])
                P.on("gpsimd", (lambda par=par: lambda e: e.tensor_scalar_min(
                    b3[par][:], b3[par][:], 2.0))(),
                    waits=[(sQp[par], sQp[par].n)], inc=sQp[par])
                P.on("gpsimd", (lambda par=par: lambda e: e.tensor_add(
                    b1[par][:], b1[par][:], b3[par][:]))(),
                    waits=[(sQp[par], sQp[par].n)], inc=sQp[par])
                P.on("gpsimd", (lambda sg3=sg3, sc_bc=sc_bc: lambda e:
                                e.tensor_tensor(sg3, sg3, sc_bc, ALU.mult))(),
                    waits=[(sQp[par], sQp[par].n)], inc=sQp[par])
                P.on("gpsimd", (lambda par=par, s=s: lambda e: e.tensor_mul(
                    qo[par][:], b1[par][:], sg[par][:]))(),
                    waits=[(sQp[par], sQp[par].n), (sQo[par], 16 * (s // 2))],
                    inc=sQp[par])
                Cm_pool[s] = sQp[par].n
                P.on("scalar", (lambda s=s, par=par: lambda e: e.dma_start(
                    out=wqh[s].rearrange("(kb kr) n -> kb kr n", kr=32),
                    in_=qo[par][:].rearrange("p (kr n) -> p kr n", kr=32)))(),
                    waits=[(sQp[par], Cm_pool[s])], inc=sQo[par], amt=16)

            for step in range(max(NT, NQS + 2)):
                if step < NT:
                    emit_x_tile(step)
                if 0 <= step - 2 < NQS:
                    emit_sq_C(step - 2)
                if 0 <= step - 1 < NQS:
                    emit_sq_B(step - 1)
                if step < NQS:
                    emit_sq_A(step)
            sx_done = [(sXo[0], sXo[0].n), (sXo[1], sXo[1].n)]
            sq_done = [(sQo[0], sQo[0].n), (sQo[1], sQo[1].n),
                       (sQp[0], sQp[0].n), (sQp[1], sQp[1].n)]
            P.flush(block)

        MIDX = {"k": 0, "v": 1, "r": 2, "g": 3}
        with ExitStack() as sta:
            mixf = sbuf("mixf", [128, NFC * T], BF16, sta)
            mixw = sbuf("mixw", [128, NFC * WU], BF16, sta)
            obf = [sbuf(f"obf{i}", [128, 512], BF16, sta) for i in range(4)]
            pP = [psum_t(f"pP{i}", [128, 512], F32, sta) for i in range(8)]
            sW = [Sem("sAW0"), Sem("sAW1")]
            sMMg = Sem("sAMMg")
            sEv = Sem("sAEv")
            sAOut = [Sem(f"sAOut{i}") for i in range(4)]
            sMx = Sem("sAMx")
            sXr = [Sem("sAXr0"), Sem("sAXr1")]
            ga = [0]
            wa = [0]

            def build_mix(m, waits0, tag, warm=False):
                base_mx = sMx.n
                opj = 4 if warm else 2
                with ExitStack() as stm:
                    xr = [sbuf(f"xr{tag}{i}", [128, TPW], BF16, stm)
                          for i in range(2)]
                    for jc in range(NFC):
                        par = jc % 2
                        ldw = list(waits0) if jc < 2 else \
                            [(sMx, base_mx + opj * (jc - 1))]
                        P.on("sync", (lambda jc=jc, par=par: lambda e:
                                      e.dma_start(out=xr[par][:],
                                                  in_=xbf[jc * 128:(jc + 1) * 128,
                                                          :]))(),
                            waits=ldw, inc=sXr[par], amt=16)
                        mfs = mixf[:, jc * T:(jc + 1) * T]
                        P.on("vector", (lambda par=par, mfs=mfs: lambda e:
                                        e.tensor_sub(mfs, xr[par][:, WU + 1:TPW],
                                                     xr[par][:, WU:WU + T]))(),
                            waits=[(sXr[par], sXr[par].n)] + list(waits0),
                            inc=sMx)
                        P.on("vector", (lambda jc=jc, par=par, m=m, mfs=mfs:
                                        lambda e: e.scalar_tensor_tensor(
                            mfs, mfs,
                            mixc[:, MIDX[m] * NFC + jc:MIDX[m] * NFC + jc + 1],
                            xr[par][:, WU:WU + T], ALU.mult, ALU.add))(),
                            waits=[(sMx, sMx.n)], inc=sMx)
                        if warm:
                            mws = mixw[:, jc * WU:(jc + 1) * WU]
                            P.on("vector", (lambda par=par, mws=mws: lambda e:
                                            e.tensor_sub(mws,
                                                         xr[par][:, 1:WU + 1],
                                                         xr[par][:, 0:WU]))(),
                                waits=[(sMx, sMx.n)], inc=sMx)
                            P.on("vector", (lambda jc=jc, par=par, m=m, mws=mws:
                                            lambda e: e.scalar_tensor_tensor(
                                mws, mws,
                                mixc[:, MIDX[m] * NFC + jc:
                                     MIDX[m] * NFC + jc + 1],
                                xr[par][:, 0:WU], ALU.mult, ALU.add))(),
                                waits=[(sMx, sMx.n)], inc=sMx)
                    P.flush(block)
                return [(sMx, sMx.n)]

            def proj(m, dst, fm, waits0, tag, act=None, warm=False):
                mix_done = build_mix(m, waits0, tag, warm=warm)
                with ExitStack() as stw:
                    wi = 128 if fm else 256
                    nslabs = NFC if fm else NFC // 2
                    wsl = [sbuf(f"wsl{tag}{i}", [128, NFC * wi], BF16, stw)
                           for i in range(2)]
                    if fm:
                        subs = [("w", s) for s in range(1 if warm else 0)] + \
                            [("m", s) for s in range(4)]
                    else:
                        subs = [("w", s) for s in range(NWC if warm else 0)] + \
                            [("m", s) for s in range(NCH)]
                    row_off = WU if (not fm and warm) else 0
                    for ns in range(nslabs):
                        par = wa[0] % 2
                        wldw = []
                        if wa[0] >= 2:
                            wldw = [(sMMg, slab_gdone[wa[0] - 2])]
                        if wa[0] == 0:
                            wldw += list(sq_done)
                        if ns < 2:
                            wldw += list(mix_done)
                        P.on("gpsimd", (lambda ns=ns, par=par, m=m, wi=wi:
                                        lambda e: e.dma_start(
                            out=wsl[par][:].rearrange("p (jc i) -> p jc i", i=wi),
                            in_=wT[m][ns]))(),
                            waits=wldw, inc=sW[par], amt=16)
                        wcount = sW[par].n
                        for (kind, sub) in subs:
                            g = ga[0]
                            mmw = [(sW[par], wcount)]
                            if g >= 8:
                                mmw.append((sEv, g - 7))
                            if (kind, sub) == subs[0]:
                                mmw += mix_done
                            for jc in range(NFC):
                                if fm:
                                    rhs_ap = (mixw[:, jc * WU:jc * WU + 512]
                                              if kind == "w" else
                                              mixf[:, jc * T + sub * 512:
                                                   jc * T + sub * 512 + 512])
                                    fn = (lambda jc=jc, par=par, g=g,
                                          rhs_ap=rhs_ap: lambda e: e.matmul(
                                        pP[g % 8][:],
                                        wsl[par][:, jc * 128:(jc + 1) * 128],
                                        rhs_ap,
                                        start=(jc == 0), stop=(jc == NFC - 1)))()
                                else:
                                    lhs_ap = (mixw[:, jc * WU + sub * 128:
                                                   jc * WU + sub * 128 + 128]
                                              if kind == "w" else
                                              mixf[:, jc * T + sub * 128:
                                                   jc * T + sub * 128 + 128])
                                    fn = (lambda jc=jc, par=par, g=g, wi=wi,
                                          lhs_ap=lhs_ap: lambda e: e.matmul(
                                        pP[g % 8][:, 0:wi],
                                        lhs_ap,
                                        wsl[par][:, jc * wi:(jc + 1) * wi],
                                        start=(jc == 0), stop=(jc == NFC - 1)))()
                                P.on("tensor", fn,
                                     waits=(mmw if jc == 0 else []),
                                     inc=(sMMg if jc == NFC - 1 else None))
                            evw = [(sMMg, sMMg.n)]
                            if g >= 4:
                                evw.append((sAOut[g % 4], 16 * (g // 4)))
                            wid = 512 if fm else wi
                            if act is None:
                                P.on("scalar", (lambda g=g, wid=wid: lambda e:
                                                e.copy(obf[g % 4][:, 0:wid],
                                                       pP[g % 8][:, 0:wid]))(),
                                    waits=evw, inc=sEv)
                            else:
                                P.on("scalar", (lambda g=g, act=act, wid=wid:
                                                lambda e: e.activation(
                                    obf[g % 4][:, 0:wid],
                                    pP[g % 8][:, 0:wid], act))(),
                                     waits=evw, inc=sEv)
                            if fm:
                                c0 = (sub * 4 if kind == "w" else
                                      NWC * (1 if warm else 0) + 4 * sub)
                                st_fn = (lambda ns=ns, c0=c0, g=g: lambda e:
                                         e.dma_start(
                                    out=dst[c0:c0 + 4,
                                            ns * 128:(ns + 1) * 128,
                                            :].rearrange("c p t -> p c t"),
                                    in_=obf[g % 4][:].rearrange(
                                        "p (c t) -> p c t", t=128)))()
                            else:
                                r0_ = (sub * 128 if kind == "w" else
                                       row_off + sub * 128)
                                st_fn = (lambda ns=ns, r0_=r0_, g=g, wi=wi:
                                         lambda e: e.dma_start(
                                    out=dst[r0_:r0_ + 128,
                                            ns * wi:(ns + 1) * wi],
                                    in_=obf[g % 4][:, 0:wi]))()
                            P.on("scalar", st_fn, waits=[(sEv, sEv.n)],
                                 inc=sAOut[g % 4], amt=16)
                            ga[0] += 1
                        slab_gdone.append(sMMg.n)
                        wa[0] += 1
                    P.flush(block)
                return [(sMMg, sMMg.n)]

            slab_gdone = []
            d1 = proj("k", kfh, True, sx_done + sq_done, "k", warm=True)
            d2 = proj("r", rfh, True, d1, "r")
            d3 = proj("v", vth, False, d2, "v", warm=True)
            d4 = proj("g", gth, False, d3, "g", act=AF.Silu)
            sa_done = [(sAOut[i], sAOut[i].n) for i in range(4)] + \
                [(sEv, sEv.n)]
            P.flush(block)

        with ExitStack() as stb:
            tb = {n: sbuf(f"tb_{n}", [128, NFC * 128], BF16, stb)
                  for n in ("rt16c", "kt16c", "rt16u", "kt16u", "rt32", "kt32e")}
            bRu = sbuf("bRu", [128, NFC * 128], BF16, stb)
            bRc = sbuf("bRc", [128, NFC * 128], BF16, stb)
            bR32 = sbuf("bR32", [128, NFC * 128], BF16, stb)
            bKu = sbuf("bKu", [128, NFC * 128], BF16, stb)
            bKc = sbuf("bKc", [128, NFC * 128], BF16, stb)
            bKe = sbuf("bKe", [128, NFC * 128], BF16, stb)
            kst = {G: sbuf(f"kst{G}", [128, NFC * 32 * G], BF16, stb)
                   for G in (1, 2, 3)}
            asb = sbuf("asb", [128, 64 * 128], BF16, stb)
            stf = sbuf("stf", [128, NFC * 64], F32, stb)
            stb16 = sbuf("stb16", [128, NFC * 64], BF16, stb)
            cq = sbuf("cq", [128, 64], F32, stb)

            pA = [psum_t("pA0", [128, 512], F32, stb),
                  psum_t("pA1", [128, 512], F32, stb)]
            pT = [psum_t("pT0", [128, 1024], BF16, stb),
                  psum_t("pT1", [128, 1024], BF16, stb)]
            pO = psum_t("pO", [128, 1024], F32, stb)
            pS = psum_t("pS", [128, 1024], F32, stb)

            sTab = Sem("sTab")
            sBk = [Sem("sBk0"), Sem("sBk1")]
            sBr = [Sem("sBr0"), Sem("sBr1")]
            sBv = [Sem("sBv0"), Sem("sBv1")]
            sPre = Sem("sPre")
            sAsm = Sem("sAsm")
            sEvA = Sem("sEvA")
            sMsk = Sem("sMsk")
            sDrv = Sem("sDrv")
            sTpw = Sem("sTpw")
            sTpk = Sem("sTpk")
            sTrW = Sem("sTrW")
            sTrK = Sem("sTrK")
            sCc = Sem("sCc")
            sO = Sem("sO")
            sS = Sem("sS")
            sOEv = Sem("sOEv")
            sStU = Sem("sStU")
            sStC = Sem("sStC")
            sObo = Sem("sObo")
            sWk = [Sem("sWk0"), Sem("sWk1")]
            sWv = [Sem("sWv0"), Sem("sWv1")]
            sWpre = Sem("sWpre")
            sWtpk = Sem("sWtpk")
            sWtrk = Sem("sWtrk")
            sWs = Sem("sWs")
            sWstU = Sem("sWstU")

            Lb = bc(Lall[:], 128)
            sttab = ExitStack()
            itb = sbuf("itb", [128, NFC * 128], I32, sttab)
            ftb = sbuf("ftb", [128, NFC * 128], F32, sttab)
            f3 = ftb[:].rearrange("p (fc t) -> p fc t", t=128)
            for (nm, base, pat, clamp) in (
                ("rt16c", -8, [[0, NFC], [0, 8], [1, 16]], True),
                ("kt16c", 7, [[0, NFC], [0, 8], [-1, 16]], True),
                ("rt16u", 0, [[0, NFC], [0, 8], [1, 16]], False),
                ("kt16u", 15, [[0, NFC], [0, 8], [-1, 16]], False),
                ("rt32", 0, [[0, NFC], [0, 4], [1, 32]], False),
                ("kt32e", 31, [[0, NFC], [0, 4], [-1, 32]], False),
            ):
                P.on("gpsimd", (lambda pat=pat, base=base: lambda e: e.iota(
                    itb[:], pattern=pat, base=base, channel_multiplier=0))(),
                    waits=[(sTab, sTab.n)] +
                          (list(s0_done) + sa_done if nm == "rt16c" else []),
                    inc=sTab)
                P.on("vector", lambda e: e.tensor_copy(ftb[:], itb[:]),
                     waits=[(sTab, sTab.n)], inc=sTab)
                P.on("vector", lambda e: e.tensor_tensor(f3, f3, Lb, ALU.mult),
                     waits=[(sTab, sTab.n)], inc=sTab)
                if clamp:
                    P.on("vector", lambda e: e.tensor_scalar_min(
                        ftb[:], ftb[:], CLAMP), waits=[(sTab, sTab.n)], inc=sTab)
                P.on("scalar", (lambda nm=nm: lambda e: e.activation(
                    tb[nm][:], ftb[:], AF.Exp))(), waits=[(sTab, sTab.n)],
                    inc=sTab)
            P.on("vector", lambda e: e.memset(stf[:], 0.0),
                 waits=[(sTab, sTab.n)], inc=sTab)
            P.on("vector", lambda e: e.memset(stb16[:], 0.0),
                 waits=[(sTab, sTab.n)], inc=sTab)
            tab_done = sTab.n
            P.flush(block)
            sttab.close()
            kch = [sbuf(f"kch{i}", [128, NFC * 128], BF16, stb) for i in range(2)]
            rch = [sbuf(f"rch{i}", [128, NFC * 128], BF16, stb) for i in range(2)]
            vch = [sbuf(f"vch{i}", [128, D], BF16, stb) for i in range(2)]
            obch = sbuf("obch", [128, D], BF16, stb)

            E32b = lambda span: bc(eE[32][:], span)
            E64b = lambda span: bc(eE[64][:], span)
            E96b = lambda span: bc(eE[96][:], span)
            tfb = bc(tfall[:], 128)

            def v3(buf, span=128):
                return buf[:].rearrange("p (fc t) -> p fc t", t=span)

            for wci in range(NWC):
                par = wci % 2
                ldw = []
                if wci == 0:
                    ldw = sa_done + [(sTab, tab_done)]
                elif wci >= 2:
                    ldw = [(sWs, 64 * (wci - 1)), (sWpre, 5 * (wci - 1))]
                P.on("sync", (lambda wci=wci, par=par: lambda e: e.dma_start(
                    out=v3(kch[par]),
                    in_=kfh[wci].rearrange("(fc p) t -> p fc t", p=128)))(),
                    waits=ldw, inc=sWk[par], amt=16)
                P.on("sync", (lambda wci=wci, par=par: lambda e: e.dma_start(
                    out=vch[par][:],
                    in_=vth[wci * 128:(wci + 1) * 128, :]))(),
                    inc=sWv[par], amt=16)
                w1 = [(sWk[par], 16 * (wci // 2 + 1)), (sTab, tab_done)]
                if wci > 0:
                    w1 += [(sWtpk, 32 * wci), (sWs, 64 * wci)]
                kev = v3(bKe)
                kcv = v3(bKc)
                P.on("vector", (lambda par=par: lambda e: e.tensor_mul(
                    bKe[:], kch[par][:], tb["kt32e"][:]))(), waits=w1, inc=sWpre)
                P.on("vector", lambda e: e.tensor_copy(
                    kcv[:, :, 96:128], kev[:, :, 96:128]),
                    waits=[(sWpre, sWpre.n)], inc=sWpre)
                for (g, Eb) in ((0, E96b), (1, E64b), (2, E32b)):
                    P.on("vector", (lambda g=g, Eb=Eb: lambda e: e.tensor_tensor(
                        kcv[:, :, 32 * g:32 * g + 32],
                        kev[:, :, 32 * g:32 * g + 32], Eb(32), ALU.mult))(),
                        inc=sWpre)
                wdrv = sWpre.n
                for fc in range(NFC):
                    gg = 32 * wci + fc
                    tp = fc % 2
                    tw = [(sWpre, wdrv)] if fc == 0 else []
                    if gg >= 2:
                        tw.append((sWtrk, gg - 1))
                    P.on("tensor", (lambda fc=fc, tp=tp: lambda e: e.transpose(
                        pT[tp][:, 128:256], bKc[:, fc * 128:(fc + 1) * 128],
                        ident[:]))(), waits=tw, inc=sWtpk)
                    P.on("scalar", (lambda fc=fc, tp=tp: lambda e: e.copy(
                        bKe[:, fc * 128:(fc + 1) * 128],
                        pT[tp][:, 128:256]))(),
                        waits=[(sWtpk, gg + 1)], inc=sWtrk)
                for h in range(64):
                    fc, hp = h // 2, h % 2
                    pb = 64 * hp
                    w4 = []
                    if h == 0:
                        w4 = [(sWtrk, 32 * (wci + 1)), (sWstU, 2 * wci)]
                    if h == 32:
                        w4.append((sWstU, 2 * wci + 1))
                    P.on("tensor", (lambda fc=fc, pb=pb, h=h, par=par:
                                    lambda e: MM(e,
                        pS[pb:pb + 64, (fc % 16) * 64:(fc % 16) * 64 + 64],
                        bKe[:, fc * 128 + pb:fc * 128 + pb + 64],
                        vch[par][:, h * 64:(h + 1) * 64],
                        ob=pb, kb=0, start=True, stop=True))(),
                        waits=w4, inc=sWs)
                    if h in (31, 63):
                        half = 0 if h == 31 else 1
                        hs = half * 1024
                        stv = stf[:, hs:hs + 1024].rearrange(
                            "p (fc d) -> p fc d", d=64)
                        e128h = eE[128][:, half * 16:half * 16 + 16].rearrange(
                            "p (fc o) -> p fc o", o=1).broadcast_to(
                            [128, 16, 64])
                        wv = [(sWs, 64 * wci + 32 * (half + 1)),
                              (sWv[par], 16 * (wci // 2 + 1))]
                        P.on("vector", (lambda stv=stv, e128h=e128h: lambda e:
                                        e.tensor_tensor(stv, stv, e128h,
                                                        ALU.mult))(),
                            waits=wv, inc=sStC)
                        P.on("vector", (lambda hs=hs: lambda e: e.tensor_add(
                            stf[:, hs:hs + 1024], stf[:, hs:hs + 1024],
                            pS[:]))(), waits=[(sStC, sStC.n)], inc=sStC)
                        P.on("vector", (lambda hs=hs: lambda e: e.tensor_copy(
                            stb16[:, hs:hs + 1024], stf[:, hs:hs + 1024]))(),
                            waits=[(sStC, sStC.n)], inc=sWstU)
            warm_done = [(sWstU, sWstU.n), (sWs, sWs.n), (sWpre, sWpre.n),
                         (sWtpk, sWtpk.n), (sWtrk, sWtrk.n)]

            preM, drvM, ccM = [], [], []
            for ci in range(NCH):
                par = ci % 2
                ldw = []
                if ci == 0:
                    ldw = list(warm_done)
                elif ci >= 2:
                    ldw = [(sDrv, drvM[ci - 1]), (sS, 64 * (ci - 1)),
                           (sCc, ccM[ci - 1])]
                P.on("sync", (lambda ci=ci, par=par: lambda e: e.dma_start(
                    out=v3(kch[par]),
                    in_=kfh[NWC + ci].rearrange("(fc p) t -> p fc t",
                                                p=128)))(),
                    waits=ldw, inc=sBk[par], amt=16)
                P.on("sync", (lambda ci=ci, par=par: lambda e: e.dma_start(
                    out=v3(rch[par]),
                    in_=rfh[ci].rearrange("(fc p) t -> p fc t", p=128)))(),
                    inc=sBr[par], amt=16)
                P.on("sync", (lambda ci=ci, par=par: lambda e: e.dma_start(
                    out=vch[par][:],
                    in_=vth[WU + ci * 128:WU + (ci + 1) * 128, :]))(),
                    inc=sBv[par], amt=16)

                p1w = [(sBk[par], 16 * (ci // 2 + 1)),
                       (sBr[par], 16 * (ci // 2 + 1)), (sTab, tab_done)]
                if ci == 0:
                    p1w += list(warm_done)
                if ci > 0:
                    p1w += [(sO, 64 * ci), (sS, 64 * ci), (sCc, ccM[ci - 1]),
                            (sOEv, 4 * ci)]
                P.on("vector", (lambda par=par: lambda e: e.tensor_mul(
                    bRu[:], rch[par][:], tb["rt16u"][:]))(), waits=p1w, inc=sPre)
                for (dstb, srcb, tbl) in ((bRc, rch, "rt16c"), (bR32, rch, "rt32"),
                                          (bKu, kch, "kt16u"), (bKc, kch, "kt16c"),
                                          (bKe, kch, "kt32e")):
                    P.on("vector", (lambda dstb=dstb, srcb=srcb, tbl=tbl, par=par:
                                    lambda e: e.tensor_mul(
                        dstb[:], srcb[par][:], tb[tbl][:]))(), inc=sPre)
                ke3 = v3(bKe)

                def ksl(G, lo, hi):
                    return kst[G][:].rearrange("p (fc j) -> p fc j",
                                               j=32 * G)[:, :, lo:hi]
                P.on("vector", lambda e: e.tensor_copy(ksl(1, 0, 32),
                                                       ke3[:, :, 0:32]),
                     waits=[(sPre, sPre.n)], inc=sPre)
                P.on("vector", lambda e: e.tensor_tensor(
                    ksl(2, 0, 32), ke3[:, :, 0:32], E32b(32), ALU.mult), inc=sPre)
                P.on("vector", lambda e: e.tensor_copy(ksl(2, 32, 64),
                                                       ke3[:, :, 32:64]), inc=sPre)
                P.on("vector", lambda e: e.tensor_tensor(
                    ksl(3, 0, 32), ke3[:, :, 0:32], E64b(32), ALU.mult), inc=sPre)
                P.on("vector", lambda e: e.tensor_tensor(
                    ksl(3, 32, 64), ke3[:, :, 32:64], E32b(32), ALU.mult),
                    inc=sPre)
                P.on("vector", lambda e: e.tensor_copy(ksl(3, 64, 96),
                                                       ke3[:, :, 64:96]), inc=sPre)
                preM.append(sPre.n)

                for h in range(64):
                    fc, hp = h // 2, h % 2
                    pb = 64 * hp
                    ap = h % 2
                    w2 = []
                    if h == 0:
                        w2.append((sPre, preM[ci]))
                    if 64 * ci + h >= 2:
                        w2.append((sEvA, 64 * ci + h - 1))
                    first = True
                    for G in range(4):
                        c0 = 32 * G
                        co = fc * 128 + c0
                        P.on("tensor", (lambda pb=pb, ap=ap, c0=c0, co=co:
                                        lambda e: MM(e,
                            pA[ap][c0:c0 + 32, c0:c0 + 32],
                            bKc[pb:pb + 64, co:co + 32],
                            bRc[pb:pb + 64, co:co + 32],
                            ob=c0, kb=pb, start=True, stop=True))(),
                            waits=(w2 if first else []))
                        first = False
                        P.on("tensor", (lambda pb=pb, ap=ap, c0=c0, co=co:
                                        lambda e: MM(e,
                            pA[ap][c0:c0 + 16, c0 + 16:c0 + 32],
                            bKu[pb:pb + 64, co:co + 16],
                            bRu[pb:pb + 64, co + 16:co + 32],
                            ob=c0, kb=pb, start=True, stop=True))())
                        if G > 0:
                            P.on("tensor", (lambda fc=fc, pb=pb, ap=ap, G=G,
                                            c0=c0, co=co: lambda e: MM(e,
                                pA[ap][0:32 * G, c0:c0 + 32],
                                kst[G][pb:pb + 64,
                                       fc * 32 * G:(fc + 1) * 32 * G],
                                bR32[pb:pb + 64, co:co + 32],
                                start=True, stop=True))(),
                                inc=(sAsm if G == 3 else None))
                    evw = [(sAsm, 64 * ci + h + 1)]
                    if h == 0 and ci > 0:
                        evw.append((sO, 64 * ci))
                    P.on("scalar", (lambda h=h, ap=ap: lambda e: e.copy(
                        asb[:, h * 128:(h + 1) * 128], pA[ap][:, 0:128]))(),
                        waits=evw, inc=sEvA)
                    P.on("gpsimd", (lambda h=h: lambda e: e.affine_select(
                        asb[:, h * 128:(h + 1) * 128],
                        asb[:, h * 128:(h + 1) * 128],
                        pattern=[[1, 128]], compare_op=ALU.is_ge, fill=0.0,
                        base=-1, channel_multiplier=-1))(),
                        waits=[(sEvA, 64 * ci + h + 1)], inc=sMsk)

                r32v, rcv = v3(bR32), v3(bRc)
                kev, kcv = v3(bKe), v3(bKc)
                P.on("vector", lambda e: e.tensor_copy(
                    rcv[:, :, 0:32], r32v[:, :, 0:32]),
                    waits=[(sAsm, 64 * (ci + 1))], inc=sDrv)
                for (g, Eb) in ((1, E32b), (2, E64b), (3, E96b)):
                    P.on("vector", (lambda g=g, Eb=Eb: lambda e: e.tensor_tensor(
                        rcv[:, :, 32 * g:32 * g + 32],
                        r32v[:, :, 32 * g:32 * g + 32], Eb(32), ALU.mult))(),
                        inc=sDrv)
                P.on("vector", lambda e: e.tensor_copy(
                    kcv[:, :, 96:128], kev[:, :, 96:128]), inc=sDrv)
                for (g, Eb) in ((0, E96b), (1, E64b), (2, E32b)):
                    P.on("vector", (lambda g=g, Eb=Eb: lambda e: e.tensor_tensor(
                        kcv[:, :, 32 * g:32 * g + 32],
                        kev[:, :, 32 * g:32 * g + 32], Eb(32), ALU.mult))(),
                        inc=sDrv)
                P.on("vector", (lambda par=par: lambda e: e.tensor_tensor(
                    v3(bKu), v3(rch[par]), tfb, ALU.mult))(), inc=sDrv)
                P.on("vector", (lambda par=par: lambda e: e.tensor_mul(
                    bRu[:], bKu[:], kch[par][:]))(),
                    waits=[(sDrv, sDrv.n)], inc=sDrv)
                drvM.append(sDrv.n)

                for fc in range(NFC):
                    gg = 32 * ci + fc
                    tp = fc % 2
                    tw = [(sDrv, drvM[ci])] if fc == 0 else []
                    if gg >= 2:
                        tw.append((sTrW, gg - 1))
                    if fc < 2 and ci > 0:
                        tw.append((sTrK, 32 * ci))
                    if fc < 2 and ci == 0:
                        tw.append((sWtrk, 32 * NWC))
                    P.on("tensor", (lambda fc=fc, tp=tp: lambda e: e.transpose(
                        pT[tp][:, 0:128], bRu[:, fc * 128:(fc + 1) * 128],
                        ident[:]))(), waits=tw, inc=sTpw)
                    P.on("scalar", (lambda fc=fc, tp=tp: lambda e: e.copy(
                        bKu[:, fc * 128:(fc + 1) * 128], pT[tp][:, 0:128]))(),
                        waits=[(sTpw, gg + 1)], inc=sTrW)
                for fc in range(NFC):
                    gg = 32 * ci + fc
                    tp = fc % 2
                    tw = [(sTrK, gg - 1)] if gg >= 2 else []
                    if fc < 2:
                        tw.append((sTrW, 32 * (ci + 1)))
                    P.on("tensor", (lambda fc=fc, tp=tp: lambda e: e.transpose(
                        pT[tp][:, 128:256], bKc[:, fc * 128:(fc + 1) * 128],
                        ident[:]))(), waits=tw, inc=sTpk)
                    P.on("scalar", (lambda fc=fc, tp=tp: lambda e: e.copy(
                        bKe[:, fc * 128:(fc + 1) * 128], pT[tp][:, 128:256]))(),
                        waits=[(sTpk, gg + 1)], inc=sTrK)

                P.on("vector", lambda e: e.reduce_sum(
                    cq[:].rearrange("p (h o) -> p h o", o=1),
                    bKu[:].rearrange("p (h d) -> p h d", d=64), AX.X),
                    waits=[(sTrW, 32 * (ci + 1))], inc=sCc)
                cqb = cq[:].rearrange("p (h o) -> p h o", o=1).broadcast_to(
                    [128, 64, 64])
                P.on("vector", (lambda par=par: lambda e: e.tensor_tensor(
                    bKc[:].rearrange("p (h d) -> p h d", d=64),
                    vch[par][:].rearrange("p (h d) -> p h d", d=64),
                    cqb, ALU.mult))(),
                    waits=[(sCc, sCc.n), (sTrK, 32 * (ci + 1)),
                           (sBv[par], 16 * (ci // 2 + 1))], inc=sCc)
                ccM.append(sCc.n)

                for h in range(64):
                    fc, hp = h // 2, h % 2
                    pb = 64 * hp
                    grp = h // 16
                    w4 = [(sMsk, 64 * ci + h + 1)]
                    if h == 0:
                        w4 += [(sTrK, 32 * (ci + 1)), (sStU, 2 * ci)]
                        if ci == 0:
                            w4.append((sWstU, 2 * NWC))
                    if h % 16 == 0 and (ci > 0 or grp > 0):
                        w4.append((sOEv, 4 * ci + grp))
                    if h == 32:
                        w4.append((sStU, 2 * ci + 1))
                    P.on("tensor", (lambda h=h, par=par: lambda e: e.matmul(
                        pO[:, (h % 16) * 64:(h % 16) * 64 + 64],
                        asb[:, h * 128:(h + 1) * 128],
                        vch[par][:, h * 64:(h + 1) * 64],
                        start=True, stop=False))(), waits=w4)
                    P.on("tensor", (lambda fc=fc, pb=pb, h=h: lambda e: e.matmul(
                        pO[:, (h % 16) * 64:(h % 16) * 64 + 64],
                        bRc[pb:pb + 64, fc * 128:(fc + 1) * 128],
                        stb16[pb:pb + 64, fc * 64:(fc + 1) * 64],
                        start=False, stop=True))(), inc=sO)
                    P.on("tensor", (lambda fc=fc, pb=pb, h=h, par=par:
                                    lambda e: MM(e,
                        pS[pb:pb + 64, (fc % 16) * 64:(fc % 16) * 64 + 64],
                        bKe[:, fc * 128 + pb:fc * 128 + pb + 64],
                        vch[par][:, h * 64:(h + 1) * 64],
                        ob=pb, kb=0, start=True, stop=True))(), inc=sS)
                    if h % 16 == 15:
                        evw = [(sO, 64 * ci + 16 * (grp + 1)), (sCc, ccM[ci])]
                        if grp == 0 and ci > 0:
                            evw.append((sObo, 16 * ci))
                        P.on("vector", (lambda grp=grp: lambda e:
                                        e.scalar_tensor_tensor(
                            obch[:, grp * 1024:(grp + 1) * 1024],
                            bKc[:, grp * 1024:(grp + 1) * 1024], 1.0,
                            pO[:], ALU.bypass, ALU.add))(), waits=evw, inc=sOEv)
                    if h in (31, 63):
                        half = 0 if h == 31 else 1
                        hs = half * 1024
                        stv = stf[:, hs:hs + 1024].rearrange(
                            "p (fc d) -> p fc d", d=64)
                        e128h = eE[128][:, half * 16:half * 16 + 16].rearrange(
                            "p (fc o) -> p fc o", o=1).broadcast_to([128, 16, 64])
                        P.on("vector", (lambda stv=stv, e128h=e128h: lambda e:
                                        e.tensor_tensor(stv, stv, e128h,
                                                        ALU.mult))(),
                            waits=[(sS, 64 * ci + 32 * (half + 1)),
                                   (sO, 64 * ci + 32 * (half + 1))], inc=sStC)
                        P.on("vector", (lambda hs=hs: lambda e: e.tensor_add(
                            stf[:, hs:hs + 1024], stf[:, hs:hs + 1024], pS[:]))(),
                            waits=[(sStC, sStC.n)], inc=sStC)
                        P.on("vector", (lambda hs=hs: lambda e: e.tensor_copy(
                            stb16[:, hs:hs + 1024], stf[:, hs:hs + 1024]))(),
                            waits=[(sStC, sStC.n)], inc=sStU)
                P.on("sync", (lambda ci=ci: lambda e: e.dma_start(
                    out=oh[ci * 128:(ci + 1) * 128, :], in_=obch[:]))(),
                    waits=[(sOEv, 4 * (ci + 1))], inc=sObo, amt=16)
            sb_done = [(sObo, sObo.n), (sStU, sStU.n), (sMsk, sMsk.n)]
            P.flush(block)

        with ExitStack() as stc:
            och = [sbuf(f"och{i}", [128, D], BF16, stc) for i in range(2)]
            gch = [sbuf(f"gch{i}", [128, D], BF16, stc) for i in range(2)]
            u1 = sbuf("u1", [128, D], F32, stc)
            u2 = sbuf("u2", [128, D], F32, stc)
            gx = sbuf("gx", [128, D], BF16, stc)
            gxc = [sbuf(f"gxc{i}", [128, D], BF16, stc) for i in range(2)]
            gnwb = sbuf("gnwb", [128, D], F32, stc)
            gnbb = sbuf("gnbb", [128, D], F32, stc)
            mu = sbuf("mu", [128, 64], F32, stc)
            vr = sbuf("vr", [128, 64], F32, stc)
            m2 = sbuf("m2", [128, 64], F32, stc)
            pC = [psum_t(f"pC{i}", [128, 1024], BF16, stc) for i in range(2)]
            sCin = [Sem("sCin0"), Sem("sCin1")]
            sCg = [Sem("sCg0"), Sem("sCg1")]
            sCs = Sem("sCs")
            sCt = Sem("sCt")
            sCe = Sem("sCe")
            sCo = [Sem("sCo0"), Sem("sCo1")]
            sCb = Sem("sCb")

            for (src_row, dst_bc) in ((gnwt, gnwb), (gnbt, gnbb)):
                P.on("sync", (lambda src_row=src_row, dst_bc=dst_bc: lambda e:
                              e.dma_start(
                    out=dst_bc[:],
                    in_=src_row.rearrange("(o d) -> o d", o=1).broadcast_to(
                        [128, D])))(), waits=sb_done, inc=sCb, amt=16)
            cb_done = sCb.n

            csM = []
            for tc in range(NCH):
                par = tc % 2
                ldw = [(sCb, cb_done)] if tc < 2 else [(sCs, csM[tc - 2])]
                P.on("sync", (lambda tc=tc, par=par: lambda e: e.dma_start(
                    out=och[par][:], in_=oh[tc * 128:(tc + 1) * 128, :]))(),
                    waits=ldw, inc=sCin[par], amt=16)
                P.on("sync", (lambda tc=tc, par=par: lambda e: e.dma_start(
                    out=gch[par][:], in_=gth[tc * 128:(tc + 1) * 128, :]))(),
                    inc=sCg[par], amt=16)
                ov = och[par][:].rearrange("p (h d) -> p h d", d=64)
                mub = mu[:].rearrange("p (h o) -> p h o", o=1).broadcast_to(
                    [128, 64, 64])
                vrb = vr[:].rearrange("p (h o) -> p h o", o=1).broadcast_to(
                    [128, 64, 64])
                u13 = u1[:].rearrange("p (h d) -> p h d", d=64)
                u23 = u2[:].rearrange("p (h d) -> p h d", d=64)
                P.on("vector", (lambda ov=ov: lambda e: e.reduce_sum(
                    mu[:].rearrange("p (h o) -> p h o", o=1), ov, AX.X))(),
                    waits=[(sCin[par], 16 * (tc // 2 + 1))], inc=sCs)
                P.on("vector", lambda e: e.tensor_scalar_mul(mu[:], mu[:],
                                                             1.0 / 64),
                     waits=[(sCs, sCs.n)], inc=sCs)
                P.on("vector", (lambda ov=ov, u13=u13, mub=mub: lambda e:
                                e.tensor_tensor(u13, ov, mub, ALU.subtract))(),
                    waits=[(sCs, sCs.n)], inc=sCs)
                P.on("scalar", lambda e: e.activation(m2[:] if False else
                                                      u2[:], u1[:], AF.Square),
                     waits=[(sCs, sCs.n)], inc=sCs)
                P.on("vector", lambda e: e.reduce_sum(
                    vr[:].rearrange("p (h o) -> p h o", o=1),
                    u2[:].rearrange("p (h d) -> p h d", d=64), AX.X),
                    waits=[(sCs, sCs.n)], inc=sCs)
                P.on("vector", lambda e: e.tensor_scalar_mul(vr[:], vr[:],
                                                             1.0 / 64),
                     waits=[(sCs, sCs.n)], inc=sCs)
                P.on("scalar", lambda e: e.activation(vr[:], vr[:], AF.Ln,
                                                      bias=epsc[:]),
                     waits=[(sCs, sCs.n)], inc=sCs)
                P.on("scalar", lambda e: e.activation(vr[:], vr[:], AF.Exp,
                                                      scale=nhc[:]),
                     waits=[(sCs, sCs.n)], inc=sCs)
                P.on("vector", (lambda u13=u13, u23=u23, vrb=vrb: lambda e:
                                e.tensor_tensor(u23, u13, vrb, ALU.mult))(),
                     waits=[(sCs, sCs.n)], inc=sCs)
                P.on("gpsimd", lambda e: e.tensor_mul(u1[:], u2[:], gnwb[:]),
                     waits=[(sCs, sCs.n)], inc=sCs)
                P.on("gpsimd", lambda e: e.tensor_add(u2[:], u1[:], gnbb[:]),
                     waits=[(sCs, sCs.n)], inc=sCs)
                P.on("gpsimd", (lambda par=par, tc=tc: lambda e: e.tensor_mul(
                    gx[:], u2[:], gch[par][:]))(),
                    waits=[(sCg[par], 16 * (tc // 2 + 1)), (sCt, 32 * tc)],
                    inc=sCs)
                csM.append(sCs.n)
                for fc in range(NFC):
                    gg = 32 * tc + fc
                    tp = fc % 2
                    tw = [(sCs, csM[tc])] if fc == 0 else []
                    if gg >= 2:
                        tw.append((sCe, gg - 1))
                    P.on("tensor", (lambda fc=fc, tp=tp: lambda e: e.transpose(
                        pC[tp][:, 0:128],
                        gx[:, fc * 128:(fc + 1) * 128], ident[:]))(),
                        waits=tw, inc=sCt)
                    ew = [(sCt, gg + 1)]
                    if fc == 0:
                        ew.append((sCo[par], 16 * (tc // 2)))
                    P.on("scalar", (lambda fc=fc, tp=tp, par=par: lambda e:
                                    e.copy(gxc[par][:, fc * 128:(fc + 1) * 128],
                                           pC[tp][:, 0:128]))(),
                        waits=ew, inc=sCe)
                P.on("sync", (lambda tc=tc, par=par: lambda e: e.dma_start(
                    out=gxh[tc].rearrange("(fc p) t -> p fc t", p=128),
                    in_=gxc[par][:].rearrange("p (fc t) -> p fc t", t=128)))(),
                    waits=[(sCe, 32 * (tc + 1))], inc=sCo[par], amt=16)
            sc_done = [(sCo[0], sCo[0].n), (sCo[1], sCo[1].n)]
            P.flush(block)

        with ExitStack() as std_:
            gxf = sbuf("gxf", [128, NFC * T], BF16, std_)
            wsl = [sbuf(f"wslD{i}", [128, NFC * 512], BF16, std_)
                   for i in range(2)]
            ares = [sbuf(f"ares{i}", [128, 512], F32, std_) for i in range(2)]
            xrs = [sbuf(f"xrs{i}", [128, 512], F32, std_) for i in range(2)]
            frsd = sbuf("frsd", [128, 512], F32, std_)
            pD = [psum_t(f"pD{i}", [128, 512], F32, std_) for i in range(4)]
            sDin = Sem("sDin")
            sDe2 = sDin
            sDW = [Sem("sDW0"), Sem("sDW1")]
            sDx = [Sem("sDx0"), Sem("sDx1")]
            sDg = Sem("sDg")
            sDe = Sem("sDe")
            sDo = [Sem("sDo0"), Sem("sDo1")]

            for ci in range(NCH):
                P.on("sync", (lambda ci=ci: lambda e: e.dma_start(
                    out=gxf[:].rearrange("p (jc t) -> p jc t", t=T)
                        [:, :, ci * 128:(ci + 1) * 128],
                    in_=gxh[ci].rearrange("(jc p) t -> p jc t", p=128)))(),
                    waits=(sc_done if ci == 0 else []), inc=sDin, amt=16)
            gxf_done = sDin.n
            if gxfd is not None:
                P.on("sync", lambda e: e.dma_start(out=gxfd, in_=gxf[:]),
                     waits=[(sDin, gxf_done)], inc=sDin, amt=16)
                gxf_done = sDin.n
            gD = [0]
            for isl in range(8):
                par = isl % 2
                wldw = [(sDg, 16 * (isl - 1))] if isl >= 2 else list(sc_done)
                P.on("gpsimd", (lambda isl=isl, par=par: lambda e: e.dma_start(
                    out=wsl[par][:].rearrange("p (jc i) -> p jc i", i=512),
                    in_=wT["o"][isl]))(),
                    waits=wldw, inc=sDW[par], amt=16)
                for tc in range(NCH):
                    g = gD[0]
                    xpar = g % 2
                    P.on("sync", (lambda tc=tc, isl=isl, xpar=xpar: lambda e:
                                  e.dma_start(
                        out=xrs[xpar][:],
                        in_=xp[1 + WU + tc * 128:1 + WU + (tc + 1) * 128,
                               isl * 512:(isl + 1) * 512]))(),
                        waits=[(sDe, g - 1)] if g >= 2 else [], inc=sDx[xpar],
                        amt=16)
                    mmw = [(sDW[par], 16 * (isl // 2 + 1))]
                    if g >= 4:
                        mmw.append((sDe, g - 3))
                    if tc == 0:
                        mmw.append((sDin, gxf_done))
                    for jc in range(NFC):
                        P.on("tensor", (lambda tc=tc, isl=isl, jc=jc, par=par,
                                        g=g: lambda e: e.matmul(
                            pD[g % 4][:],
                            gxf[:, jc * T + tc * 128:jc * T + tc * 128 + 128],
                            wsl[par][:, jc * 512:(jc + 1) * 512],
                            start=(jc == 0), stop=(jc == NFC - 1)))(),
                            waits=(mmw if jc == 0 else []),
                            inc=(sDg if jc == NFC - 1 else None))
                    evw = [(sDg, sDg.n), (sDx[xpar], 16 * (g // 2 + 1))]
                    if g >= 2:
                        evw.append((sDo[g % 2], 16 * (g // 2)))
                    if g == 0 and mmd is not None:
                        base2 = sDe2.n
                        P.on("vector", lambda e: e.tensor_copy(
                            frsd[:], pD[0][:]), waits=list(evw), inc=sDe2)
                        P.on("sync", (lambda base2=base2: lambda e: e.dma_start(
                            out=mmd, in_=frsd[:]))(),
                             waits=[(sDe2, base2 + 1)], inc=sDe2, amt=16)
                        P.on("sync", lambda e: e.dma_start(out=xrd,
                                                           in_=xrs[0][:]),
                             waits=[(sDx[0], 16)], inc=sDe2, amt=16)
                    P.on("vector", (lambda g=g, xpar=xpar: lambda e:
                                    e.scalar_tensor_tensor(
                        ares[g % 2][:], xrs[xpar][:], 1.0, pD[g % 4][:],
                        ALU.bypass, ALU.add))(),
                        waits=list(evw) + ([(sDe2, sDe2.n)] if (g == 0 and mmd is not None) else []),
                        inc=sDe)
                    P.on("sync", (lambda tc=tc, isl=isl, g=g: lambda e:
                                  e.dma_start(
                        out=arh[tc * 128:(tc + 1) * 128,
                                isl * 512:(isl + 1) * 512],
                        in_=ares[g % 2][:]))(),
                        waits=[(sDe, sDe.n)], inc=sDo[g % 2], amt=16)
                    gD[0] += 1
            sd_done = [(sDo[0], sDo[0].n), (sDo[1], sDo[1].n)]
            P.flush(block)

        def ln_stage(src_h, dst_h, dst_bf_h, waits0, tag, final_out=None):
            with ExitStack() as stf_:
                art = [sbuf(f"art{tag}{i}", [128, D], F32, stf_)
                       for i in range(2)]
                sq = sbuf(f"sq{tag}", [128, D], F32, stf_)
                xlt = sbuf(f"xlt{tag}", [128, D], F32, stf_)
                xbt = sbuf(f"xbt{tag}", [128, D], BF16, stf_)
                xcl = [sbuf(f"xcl{tag}{i}", [128, D], BF16, stf_)
                       for i in range(2)]
                mcol = sbuf(f"mcol{tag}", [128, 1], F32, stf_)
                vcol = sbuf(f"vcol{tag}", [128, 1], F32, stf_)
                m2c = sbuf(f"m2c{tag}", [128, 1], F32, stf_)
                pL = [psum_t(f"pL{tag}{i}", [128, 1024], BF16, stf_)
                      for i in range(2)]
                sLin = [Sem(f"sLin{tag}0"), Sem(f"sLin{tag}1")]
                sLs = Sem(f"sLs{tag}")
                if final_out is None:
                    sLt = Sem(f"sLt{tag}")
                    sLe = Sem(f"sLe{tag}")
                    sLo = [Sem(f"sLo{tag}0"), Sem(f"sLo{tag}1"),
                           Sem(f"sLo{tag}2")]
                else:
                    sLt = sLe = sLs
                    sLo = [sLs, sLs, Sem(f"sLo{tag}2")]
                lsM = []
                for tc in range(NCH):
                    par = tc % 2
                    ldw = list(waits0) if tc < 2 else [(sLs, lsM[tc - 2])]
                    P.on("sync", (lambda tc=tc, par=par: lambda e: e.dma_start(
                        out=art[par][:],
                        in_=src_h[tc * 128:(tc + 1) * 128, :]))(),
                        waits=ldw, inc=sLin[par], amt=16)
                    P.on("vector", (lambda par=par: lambda e: e.reduce_sum(
                        mcol[:].rearrange("p (a o) -> p a o", o=1),
                        art[par][:].rearrange("p (a d) -> p a d", a=1), AX.X))(),
                        waits=[(sLin[par], 16 * (tc // 2 + 1))], inc=sLs)
                    P.on("vector", lambda e: e.tensor_scalar_mul(
                        mcol[:], mcol[:], 1.0 / D),
                        waits=[(sLs, sLs.n)], inc=sLs)
                    P.on("vector", (lambda par=par: lambda e:
                                    e.tensor_scalar_sub(
                        xlt[:], art[par][:], mcol[:]))(),
                        waits=[(sLs, sLs.n), (sLo[2], 16 * tc)], inc=sLs)
                    P.on("scalar", lambda e: e.activation(
                        sq[:], xlt[:], AF.Square),
                        waits=[(sLs, sLs.n)], inc=sLs)
                    P.on("vector", lambda e: e.reduce_sum(
                        vcol[:].rearrange("p (a o) -> p a o", o=1),
                        sq[:].rearrange("p (a d) -> p a d", a=1), AX.X),
                        waits=[(sLs, sLs.n)], inc=sLs)
                    P.on("vector", lambda e: e.tensor_scalar_mul(
                        vcol[:], vcol[:], 1.0 / D),
                        waits=[(sLs, sLs.n)], inc=sLs)
                    P.on("scalar", lambda e: e.activation(vcol[:], vcol[:],
                                                          AF.Ln, bias=epsc[:]),
                         waits=[(sLs, sLs.n)], inc=sLs)
                    P.on("scalar", lambda e: e.activation(vcol[:], vcol[:],
                                                          AF.Exp,
                                                          scale=nhc[:]),
                         waits=[(sLs, sLs.n)], inc=sLs)
                    P.on("vector", lambda e: e.tensor_scalar_mul(
                        xlt[:], xlt[:], vcol[:]),
                        waits=[(sLs, sLs.n)], inc=sLs)
                    ncount = sLs.n
                    if final_out is None:
                        P.on("sync", (lambda tc=tc: lambda e: e.dma_start(
                            out=dst_h[tc * 128:(tc + 1) * 128, :],
                            in_=xlt[:]))(),
                            waits=[(sLs, ncount), (sLo[2], 16 * tc)],
                            inc=sLo[2], amt=16)
                        P.on("vector", (lambda tc=tc: lambda e: e.tensor_copy(
                            xbt[:], xlt[:]))(),
                            waits=[(sLs, ncount), (sLt, 32 * tc)], inc=sLs)
                        bcount = sLs.n
                        for fc in range(NFC):
                            gg = 32 * tc + fc
                            tp = fc % 2
                            tw = [(sLs, bcount)] if fc == 0 else []
                            if gg >= 2:
                                tw.append((sLe, gg - 1))
                            P.on("tensor", (lambda fc=fc, tp=tp: lambda e:
                                            e.transpose(
                                pL[tp][:, 0:128],
                                xbt[:, fc * 128:(fc + 1) * 128], ident[:]))(),
                                waits=tw, inc=sLt)
                            ew = [(sLt, gg + 1)]
                            if fc == 0:
                                ew.append((sLo[par], 16 * (tc // 2)))
                            P.on("scalar", (lambda fc=fc, tp=tp, par=par:
                                            lambda e: e.copy(
                                xcl[par][:, fc * 128:(fc + 1) * 128],
                                pL[tp][:, 0:128]))(), waits=ew, inc=sLe)
                        P.on("sync", (lambda tc=tc, par=par: lambda e:
                                      e.dma_start(
                            out=dst_bf_h[tc].rearrange("(fc p) t -> p fc t",
                                                       p=128),
                            in_=xcl[par][:].rearrange("p (fc t) -> p fc t",
                                                      t=128)))(),
                            waits=[(sLe, 32 * (tc + 1))], inc=sLo[par], amt=16)
                        lsM.append(bcount)
                    else:
                        P.on("sync", (lambda tc=tc: lambda e: e.dma_start(
                            out=final_out[tc * 128:(tc + 1) * 128, :],
                            in_=xlt[:]))(),
                            waits=[(sLs, ncount), (sLo[2], 16 * tc)],
                            inc=sLo[2], amt=16)
                        lsM.append(ncount)
                done = [(sLo[0], sLo[0].n), (sLo[1], sLo[1].n),
                        (sLo[2], sLo[2].n)]
                P.on("sync", lambda e: None, waits=done)
                P.flush(block)
            return done

        if ard is not None:
            P.on("sync", lambda e: e.dma_start(out=ard, in_=arh),
                 waits=sd_done, inc=sQo, amt=16)
            P.flush(block)
        sf_done = ln_stage(arh, xlh, xlbh, sd_done, "F")
        if xld is not None:
            P.on("sync", lambda e: e.dma_start(out=xld, in_=xlh),
                 waits=sf_done, inc=sQo, amt=16)
            P.flush(block)

        with ExitStack() as stg:
            xlf = sbuf("xlf", [128, NFC * T], BF16, stg)
            wslg = [sbuf(f"wslG{i}", [128, NFC * 512], BF16, stg)
                    for i in range(2)]
            frs = [sbuf(f"frs{i}", [128, 512], F32, stg) for i in range(2)]
            xr2 = [sbuf(f"xr2{i}", [128, 512], F32, stg) for i in range(2)]
            fbc = sbuf("fbc", [128, 512], F32, stg)
            pG = [psum_t(f"pG{i}", [128, 512], F32, stg) for i in range(4)]
            sGin = Sem("sGin")
            sGW = [Sem("sGW0"), Sem("sGW1")]
            sGx = [Sem("sGx0"), Sem("sGx1")]
            sGg = Sem("sGg")
            sGe = Sem("sGe")
            sGb = Sem("sGb")
            sGo = [Sem("sGo0"), Sem("sGo1")]

            for ci in range(NCH):
                P.on("sync", (lambda ci=ci: lambda e: e.dma_start(
                    out=xlf[:].rearrange("p (jc t) -> p jc t", t=T)
                        [:, :, ci * 128:(ci + 1) * 128],
                    in_=xlbh[ci].rearrange("(jc p) t -> p jc t", p=128)))(),
                    waits=(sf_done if ci == 0 else []), inc=sGin, amt=16)
            xlf_done = sGin.n
            gG = [0]
            for isl in range(8):
                par = isl % 2
                wldw = [(sGg, 16 * (isl - 1))] if isl >= 2 else list(sf_done)
                for sp in range(8):
                    P.on("sync", (lambda isl=isl, par=par, sp=sp: lambda e:
                                  e.dma_start(
                        out=wslg[par][:].rearrange(
                            "p (jc i) -> p jc i", i=512)[:, :,
                                                         sp * WQS:(sp + 1) * WQS],
                        in_=wqh[isl * 8 + sp].rearrange(
                            "(jc p) n -> p jc n", p=128)))(),
                        waits=(wldw if sp == 0 else []), inc=sGW[par], amt=16)
                fcw = list(sf_done) if isl == 0 else [(sGe, 2 * 16 * isl)]
                P.on("sync", (lambda isl=isl: lambda e: e.dma_start(
                    out=fbc[:],
                    in_=fbi.rearrange("(o d) -> o d", o=1).broadcast_to(
                        [128, D])[:, isl * 512:(isl + 1) * 512]))(),
                    waits=fcw, inc=sGb, amt=16)
                fb_ready = sGb.n
                for tc in range(NCH):
                    g = gG[0]
                    xpar = g % 2
                    P.on("sync", (lambda tc=tc, isl=isl, xpar=xpar: lambda e:
                                  e.dma_start(
                        out=xr2[xpar][:],
                        in_=xlh[tc * 128:(tc + 1) * 128,
                                isl * 512:(isl + 1) * 512]))(),
                        waits=[(sGe, 2 * (g - 1))] if g >= 2 else [],
                        inc=sGx[xpar], amt=16)
                    mmw = [(sGW[par], 128 * (isl // 2 + 1))]
                    if g >= 4:
                        mmw.append((sGe, 2 * (g - 3)))
                    if tc == 0:
                        mmw.append((sGin, xlf_done))
                    for jc in range(NFC):
                        P.on("tensor", (lambda tc=tc, jc=jc, par=par, g=g:
                                        lambda e: e.matmul(
                            pG[g % 4][:],
                            xlf[:, jc * T + tc * 128:jc * T + tc * 128 + 128],
                            wslg[par][:, jc * 512:(jc + 1) * 512],
                            start=(jc == 0), stop=(jc == NFC - 1)))(),
                            waits=(mmw if jc == 0 else []),
                            inc=(sGg if jc == NFC - 1 else None))
                    evw = [(sGg, sGg.n), (sGx[xpar], 16 * (g // 2 + 1)),
                           (sGb, fb_ready)]
                    if g >= 2:
                        evw.append((sGo[g % 2], 16 * (g // 2)))
                    P.on("vector", (lambda g=g: lambda e:
                                    e.scalar_tensor_tensor(
                        frs[g % 2][:], fbc[:], 1.0, pG[g % 4][:],
                        ALU.bypass, ALU.add))(), waits=evw, inc=sGe)
                    P.on("vector", (lambda g=g, xpar=xpar: lambda e:
                                    e.tensor_add(frs[g % 2][:], frs[g % 2][:],
                                                 xr2[xpar][:]))(),
                        waits=[(sGe, sGe.n)], inc=sGe)
                    P.on("sync", (lambda tc=tc, isl=isl, g=g: lambda e:
                                  e.dma_start(
                        out=arh[tc * 128:(tc + 1) * 128,
                                isl * 512:(isl + 1) * 512],
                        in_=frs[g % 2][:]))(),
                        waits=[(sGe, sGe.n)], inc=sGo[g % 2], amt=16)
                    gG[0] += 1
            sg_done = [(sGo[0], sGo[0].n), (sGo[1], sGo[1].n)]
            P.flush(block)

        ln_stage(arh, None, None, sg_done, "H", final_out=out)

        for (s, v) in P.all_waits:
            assert v <= s.n, f"wait {v} > emitted {s.n} on sem (hang)"

    return nc


def kernel(**inputs):
    x = inputs["L0_attn_hidden"]
    B, S, _ = x.shape
    xpad = np.concatenate([np.zeros((B, WU + 1, D), np.float32),
                           np.asarray(x, np.float32)], axis=1)
    WSLAB = {"k": 128, "r": 128, "v": 256, "g": 256, "o": 512}
    w = {}
    for (m, n) in (("k", "key"), ("v", "value"), ("r", "receptance"),
                   ("g", "gate"), ("o", "output")):
        wi = WSLAB[m]
        A = np.asarray(inputs[f"L0_attn_w_{n}"], np.float32).T
        w[m] = np.ascontiguousarray(
            A.reshape(NFC, 128, D // wi, wi).transpose(2, 1, 0, 3))
    tmix = np.ascontiguousarray(np.stack([
        np.asarray(inputs["L0_attn_time_mix_key"]),
        np.asarray(inputs["L0_attn_time_mix_value"]),
        np.asarray(inputs["L0_attn_time_mix_receptance"]),
        np.asarray(inputs["L0_attn_time_mix_gate"])], axis=0)).astype(np.float32)
    tdec = np.ascontiguousarray(
        np.asarray(inputs["L0_attn_time_decay"]).reshape(-1)).astype(np.float32)
    tfaa = np.ascontiguousarray(
        np.asarray(inputs["L0_attn_time_faaaa"]).reshape(-1)).astype(np.float32)
    gnw = np.ascontiguousarray(np.asarray(inputs["L0_attn_gn_weight"])).astype(
        np.float32)
    gnb = np.ascontiguousarray(np.asarray(inputs["L0_attn_gn_bias"])).astype(
        np.float32)
    WQS = 64
    wf = np.asarray(inputs["L0_ffn_kernel"], np.float32)
    wfn = np.ascontiguousarray(
        wf.reshape(128, 32, D // WQS, WQS).transpose(2, 0, 1, 3))
    fbi = np.ascontiguousarray(np.asarray(inputs["L0_ffn_bias"])).astype(
        np.float32)

    nc = build_program()
    in_maps = []
    for c in range(NCORES):
        b, q = c // 4, c % 4
        xs = np.ascontiguousarray(xpad[b, q * T:q * T + TPW, :])
        m = {"xp": xs, "wfn": wfn, "tmix": tmix, "tdec": tdec, "tfaa": tfaa,
             "gnwt": gnw, "gnbt": gnb, "fbi": fbi}
        for k2 in ("k", "v", "r", "g", "o"):
            m[f"w{k2}T"] = w[k2]
        in_maps.append(m)
    res = run_bass_kernel_spmd(nc, in_maps, core_ids=list(range(NCORES)))
    outp = np.empty((B, S, D), np.float32)
    for c in range(NCORES):
        b, q = c // 4, c % 4
        outp[b, q * T:(q + 1) * T, :] = res.results[c]["out"]
    return outp


# version 09
# speedup vs baseline: 2.3195x; 2.3195x over previous
import numpy as np
import ml_dtypes
import concourse.bass as bass
import concourse.mybir as mybir
from concourse.bass_utils import run_bass_kernel_spmd
from contextlib import ExitStack

F32 = mybir.dt.float32
BF16 = mybir.dt.bfloat16
I32 = mybir.dt.int32
AF = mybir.ActivationFunctionType
ALU = mybir.AluOpType
AX = mybir.AxisListType

D = 4096
T = 2048
WU = 512
NWC = WU // 128
TPW = T + WU + 1
NFC = 32
NCH = 16
CLAMP = 80.0
NCORES = 8
ENGINES = ("sync", "vector", "scalar", "tensor", "gpsimd")

DEBUG_TAPS = []


class Prog:

    def __init__(self, nc):
        self.nc = nc
        self.q = {e: [] for e in ENGINES}
        self.all_waits = []

    def on(self, eng, fn, waits=(), inc=None, amt=1):
        if inc is not None:
            inc.n += amt
        n_after = inc.n if inc is not None else None
        self.q[eng].append((fn, tuple(waits), inc, amt))
        self.all_waits.extend(waits)
        return n_after

    def flush(self, block):
        for e in ENGINES:
            ops = self.q[e]
            if not ops:
                continue

            def body(eng, ops=ops):
                for (fn, waits, inc, amt) in ops:
                    for (s, v) in waits:
                        if v > 0:
                            eng.wait_ge(s.h, v)
                    ins = fn(eng)
                    if inc is not None and ins is not None:
                        ins.then_inc(inc.h, amt)
            getattr(block, e)(body)
        self.q = {e: [] for e in ENGINES}


def MM(t, out, lhsT, rhs, ob=0, kb=0, **kw):
    if 96 in (ob, kb):
        kw.setdefault("tile_position", (kb, ob))
    return t.matmul(out, lhsT, rhs, **kw)


def build_program():
    nc = bass.Bass("TRN2", target_bir_lowering=False)

    xp = nc.dram_tensor("xp", [TPW, D], F32, kind="ExternalInput").ap()
    WSLAB = {"k": 128, "r": 128, "v": 256, "g": 256, "o": 512}
    wT = {m: nc.dram_tensor(f"w{m}T", [D // WSLAB[m], 128, NFC, WSLAB[m]], F32,
                            kind="ExternalInput").ap()
          for m in ("k", "v", "r", "g", "o")}
    WQS = 64
    wfn = nc.dram_tensor("wfn", [D // WQS, 128, 32, WQS], F32,
                         kind="ExternalInput").ap()
    tmix = nc.dram_tensor("tmix", [4, D], F32, kind="ExternalInput").ap()
    tdec = nc.dram_tensor("tdec", [D], F32, kind="ExternalInput").ap()
    tfaa = nc.dram_tensor("tfaa", [D], F32, kind="ExternalInput").ap()
    gnwt = nc.dram_tensor("gnwt", [D], F32, kind="ExternalInput").ap()
    gnbt = nc.dram_tensor("gnbt", [D], F32, kind="ExternalInput").ap()
    fbi = nc.dram_tensor("fbi", [D], F32, kind="ExternalInput").ap()
    out = nc.dram_tensor("out", [T, D], F32, kind="ExternalOutput").ap()

    def internal(name, shape, dt):
        kind = "ExternalOutput" if name in DEBUG_TAPS else "Internal"
        return nc.dram_tensor(name, shape, dt, kind=kind).ap()

    xbf = internal("xbf", [D, TPW], BF16)
    kfh = internal("kfh", [NCH + NWC, D, 128], BF16)
    rfh = internal("rfh", [NCH, D, 128], BF16)
    vth = internal("vth", [T + WU, D], BF16)
    gth = internal("gth", [T, D], BF16)
    oh = internal("oh", [T, D], BF16)
    gxh = internal("gxh", [NCH, D, 128], BF16)
    arh = internal("arh", [T, D], F32)
    xlh = internal("xlh", [T, D], F32)
    xlbh = internal("xlbh", [NCH, D, 128], BF16)
    wqh = internal("wqh", [D // WQS, D, WQS], BF16)
    gxfd = internal("gxfd", [128, NFC * T], BF16) if "gxfd" in DEBUG_TAPS else None
    mmd = internal("mmd", [128, 512], F32) if "mmd" in DEBUG_TAPS else None
    ard = internal("ard", [T, D], F32) if "ard" in DEBUG_TAPS else None
    xld = internal("xld", [T, D], F32) if "xld" in DEBUG_TAPS else None
    xrd = internal("xrd", [128, 512], F32) if "xrd" in DEBUG_TAPS else None

    st = ExitStack()

    def sbuf(n, s, d=F32, stk=None):
        return (stk or st).enter_context(nc.sbuf_tensor(n, s, d))

    def psum_t(n, s, d=F32, stk=None):
        return (stk or st).enter_context(nc.psum_tensor(n, s, d))

    class Sem:
        def __init__(self, name):
            self.h = st.enter_context(nc.semaphore(name))
            self.n = 0

    with st:
        ident = sbuf("ident", [128, 128], BF16)
        Lall = sbuf("Lall", [128, NFC])
        tfall = sbuf("tfall", [128, NFC])
        eE = {k: sbuf(f"eE{k}", [128, NFC]) for k in (32, 64, 96, 128)}
        mixc = sbuf("mixc", [128, 4 * NFC])
        csc = sbuf("csc", [128, NFC])
        epsc = sbuf("epsc", [128, 1])
        nhc = sbuf("nhc", [128, 1])
        ii = sbuf("ii", [128, 128], I32)

        block = st.enter_context(nc.Block())
        P = Prog(nc)

        def bc(ap2d, span):
            return ap2d.rearrange("p (fc o) -> p fc o", o=1).broadcast_to(
                [128, NFC, span])

        s0 = Sem("s0")
        P.on("sync", lambda e: e.dma_start(
            out=Lall[:], in_=tdec.rearrange("(fc p) -> p fc", p=128),
            allow_slow_non_contiguous=True), inc=s0, amt=16)
        P.on("sync", lambda e: e.dma_start(
            out=tfall[:], in_=tfaa.rearrange("(fc p) -> p fc", p=128),
            allow_slow_non_contiguous=True), inc=s0, amt=16)
        P.on("sync", lambda e: e.dma_start(
            out=mixc[:].rearrange("p (m fc) -> p m fc", m=4),
            in_=tmix.rearrange("m (fc p) -> p m fc", p=128),
            allow_slow_non_contiguous=True), inc=s0, amt=16)
        P.on("gpsimd", lambda e: e.iota(ii[:], pattern=[[-1, 128]], base=0,
                                        channel_multiplier=1), inc=s0)
        P.on("vector", lambda e: e.memset(epsc[:], 1e-5), inc=s0)
        P.on("vector", lambda e: e.memset(nhc[:], -0.5), inc=s0)
        P.on("gpsimd", lambda e: e.tensor_single_scalar(
            ident[:], ii[:], 0, ALU.is_equal), waits=[(s0, s0.n)], inc=s0)
        P.on("scalar", lambda e: e.activation(csc[:], Lall[:], AF.Exp),
             waits=[(s0, 16)], inc=s0)
        P.on("vector", lambda e: e.tensor_scalar_mul(Lall[:], csc[:], -1.0),
             waits=[(s0, s0.n)], inc=s0)
        for k in (32, 64, 96, 128):
            P.on("vector", (lambda kk=k: lambda e: e.tensor_scalar_mul(
                csc[:], Lall[:], float(kk)))(), waits=[(s0, s0.n)], inc=s0)
            P.on("scalar", (lambda kk=k: lambda e: e.activation(
                eE[kk][:], csc[:], AF.Exp))(), waits=[(s0, s0.n)], inc=s0)
        s0_done = [(s0, s0.n)]

        with ExitStack() as stx:
            xt = [sbuf(f"xt{i}", [128, D], F32, stx) for i in range(2)]
            xb = sbuf("xb", [128, D], BF16, stx)
            xc = [sbuf(f"xc{i}", [128, D], BF16, stx) for i in range(2)]
            pX = [psum_t(f"pX{i}", [128, 1024], BF16, stx) for i in range(2)]
            sXin = [Sem("sXin0"), Sem("sXin1")]
            sXc = Sem("sXc")
            sXt = Sem("sXt")
            sXe = Sem("sXe")
            sXo = [Sem("sXo0"), Sem("sXo1")]
            NT = (TPW + 127) // 128

            QW = WQS
            qn = 32 * QW
            qw = [sbuf(f"qw{i}", [128, qn], F32, stx) for i in range(2)]
            tq = [sbuf(f"tq{i}", [128, qn], F32, stx) for i in range(2)]
            aq = [sbuf(f"aq{i}", [128, qn], F32, stx) for i in range(2)]
            b1 = [sbuf(f"b1{i}", [128, qn], F32, stx) for i in range(2)]
            b3 = [sbuf(f"b3{i}", [128, qn], F32, stx) for i in range(2)]
            sg = [sbuf(f"sg{i}", [128, qn], F32, stx) for i in range(2)]
            qo = [sbuf(f"qo{i}", [128, qn], BF16, stx) for i in range(2)]
            axq = [sbuf(f"axq{i}", [128, QW], F32, stx) for i in range(2)]
            eiq = [sbuf(f"eiq{i}", [128, QW], I32, stx) for i in range(2)]
            miq = [sbuf(f"miq{i}", [128, QW], I32, stx) for i in range(2)]
            scf = [sbuf(f"scf{i}", [128, QW], F32, stx) for i in range(2)]
            rsf = [sbuf(f"rsf{i}", [128, QW], F32, stx) for i in range(2)]
            sQin = [Sem("sQin0"), Sem("sQin1")]
            sQv = [Sem("sQv0"), Sem("sQv1")]
            sQp = [Sem("sQp0"), Sem("sQp1")]
            sQa = sQv
            sQo = [Sem("sQo0"), Sem("sQo1")]
            C1, C2, C3 = float(2 ** 22), float(2 ** 23), float(2 ** 24)
            NQS = D // QW

            def emit_x_tile(i):
                par = i % 2
                row0 = min(i * 128, TPW - 128)
                ldw = list(s0_done) if i < 2 else [(sXc, i - 1)]
                P.on("sync", (lambda par=par, row0=row0: lambda e: e.dma_start(
                    out=xt[par][:], in_=xp[row0:row0 + 128, :]))(),
                    waits=ldw, inc=sXin[par], amt=16)
                P.on("vector", (lambda par=par: lambda e: e.tensor_copy(
                    xb[:], xt[par][:]))(),
                    waits=[(sXin[par], 16 * (i // 2 + 1)), (sXt, 32 * i)],
                    inc=sXc)
                for fc in range(NFC):
                    g = 32 * i + fc
                    tp = fc % 2
                    tw = [(sXc, i + 1)]
                    if g >= 2:
                        tw.append((sXe, g - 1))
                    P.on("tensor", (lambda fc=fc, tp=tp: lambda e: e.transpose(
                        pX[tp][:, 0:128], xb[:, fc * 128:(fc + 1) * 128],
                        ident[:]))(), waits=tw, inc=sXt)
                    ew = [(sXt, g + 1)]
                    if fc == 0:
                        ew.append((sXo[par], 16 * (i // 2)))
                    P.on("scalar", (lambda fc=fc, tp=tp, par=par: lambda e: e.copy(
                        xc[par][:, fc * 128:(fc + 1) * 128], pX[tp][:, 0:128]))(),
                        waits=ew, inc=sXe)
                P.on("sync", (lambda par=par, row0=row0: lambda e: e.dma_start(
                    out=xbf[:, row0:row0 + 128].rearrange(
                        "(fc p) t -> p fc t", p=128),
                    in_=xc[par][:].rearrange("p (fc t) -> p fc t", t=128)))(),
                    waits=[(sXe, 32 * (i + 1))], inc=sXo[par], amt=16)

            Am_pool, Bm_act, Bm_dve, Cm_pool = {}, {}, {}, {}

            def emit_sq_A(s):
                par = s % 2
                sQ = sQv[par]
                qw3 = qw[par][:].rearrange("p (kr n) -> p kr n", kr=32)
                tq3 = tq[par][:].rearrange("p (kr n) -> p kr n", kr=32)
                rs_bc = rsf[par][:].rearrange("p (n o) -> p o n", o=1
                                              ).broadcast_to([128, 32, QW])
                ldw = [(sQv[par], Am_pool[s - 2])] if s >= 2 else list(s0_done)
                P.on("scalar", (lambda s=s, qw3=qw3: lambda e:
                                e.dma_start(out=qw3, in_=wfn[s]))(),
                    waits=ldw, inc=sQin[par], amt=16)
                P.on("vector", (lambda par=par: lambda e: e.reduce_max(
                    axq[par][:].rearrange("p (n o) -> p n o", o=1),
                    qw[par][:].rearrange("p (kr n) -> p n kr", n=QW),
                    AX.X, apply_absolute_value=True))(),
                    waits=[(sQin[par], 16 * (s // 2 + 1)), (sQv[par], sQv[par].n)],
                    inc=sQ)
                for fn in (
                    (lambda par=par: lambda e: e.tensor_scalar_max(
                        axq[par][:], axq[par][:], 1e-30))(),
                    (lambda par=par: lambda e: e.tensor_single_scalar(
                        eiq[par][:], axq[par][:].bitcast(I32), 23,
                        ALU.logical_shift_right))(),
                    (lambda par=par: lambda e: e.tensor_single_scalar(
                        miq[par][:], axq[par][:].bitcast(I32), 0x7FFFFF,
                        ALU.bitwise_and))(),
                    (lambda par=par: lambda e: e.tensor_single_scalar(
                        miq[par][:], miq[par][:], 0x400000, ALU.is_gt))(),
                    (lambda par=par: lambda e: e.tensor_tensor(
                        eiq[par][:], eiq[par][:], miq[par][:], ALU.add))(),
                    (lambda par=par: lambda e: e.tensor_single_scalar(
                        eiq[par][:], eiq[par][:], 2, ALU.subtract))(),
                ):
                    P.on("vector", fn, waits=[(sQ, sQ.n)], inc=sQ)
                scw = [(sQ, sQ.n)]
                if s >= 2:
                    scw.append((sQv[par], Cm_pool[s - 2]))
                P.on("vector", (lambda par=par: lambda e: e.tensor_single_scalar(
                    scf[par][:].bitcast(I32), eiq[par][:], 23,
                    ALU.logical_shift_left))(), waits=scw, inc=sQ)
                P.on("vector", (lambda par=par: lambda e: e.tensor_scalar(
                    miq[par][:], eiq[par][:], -1, 254, ALU.mult, ALU.add))(),
                    waits=[(sQ, sQ.n)], inc=sQ)
                P.on("vector", (lambda par=par: lambda e: e.tensor_single_scalar(
                    rsf[par][:].bitcast(I32), miq[par][:], 23,
                    ALU.logical_shift_left))(), waits=[(sQ, sQ.n)], inc=sQ)
                dve_scale = sQ.n
                tqw = [(sQv[par], dve_scale)]
                if s >= 2:
                    tqw.append((sQa[par], Bm_act[s - 2]))
                P.on("vector", (lambda tq3=tq3, qw3=qw3, rs_bc=rs_bc: lambda e:
                                e.tensor_tensor(tq3, qw3, rs_bc, ALU.mult))(),
                    waits=tqw, inc=sQv[par])
                Am_pool[s] = sQv[par].n

            def emit_sq_B(s):
                par = s % 2
                sQ = sQv[par]
                sQ = sQv[par]
                abw = [(sQv[par], Am_pool[s]), (sQa[par], sQa[par].n)]
                if s >= 2:
                    abw.append((sQv[par], Cm_pool[s - 2]))
                P.on("scalar", (lambda par=par: lambda e: e.activation(
                    aq[par][:], tq[par][:], AF.Abs))(), waits=abw, inc=sQa[par])
                P.on("scalar", (lambda par=par: lambda e: e.activation(
                    sg[par][:], tq[par][:], AF.Sign))(),
                    waits=[(sQa[par], sQa[par].n)], inc=sQa[par])
                Bm_act[s] = sQa[par].n
                P.on("vector", (lambda par=par: lambda e: e.tensor_scalar(
                    b1[par][:], aq[par][:], C1, C1, ALU.add, ALU.subtract))(),
                    waits=[(sQa[par], Bm_act[s] - 1), (sQv[par], sQv[par].n)],
                    inc=sQ)
                P.on("vector", (lambda par=par: lambda e: e.tensor_scalar(
                    b3[par][:], aq[par][:], C3, C3, ALU.add, ALU.subtract))(),
                    waits=[(sQ, sQ.n)], inc=sQ)
                P.on("vector", (lambda par=par: lambda e: e.tensor_scalar(
                    aq[par][:], aq[par][:], C2, C2, ALU.add, ALU.subtract))(),
                    waits=[(sQ, sQ.n)], inc=sQ)
                P.on("vector", (lambda par=par: lambda e: e.tensor_scalar_min(
                    b1[par][:], b1[par][:], 2.0))(), waits=[(sQ, sQ.n)], inc=sQ)
                P.on("vector", (lambda par=par: lambda e: e.tensor_scalar(
                    aq[par][:], aq[par][:], 2.0, 0.0, ALU.subtract, ALU.max))(),
                    waits=[(sQ, sQ.n)], inc=sQ)
                P.on("vector", (lambda par=par: lambda e: e.tensor_scalar(
                    b3[par][:], b3[par][:], 4.0, 0.0, ALU.subtract, ALU.max))(),
                    waits=[(sQ, sQ.n)], inc=sQ)
                Bm_dve[s] = sQ.n

            def emit_sq_C(s):
                par = s % 2
                sQ = sQv[par]
                sg3 = sg[par][:].rearrange("p (kr n) -> p kr n", kr=32)
                sc_bc = scf[par][:].rearrange("p (n o) -> p o n", o=1
                                              ).broadcast_to([128, 32, QW])
                P.on("vector", (lambda par=par: lambda e: e.tensor_scalar_min(
                    aq[par][:], aq[par][:], 2.0))(),
                    waits=[(sQ, sQ.n)], inc=sQ)
                P.on("vector", (lambda par=par: lambda e: e.tensor_add(
                    b1[par][:], b1[par][:], aq[par][:]))(),
                    waits=[(sQ, sQ.n)], inc=sQ)
                P.on("vector", (lambda par=par: lambda e: e.tensor_scalar_min(
                    b3[par][:], b3[par][:], 2.0))(),
                    waits=[(sQ, sQ.n)], inc=sQ)
                P.on("vector", (lambda par=par: lambda e: e.tensor_add(
                    b1[par][:], b1[par][:], b3[par][:]))(),
                    waits=[(sQ, sQ.n)], inc=sQ)
                P.on("vector", (lambda sg3=sg3, sc_bc=sc_bc, sQ=sQ: lambda e:
                                e.tensor_tensor(sg3, sg3, sc_bc, ALU.mult))(),
                    waits=[(sQ, sQ.n)], inc=sQ)
                P.on("vector", (lambda par=par, s=s: lambda e: e.tensor_mul(
                    qo[par][:], b1[par][:], sg[par][:]))(),
                    waits=[(sQ, sQ.n), (sQo[par], 16 * (s // 2))],
                    inc=sQ)
                Cm_pool[s] = sQ.n
                sQp[par].n += 0
                P.on("scalar", (lambda s=s, par=par: lambda e: e.dma_start(
                    out=wqh[s].rearrange("(kb kr) n -> kb kr n", kr=32),
                    in_=qo[par][:].rearrange("p (kr n) -> p kr n", kr=32)))(),
                    waits=[(sQv[par], Cm_pool[s])], inc=sQo[par], amt=16)

            for step in range(max(NT, NQS + 2)):
                if step < NT:
                    emit_x_tile(step)
                if 0 <= step - 2 < NQS:
                    emit_sq_C(step - 2)
                if 0 <= step - 1 < NQS:
                    emit_sq_B(step - 1)
                if step < NQS:
                    emit_sq_A(step)
            sx_done = [(sXo[0], sXo[0].n), (sXo[1], sXo[1].n)]
            sq_done = [(sQo[0], sQo[0].n), (sQo[1], sQo[1].n),
                       (sQp[0], sQp[0].n), (sQp[1], sQp[1].n)]
            P.flush(block)

        MIDX = {"k": 0, "v": 1, "r": 2, "g": 3}
        with ExitStack() as sta:
            mixf = sbuf("mixf", [128, NFC * T], BF16, sta)
            mixw = sbuf("mixw", [128, NFC * WU], BF16, sta)
            obf = [sbuf(f"obf{i}", [128, 512], BF16, sta) for i in range(4)]
            pP = [psum_t(f"pP{i}", [128, 512], F32, sta) for i in range(8)]
            sW = [Sem("sAW0"), Sem("sAW1")]
            sMMg = Sem("sAMMg")
            sEv = Sem("sAEv")
            sAOut = [Sem(f"sAOut{i}") for i in range(4)]
            sMx = Sem("sAMx")
            sXr = [Sem("sAXr0"), Sem("sAXr1")]
            ga = [0]
            wa = [0]

            def build_mix(m, waits0, tag, warm=False):
                base_mx = sMx.n
                opj = 4 if warm else 2
                with ExitStack() as stm:
                    xr = [sbuf(f"xr{tag}{i}", [128, TPW], BF16, stm)
                          for i in range(2)]
                    for jc in range(NFC):
                        par = jc % 2
                        ldw = list(waits0) if jc < 2 else \
                            [(sMx, base_mx + opj * (jc - 1))]
                        P.on("sync", (lambda jc=jc, par=par: lambda e:
                                      e.dma_start(out=xr[par][:],
                                                  in_=xbf[jc * 128:(jc + 1) * 128,
                                                          :]))(),
                            waits=ldw, inc=sXr[par], amt=16)
                        mfs = mixf[:, jc * T:(jc + 1) * T]
                        P.on("vector", (lambda par=par, mfs=mfs: lambda e:
                                        e.tensor_sub(mfs, xr[par][:, WU + 1:TPW],
                                                     xr[par][:, WU:WU + T]))(),
                            waits=[(sXr[par], sXr[par].n)] + list(waits0),
                            inc=sMx)
                        P.on("vector", (lambda jc=jc, par=par, m=m, mfs=mfs:
                                        lambda e: e.scalar_tensor_tensor(
                            mfs, mfs,
                            mixc[:, MIDX[m] * NFC + jc:MIDX[m] * NFC + jc + 1],
                            xr[par][:, WU:WU + T], ALU.mult, ALU.add))(),
                            waits=[(sMx, sMx.n)], inc=sMx)
                        if warm:
                            mws = mixw[:, jc * WU:(jc + 1) * WU]
                            P.on("vector", (lambda par=par, mws=mws: lambda e:
                                            e.tensor_sub(mws,
                                                         xr[par][:, 1:WU + 1],
                                                         xr[par][:, 0:WU]))(),
                                waits=[(sMx, sMx.n)], inc=sMx)
                            P.on("vector", (lambda jc=jc, par=par, m=m, mws=mws:
                                            lambda e: e.scalar_tensor_tensor(
                                mws, mws,
                                mixc[:, MIDX[m] * NFC + jc:
                                     MIDX[m] * NFC + jc + 1],
                                xr[par][:, 0:WU], ALU.mult, ALU.add))(),
                                waits=[(sMx, sMx.n)], inc=sMx)
                    P.flush(block)
                return [(sMx, sMx.n)]

            def proj(m, dst, fm, waits0, tag, act=None, warm=False):
                mix_done = build_mix(m, waits0, tag, warm=warm)
                with ExitStack() as stw:
                    wi = 128 if fm else 256
                    nslabs = NFC if fm else NFC // 2
                    wsl = [sbuf(f"wsl{tag}{i}", [128, NFC * wi], BF16, stw)
                           for i in range(2)]
                    if fm:
                        subs = [("w", s) for s in range(1 if warm else 0)] + \
                            [("m", s) for s in range(4)]
                    else:
                        subs = [("w", s) for s in range(NWC if warm else 0)] + \
                            [("m", s) for s in range(NCH)]
                    row_off = WU if (not fm and warm) else 0
                    for ns in range(nslabs):
                        par = wa[0] % 2
                        wldw = []
                        if wa[0] >= 2:
                            wldw = [(sMMg, slab_gdone[wa[0] - 2])]
                        if wa[0] == 0:
                            wldw += list(sq_done)
                        if ns < 2:
                            wldw += list(mix_done)
                        P.on("gpsimd", (lambda ns=ns, par=par, m=m, wi=wi:
                                        lambda e: e.dma_start(
                            out=wsl[par][:].rearrange("p (jc i) -> p jc i", i=wi),
                            in_=wT[m][ns]))(),
                            waits=wldw, inc=sW[par], amt=16)
                        wcount = sW[par].n
                        for (kind, sub) in subs:
                            g = ga[0]
                            mmw = [(sW[par], wcount)]
                            if g >= 8:
                                mmw.append((sEv, g - 7))
                            if (kind, sub) == subs[0]:
                                mmw += mix_done
                            for jc in range(NFC):
                                if fm:
                                    rhs_ap = (mixw[:, jc * WU:jc * WU + 512]
                                              if kind == "w" else
                                              mixf[:, jc * T + sub * 512:
                                                   jc * T + sub * 512 + 512])
                                    fn = (lambda jc=jc, par=par, g=g,
                                          rhs_ap=rhs_ap: lambda e: e.matmul(
                                        pP[g % 8][:],
                                        wsl[par][:, jc * 128:(jc + 1) * 128],
                                        rhs_ap,
                                        start=(jc == 0), stop=(jc == NFC - 1)))()
                                else:
                                    lhs_ap = (mixw[:, jc * WU + sub * 128:
                                                   jc * WU + sub * 128 + 128]
                                              if kind == "w" else
                                              mixf[:, jc * T + sub * 128:
                                                   jc * T + sub * 128 + 128])
                                    fn = (lambda jc=jc, par=par, g=g, wi=wi,
                                          lhs_ap=lhs_ap: lambda e: e.matmul(
                                        pP[g % 8][:, 0:wi],
                                        lhs_ap,
                                        wsl[par][:, jc * wi:(jc + 1) * wi],
                                        start=(jc == 0), stop=(jc == NFC - 1)))()
                                P.on("tensor", fn,
                                     waits=(mmw if jc == 0 else []),
                                     inc=(sMMg if jc == NFC - 1 else None))
                            evw = [(sMMg, sMMg.n)]
                            if g >= 4:
                                evw.append((sAOut[g % 4], 16 * (g // 4)))
                            wid = 512 if fm else wi
                            if act is None:
                                P.on("scalar", (lambda g=g, wid=wid: lambda e:
                                                e.copy(obf[g % 4][:, 0:wid],
                                                       pP[g % 8][:, 0:wid]))(),
                                    waits=evw, inc=sEv)
                            else:
                                P.on("scalar", (lambda g=g, act=act, wid=wid:
                                                lambda e: e.activation(
                                    obf[g % 4][:, 0:wid],
                                    pP[g % 8][:, 0:wid], act))(),
                                     waits=evw, inc=sEv)
                            if fm:
                                c0 = (sub * 4 if kind == "w" else
                                      NWC * (1 if warm else 0) + 4 * sub)
                                st_fn = (lambda ns=ns, c0=c0, g=g: lambda e:
                                         e.dma_start(
                                    out=dst[c0:c0 + 4,
                                            ns * 128:(ns + 1) * 128,
                                            :].rearrange("c p t -> p c t"),
                                    in_=obf[g % 4][:].rearrange(
                                        "p (c t) -> p c t", t=128)))()
                            else:
                                r0_ = (sub * 128 if kind == "w" else
                                       row_off + sub * 128)
                                st_fn = (lambda ns=ns, r0_=r0_, g=g, wi=wi:
                                         lambda e: e.dma_start(
                                    out=dst[r0_:r0_ + 128,
                                            ns * wi:(ns + 1) * wi],
                                    in_=obf[g % 4][:, 0:wi]))()
                            P.on("scalar", st_fn, waits=[(sEv, sEv.n)],
                                 inc=sAOut[g % 4], amt=16)
                            ga[0] += 1
                        slab_gdone.append(sMMg.n)
                        wa[0] += 1
                    P.flush(block)
                return [(sMMg, sMMg.n)]

            slab_gdone = []
            d1 = proj("k", kfh, True, sx_done + sq_done, "k", warm=True)
            d2 = proj("r", rfh, True, d1, "r")
            d3 = proj("v", vth, False, d2, "v", warm=True)
            d4 = proj("g", gth, False, d3, "g", act=AF.Silu)
            sa_done = [(sAOut[i], sAOut[i].n) for i in range(4)] + \
                [(sEv, sEv.n)]
            P.flush(block)

        with ExitStack() as stb:
            tb = {n: sbuf(f"tb_{n}", [128, NFC * 128], BF16, stb)
                  for n in ("rt16c", "kt16c", "rt16u", "kt16u", "rt32", "kt32e")}
            bRu = sbuf("bRu", [128, NFC * 128], BF16, stb)
            bRc = sbuf("bRc", [128, NFC * 128], BF16, stb)
            bR32 = sbuf("bR32", [128, NFC * 128], BF16, stb)
            bKu = sbuf("bKu", [128, NFC * 128], BF16, stb)
            bKc = sbuf("bKc", [128, NFC * 128], BF16, stb)
            bKe = sbuf("bKe", [128, NFC * 128], BF16, stb)
            kst = {G: sbuf(f"kst{G}", [128, NFC * 32 * G], BF16, stb)
                   for G in (1, 2, 3)}
            asb = sbuf("asb", [128, 64 * 128], BF16, stb)
            stf = sbuf("stf", [128, NFC * 64], F32, stb)
            stb16 = sbuf("stb16", [128, NFC * 64], BF16, stb)
            cq = sbuf("cq", [128, 64], F32, stb)

            pA = [psum_t("pA0", [128, 512], F32, stb),
                  psum_t("pA1", [128, 512], F32, stb)]
            pT = [psum_t("pT0", [128, 1024], BF16, stb),
                  psum_t("pT1", [128, 1024], BF16, stb)]
            pO = psum_t("pO", [128, 1024], F32, stb)
            pS = psum_t("pS", [128, 1024], F32, stb)

            sTab = Sem("sTab")
            sBk = [Sem("sBk0"), Sem("sBk1")]
            sBr = [Sem("sBr0"), Sem("sBr1")]
            sBv = [Sem("sBv0"), Sem("sBv1")]
            sPre = Sem("sPre")
            sAsm = Sem("sAsm")
            sEvA = Sem("sEvA")
            sMsk = Sem("sMsk")
            sDrv = Sem("sDrv")
            sTpw = Sem("sTpw")
            sTpk = Sem("sTpk")
            sTrW = Sem("sTrW")
            sTrK = Sem("sTrK")
            sCc = Sem("sCc")
            sO = Sem("sO")
            sS = Sem("sS")
            sOEv = Sem("sOEv")
            sStU = Sem("sStU")
            sStC = Sem("sStC")
            sObo = Sem("sObo")
            sWk = [Sem("sWk0"), Sem("sWk1")]
            sWv = [Sem("sWv0"), Sem("sWv1")]
            sWpre = Sem("sWpre")
            sWtpk = Sem("sWtpk")
            sWtrk = Sem("sWtrk")
            sWs = Sem("sWs")
            sWstU = Sem("sWstU")

            Lb = bc(Lall[:], 128)
            sttab = ExitStack()
            itb = sbuf("itb", [128, NFC * 128], I32, sttab)
            ftb = sbuf("ftb", [128, NFC * 128], F32, sttab)
            f3 = ftb[:].rearrange("p (fc t) -> p fc t", t=128)
            for (nm, base, pat, clamp) in (
                ("rt16c", -8, [[0, NFC], [0, 8], [1, 16]], True),
                ("kt16c", 7, [[0, NFC], [0, 8], [-1, 16]], True),
                ("rt16u", 0, [[0, NFC], [0, 8], [1, 16]], False),
                ("kt16u", 15, [[0, NFC], [0, 8], [-1, 16]], False),
                ("rt32", 0, [[0, NFC], [0, 4], [1, 32]], False),
                ("kt32e", 31, [[0, NFC], [0, 4], [-1, 32]], False),
            ):
                P.on("gpsimd", (lambda pat=pat, base=base: lambda e: e.iota(
                    itb[:], pattern=pat, base=base, channel_multiplier=0))(),
                    waits=[(sTab, sTab.n)] +
                          (list(s0_done) + sa_done if nm == "rt16c" else []),
                    inc=sTab)
                P.on("vector", lambda e: e.tensor_copy(ftb[:], itb[:]),
                     waits=[(sTab, sTab.n)], inc=sTab)
                P.on("vector", lambda e: e.tensor_tensor(f3, f3, Lb, ALU.mult),
                     waits=[(sTab, sTab.n)], inc=sTab)
                if clamp:
                    P.on("vector", lambda e: e.tensor_scalar_min(
                        ftb[:], ftb[:], CLAMP), waits=[(sTab, sTab.n)], inc=sTab)
                P.on("scalar", (lambda nm=nm: lambda e: e.activation(
                    tb[nm][:], ftb[:], AF.Exp))(), waits=[(sTab, sTab.n)],
                    inc=sTab)
            P.on("vector", lambda e: e.memset(stf[:], 0.0),
                 waits=[(sTab, sTab.n)], inc=sTab)
            P.on("vector", lambda e: e.memset(stb16[:], 0.0),
                 waits=[(sTab, sTab.n)], inc=sTab)
            tab_done = sTab.n
            P.flush(block)
            sttab.close()
            kch = [sbuf(f"kch{i}", [128, NFC * 128], BF16, stb) for i in range(2)]
            rch = [sbuf(f"rch{i}", [128, NFC * 128], BF16, stb) for i in range(2)]
            vch = [sbuf(f"vch{i}", [128, D], BF16, stb) for i in range(2)]
            obch = sbuf("obch", [128, D], BF16, stb)

            E32b = lambda span: bc(eE[32][:], span)
            E64b = lambda span: bc(eE[64][:], span)
            E96b = lambda span: bc(eE[96][:], span)
            tfb = bc(tfall[:], 128)

            def v3(buf, span=128):
                return buf[:].rearrange("p (fc t) -> p fc t", t=span)

            for wci in range(NWC):
                par = wci % 2
                ldw = []
                if wci == 0:
                    ldw = sa_done + [(sTab, tab_done)]
                elif wci >= 2:
                    ldw = [(sWs, 64 * (wci - 1)), (sWpre, 5 * (wci - 1))]
                P.on("sync", (lambda wci=wci, par=par: lambda e: e.dma_start(
                    out=v3(kch[par]),
                    in_=kfh[wci].rearrange("(fc p) t -> p fc t", p=128)))(),
                    waits=ldw, inc=sWk[par], amt=16)
                P.on("sync", (lambda wci=wci, par=par: lambda e: e.dma_start(
                    out=vch[par][:],
                    in_=vth[wci * 128:(wci + 1) * 128, :]))(),
                    inc=sWv[par], amt=16)
                w1 = [(sWk[par], 16 * (wci // 2 + 1)), (sTab, tab_done)]
                if wci > 0:
                    w1 += [(sWtpk, 32 * wci), (sWs, 64 * wci)]
                kev = v3(bKe)
                kcv = v3(bKc)
                P.on("vector", (lambda par=par: lambda e: e.tensor_mul(
                    bKe[:], kch[par][:], tb["kt32e"][:]))(), waits=w1, inc=sWpre)
                P.on("vector", lambda e: e.tensor_copy(
                    kcv[:, :, 96:128], kev[:, :, 96:128]),
                    waits=[(sWpre, sWpre.n)], inc=sWpre)
                for (g, Eb) in ((0, E96b), (1, E64b), (2, E32b)):
                    P.on("vector", (lambda g=g, Eb=Eb: lambda e: e.tensor_tensor(
                        kcv[:, :, 32 * g:32 * g + 32],
                        kev[:, :, 32 * g:32 * g + 32], Eb(32), ALU.mult))(),
                        inc=sWpre)
                wdrv = sWpre.n
                for fc in range(NFC):
                    gg = 32 * wci + fc
                    tp = fc % 2
                    tw = [(sWpre, wdrv)] if fc == 0 else []
                    if gg >= 2:
                        tw.append((sWtrk, gg - 1))
                    P.on("tensor", (lambda fc=fc, tp=tp: lambda e: e.transpose(
                        pT[tp][:, 128:256], bKc[:, fc * 128:(fc + 1) * 128],
                        ident[:]))(), waits=tw, inc=sWtpk)
                    P.on("scalar", (lambda fc=fc, tp=tp: lambda e: e.copy(
                        bKe[:, fc * 128:(fc + 1) * 128],
                        pT[tp][:, 128:256]))(),
                        waits=[(sWtpk, gg + 1)], inc=sWtrk)
                for h in range(64):
                    fc, hp = h // 2, h % 2
                    pb = 64 * hp
                    w4 = []
                    if h == 0:
                        w4 = [(sWtrk, 32 * (wci + 1)), (sWstU, 2 * wci)]
                    if h == 32:
                        w4.append((sWstU, 2 * wci + 1))
                    P.on("tensor", (lambda fc=fc, pb=pb, h=h, par=par:
                                    lambda e: MM(e,
                        pS[pb:pb + 64, (fc % 16) * 64:(fc % 16) * 64 + 64],
                        bKe[:, fc * 128 + pb:fc * 128 + pb + 64],
                        vch[par][:, h * 64:(h + 1) * 64],
                        ob=pb, kb=0, start=True, stop=True))(),
                        waits=w4, inc=sWs)
                    if h in (31, 63):
                        half = 0 if h == 31 else 1
                        hs = half * 1024
                        stv = stf[:, hs:hs + 1024].rearrange(
                            "p (fc d) -> p fc d", d=64)
                        e128h = eE[128][:, half * 16:half * 16 + 16].rearrange(
                            "p (fc o) -> p fc o", o=1).broadcast_to(
                            [128, 16, 64])
                        wv = [(sWs, 64 * wci + 32 * (half + 1)),
                              (sWv[par], 16 * (wci // 2 + 1))]
                        P.on("vector", (lambda stv=stv, e128h=e128h: lambda e:
                                        e.tensor_tensor(stv, stv, e128h,
                                                        ALU.mult))(),
                            waits=wv, inc=sStC)
                        P.on("vector", (lambda hs=hs: lambda e: e.tensor_add(
                            stf[:, hs:hs + 1024], stf[:, hs:hs + 1024],
                            pS[:]))(), waits=[(sStC, sStC.n)], inc=sStC)
                        P.on("vector", (lambda hs=hs: lambda e: e.tensor_copy(
                            stb16[:, hs:hs + 1024], stf[:, hs:hs + 1024]))(),
                            waits=[(sStC, sStC.n)], inc=sWstU)
            warm_done = [(sWstU, sWstU.n), (sWs, sWs.n), (sWpre, sWpre.n),
                         (sWtpk, sWtpk.n), (sWtrk, sWtrk.n)]

            preM, drvM, ccM = [], [], []
            for ci in range(NCH):
                par = ci % 2
                ldw = []
                if ci == 0:
                    ldw = list(warm_done)
                elif ci >= 2:
                    ldw = [(sDrv, drvM[ci - 1]), (sS, 64 * (ci - 1)),
                           (sCc, ccM[ci - 1])]
                P.on("sync", (lambda ci=ci, par=par: lambda e: e.dma_start(
                    out=v3(kch[par]),
                    in_=kfh[NWC + ci].rearrange("(fc p) t -> p fc t",
                                                p=128)))(),
                    waits=ldw, inc=sBk[par], amt=16)
                P.on("sync", (lambda ci=ci, par=par: lambda e: e.dma_start(
                    out=v3(rch[par]),
                    in_=rfh[ci].rearrange("(fc p) t -> p fc t", p=128)))(),
                    inc=sBr[par], amt=16)
                P.on("sync", (lambda ci=ci, par=par: lambda e: e.dma_start(
                    out=vch[par][:],
                    in_=vth[WU + ci * 128:WU + (ci + 1) * 128, :]))(),
                    inc=sBv[par], amt=16)

                p1w = [(sBk[par], 16 * (ci // 2 + 1)),
                       (sBr[par], 16 * (ci // 2 + 1)), (sTab, tab_done)]
                if ci == 0:
                    p1w += list(warm_done)
                if ci > 0:
                    p1w += [(sO, 64 * ci), (sS, 64 * ci), (sCc, ccM[ci - 1]),
                            (sOEv, 4 * ci)]
                P.on("vector", (lambda par=par: lambda e: e.tensor_mul(
                    bRu[:], rch[par][:], tb["rt16u"][:]))(), waits=p1w, inc=sPre)
                for (dstb, srcb, tbl) in ((bRc, rch, "rt16c"), (bR32, rch, "rt32"),
                                          (bKu, kch, "kt16u"), (bKc, kch, "kt16c"),
                                          (bKe, kch, "kt32e")):
                    P.on("vector", (lambda dstb=dstb, srcb=srcb, tbl=tbl, par=par:
                                    lambda e: e.tensor_mul(
                        dstb[:], srcb[par][:], tb[tbl][:]))(), inc=sPre)
                ke3 = v3(bKe)

                def ksl(G, lo, hi):
                    return kst[G][:].rearrange("p (fc j) -> p fc j",
                                               j=32 * G)[:, :, lo:hi]
                P.on("vector", lambda e: e.tensor_copy(ksl(1, 0, 32),
                                                       ke3[:, :, 0:32]),
                     waits=[(sPre, sPre.n)], inc=sPre)
                P.on("vector", lambda e: e.tensor_tensor(
                    ksl(2, 0, 32), ke3[:, :, 0:32], E32b(32), ALU.mult), inc=sPre)
                P.on("vector", lambda e: e.tensor_copy(ksl(2, 32, 64),
                                                       ke3[:, :, 32:64]), inc=sPre)
                P.on("vector", lambda e: e.tensor_tensor(
                    ksl(3, 0, 32), ke3[:, :, 0:32], E64b(32), ALU.mult), inc=sPre)
                P.on("vector", lambda e: e.tensor_tensor(
                    ksl(3, 32, 64), ke3[:, :, 32:64], E32b(32), ALU.mult),
                    inc=sPre)
                P.on("vector", lambda e: e.tensor_copy(ksl(3, 64, 96),
                                                       ke3[:, :, 64:96]), inc=sPre)
                preM.append(sPre.n)

                for h in range(64):
                    fc, hp = h // 2, h % 2
                    pb = 64 * hp
                    ap = h % 2
                    w2 = []
                    if h == 0:
                        w2.append((sPre, preM[ci]))
                    if 64 * ci + h >= 2:
                        w2.append((sEvA, 64 * ci + h - 1))
                    first = True
                    for G in range(4):
                        c0 = 32 * G
                        co = fc * 128 + c0
                        P.on("tensor", (lambda pb=pb, ap=ap, c0=c0, co=co:
                                        lambda e: MM(e,
                            pA[ap][c0:c0 + 32, c0:c0 + 32],
                            bKc[pb:pb + 64, co:co + 32],
                            bRc[pb:pb + 64, co:co + 32],
                            ob=c0, kb=pb, start=True, stop=True))(),
                            waits=(w2 if first else []))
                        first = False
                        P.on("tensor", (lambda pb=pb, ap=ap, c0=c0, co=co:
                                        lambda e: MM(e,
                            pA[ap][c0:c0 + 16, c0 + 16:c0 + 32],
                            bKu[pb:pb + 64, co:co + 16],
                            bRu[pb:pb + 64, co + 16:co + 32],
                            ob=c0, kb=pb, start=True, stop=True))())
                        if G > 0:
                            P.on("tensor", (lambda fc=fc, pb=pb, ap=ap, G=G,
                                            c0=c0, co=co: lambda e: MM(e,
                                pA[ap][0:32 * G, c0:c0 + 32],
                                kst[G][pb:pb + 64,
                                       fc * 32 * G:(fc + 1) * 32 * G],
                                bR32[pb:pb + 64, co:co + 32],
                                start=True, stop=True))(),
                                inc=(sAsm if G == 3 else None))
                    evw = [(sAsm, 64 * ci + h + 1)]
                    if h == 0 and ci > 0:
                        evw.append((sO, 64 * ci))
                    P.on("scalar", (lambda h=h, ap=ap: lambda e: e.copy(
                        asb[:, h * 128:(h + 1) * 128], pA[ap][:, 0:128]))(),
                        waits=evw, inc=sEvA)
                    P.on("gpsimd", (lambda h=h: lambda e: e.affine_select(
                        asb[:, h * 128:(h + 1) * 128],
                        asb[:, h * 128:(h + 1) * 128],
                        pattern=[[1, 128]], compare_op=ALU.is_ge, fill=0.0,
                        base=-1, channel_multiplier=-1))(),
                        waits=[(sEvA, 64 * ci + h + 1)], inc=sMsk)

                r32v, rcv = v3(bR32), v3(bRc)
                kev, kcv = v3(bKe), v3(bKc)
                P.on("vector", lambda e: e.tensor_copy(
                    rcv[:, :, 0:32], r32v[:, :, 0:32]),
                    waits=[(sAsm, 64 * (ci + 1))], inc=sDrv)
                for (g, Eb) in ((1, E32b), (2, E64b), (3, E96b)):
                    P.on("vector", (lambda g=g, Eb=Eb: lambda e: e.tensor_tensor(
                        rcv[:, :, 32 * g:32 * g + 32],
                        r32v[:, :, 32 * g:32 * g + 32], Eb(32), ALU.mult))(),
                        inc=sDrv)
                P.on("vector", lambda e: e.tensor_copy(
                    kcv[:, :, 96:128], kev[:, :, 96:128]), inc=sDrv)
                for (g, Eb) in ((0, E96b), (1, E64b), (2, E32b)):
                    P.on("vector", (lambda g=g, Eb=Eb: lambda e: e.tensor_tensor(
                        kcv[:, :, 32 * g:32 * g + 32],
                        kev[:, :, 32 * g:32 * g + 32], Eb(32), ALU.mult))(),
                        inc=sDrv)
                P.on("vector", (lambda par=par: lambda e: e.tensor_tensor(
                    v3(bKu), v3(rch[par]), tfb, ALU.mult))(), inc=sDrv)
                P.on("vector", (lambda par=par: lambda e: e.tensor_mul(
                    bRu[:], bKu[:], kch[par][:]))(),
                    waits=[(sDrv, sDrv.n)], inc=sDrv)
                drvM.append(sDrv.n)

                for fc in range(NFC):
                    gg = 32 * ci + fc
                    tp = fc % 2
                    tw = [(sDrv, drvM[ci])] if fc == 0 else []
                    if gg >= 2:
                        tw.append((sTrW, gg - 1))
                    if fc < 2 and ci > 0:
                        tw.append((sTrK, 32 * ci))
                    if fc < 2 and ci == 0:
                        tw.append((sWtrk, 32 * NWC))
                    P.on("tensor", (lambda fc=fc, tp=tp: lambda e: e.transpose(
                        pT[tp][:, 0:128], bRu[:, fc * 128:(fc + 1) * 128],
                        ident[:]))(), waits=tw, inc=sTpw)
                    P.on("scalar", (lambda fc=fc, tp=tp: lambda e: e.copy(
                        bKu[:, fc * 128:(fc + 1) * 128], pT[tp][:, 0:128]))(),
                        waits=[(sTpw, gg + 1)], inc=sTrW)
                for fc in range(NFC):
                    gg = 32 * ci + fc
                    tp = fc % 2
                    tw = [(sTrK, gg - 1)] if gg >= 2 else []
                    if fc < 2:
                        tw.append((sTrW, 32 * (ci + 1)))
                    P.on("tensor", (lambda fc=fc, tp=tp: lambda e: e.transpose(
                        pT[tp][:, 128:256], bKc[:, fc * 128:(fc + 1) * 128],
                        ident[:]))(), waits=tw, inc=sTpk)
                    P.on("scalar", (lambda fc=fc, tp=tp: lambda e: e.copy(
                        bKe[:, fc * 128:(fc + 1) * 128], pT[tp][:, 128:256]))(),
                        waits=[(sTpk, gg + 1)], inc=sTrK)

                P.on("vector", lambda e: e.reduce_sum(
                    cq[:].rearrange("p (h o) -> p h o", o=1),
                    bKu[:].rearrange("p (h d) -> p h d", d=64), AX.X),
                    waits=[(sTrW, 32 * (ci + 1))], inc=sCc)
                cqb = cq[:].rearrange("p (h o) -> p h o", o=1).broadcast_to(
                    [128, 64, 64])
                P.on("vector", (lambda par=par: lambda e: e.tensor_tensor(
                    bKc[:].rearrange("p (h d) -> p h d", d=64),
                    vch[par][:].rearrange("p (h d) -> p h d", d=64),
                    cqb, ALU.mult))(),
                    waits=[(sCc, sCc.n), (sTrK, 32 * (ci + 1)),
                           (sBv[par], 16 * (ci // 2 + 1))], inc=sCc)
                ccM.append(sCc.n)

                for h in range(64):
                    fc, hp = h // 2, h % 2
                    pb = 64 * hp
                    grp = h // 16
                    w4 = [(sMsk, 64 * ci + h + 1)]
                    if h == 0:
                        w4 += [(sTrK, 32 * (ci + 1)), (sStU, 2 * ci)]
                        if ci == 0:
                            w4.append((sWstU, 2 * NWC))
                    if h % 16 == 0 and (ci > 0 or grp > 0):
                        w4.append((sOEv, 4 * ci + grp))
                    if h == 32:
                        w4.append((sStU, 2 * ci + 1))
                    P.on("tensor", (lambda h=h, par=par: lambda e: e.matmul(
                        pO[:, (h % 16) * 64:(h % 16) * 64 + 64],
                        asb[:, h * 128:(h + 1) * 128],
                        vch[par][:, h * 64:(h + 1) * 64],
                        start=True, stop=False))(), waits=w4)
                    P.on("tensor", (lambda fc=fc, pb=pb, h=h: lambda e: e.matmul(
                        pO[:, (h % 16) * 64:(h % 16) * 64 + 64],
                        bRc[pb:pb + 64, fc * 128:(fc + 1) * 128],
                        stb16[pb:pb + 64, fc * 64:(fc + 1) * 64],
                        start=False, stop=True))(), inc=sO)
                    P.on("tensor", (lambda fc=fc, pb=pb, h=h, par=par:
                                    lambda e: MM(e,
                        pS[pb:pb + 64, (fc % 16) * 64:(fc % 16) * 64 + 64],
                        bKe[:, fc * 128 + pb:fc * 128 + pb + 64],
                        vch[par][:, h * 64:(h + 1) * 64],
                        ob=pb, kb=0, start=True, stop=True))(), inc=sS)
                    if h % 16 == 15:
                        evw = [(sO, 64 * ci + 16 * (grp + 1)), (sCc, ccM[ci])]
                        if grp == 0 and ci > 0:
                            evw.append((sObo, 16 * ci))
                        P.on("vector", (lambda grp=grp: lambda e:
                                        e.scalar_tensor_tensor(
                            obch[:, grp * 1024:(grp + 1) * 1024],
                            bKc[:, grp * 1024:(grp + 1) * 1024], 1.0,
                            pO[:], ALU.bypass, ALU.add))(), waits=evw, inc=sOEv)
                    if h in (31, 63):
                        half = 0 if h == 31 else 1
                        hs = half * 1024
                        stv = stf[:, hs:hs + 1024].rearrange(
                            "p (fc d) -> p fc d", d=64)
                        e128h = eE[128][:, half * 16:half * 16 + 16].rearrange(
                            "p (fc o) -> p fc o", o=1).broadcast_to([128, 16, 64])
                        P.on("vector", (lambda stv=stv, e128h=e128h: lambda e:
                                        e.tensor_tensor(stv, stv, e128h,
                                                        ALU.mult))(),
                            waits=[(sS, 64 * ci + 32 * (half + 1)),
                                   (sO, 64 * ci + 32 * (half + 1))], inc=sStC)
                        P.on("vector", (lambda hs=hs: lambda e: e.tensor_add(
                            stf[:, hs:hs + 1024], stf[:, hs:hs + 1024], pS[:]))(),
                            waits=[(sStC, sStC.n)], inc=sStC)
                        P.on("vector", (lambda hs=hs: lambda e: e.tensor_copy(
                            stb16[:, hs:hs + 1024], stf[:, hs:hs + 1024]))(),
                            waits=[(sStC, sStC.n)], inc=sStU)
                P.on("sync", (lambda ci=ci: lambda e: e.dma_start(
                    out=oh[ci * 128:(ci + 1) * 128, :], in_=obch[:]))(),
                    waits=[(sOEv, 4 * (ci + 1))], inc=sObo, amt=16)
            sb_done = [(sObo, sObo.n), (sStU, sStU.n), (sMsk, sMsk.n)]
            P.flush(block)

        with ExitStack() as stc:
            och = [sbuf(f"och{i}", [128, D], BF16, stc) for i in range(2)]
            gch = [sbuf(f"gch{i}", [128, D], BF16, stc) for i in range(2)]
            u1 = sbuf("u1", [128, D], F32, stc)
            u2 = sbuf("u2", [128, D], F32, stc)
            gx = sbuf("gx", [128, D], BF16, stc)
            gxc = [sbuf(f"gxc{i}", [128, D], BF16, stc) for i in range(2)]
            gnwb = sbuf("gnwb", [128, D], F32, stc)
            gnbb = sbuf("gnbb", [128, D], F32, stc)
            mu = sbuf("mu", [128, 64], F32, stc)
            vr = sbuf("vr", [128, 64], F32, stc)
            m2 = sbuf("m2", [128, 64], F32, stc)
            pC = [psum_t(f"pC{i}", [128, 1024], BF16, stc) for i in range(2)]
            sCin = [Sem("sCin0"), Sem("sCin1")]
            sCg = [Sem("sCg0"), Sem("sCg1")]
            sCs = Sem("sCs")
            sCt = Sem("sCt")
            sCe = Sem("sCe")
            sCo = [Sem("sCo0"), Sem("sCo1")]
            sCb = Sem("sCb")

            for (src_row, dst_bc) in ((gnwt, gnwb), (gnbt, gnbb)):
                P.on("sync", (lambda src_row=src_row, dst_bc=dst_bc: lambda e:
                              e.dma_start(
                    out=dst_bc[:],
                    in_=src_row.rearrange("(o d) -> o d", o=1).broadcast_to(
                        [128, D])))(), waits=sb_done, inc=sCb, amt=16)
            cb_done = sCb.n

            csM = []
            for tc in range(NCH):
                par = tc % 2
                ldw = [(sCb, cb_done)] if tc < 2 else [(sCs, csM[tc - 2])]
                P.on("sync", (lambda tc=tc, par=par: lambda e: e.dma_start(
                    out=och[par][:], in_=oh[tc * 128:(tc + 1) * 128, :]))(),
                    waits=ldw, inc=sCin[par], amt=16)
                P.on("sync", (lambda tc=tc, par=par: lambda e: e.dma_start(
                    out=gch[par][:], in_=gth[tc * 128:(tc + 1) * 128, :]))(),
                    inc=sCg[par], amt=16)
                ov = och[par][:].rearrange("p (h d) -> p h d", d=64)
                mub = mu[:].rearrange("p (h o) -> p h o", o=1).broadcast_to(
                    [128, 64, 64])
                vrb = vr[:].rearrange("p (h o) -> p h o", o=1).broadcast_to(
                    [128, 64, 64])
                u13 = u1[:].rearrange("p (h d) -> p h d", d=64)
                u23 = u2[:].rearrange("p (h d) -> p h d", d=64)
                P.on("vector", (lambda ov=ov: lambda e: e.reduce_sum(
                    mu[:].rearrange("p (h o) -> p h o", o=1), ov, AX.X))(),
                    waits=[(sCin[par], 16 * (tc // 2 + 1))], inc=sCs)
                P.on("vector", lambda e: e.tensor_scalar_mul(mu[:], mu[:],
                                                             1.0 / 64),
                     waits=[(sCs, sCs.n)], inc=sCs)
                P.on("vector", (lambda ov=ov, u13=u13, mub=mub: lambda e:
                                e.tensor_tensor(u13, ov, mub, ALU.subtract))(),
                    waits=[(sCs, sCs.n)], inc=sCs)
                P.on("scalar", lambda e: e.activation(m2[:] if False else
                                                      u2[:], u1[:], AF.Square),
                     waits=[(sCs, sCs.n)], inc=sCs)
                P.on("vector", lambda e: e.reduce_sum(
                    vr[:].rearrange("p (h o) -> p h o", o=1),
                    u2[:].rearrange("p (h d) -> p h d", d=64), AX.X),
                    waits=[(sCs, sCs.n)], inc=sCs)
                P.on("vector", lambda e: e.tensor_scalar_mul(vr[:], vr[:],
                                                             1.0 / 64),
                     waits=[(sCs, sCs.n)], inc=sCs)
                P.on("scalar", lambda e: e.activation(vr[:], vr[:], AF.Ln,
                                                      bias=epsc[:]),
                     waits=[(sCs, sCs.n)], inc=sCs)
                P.on("scalar", lambda e: e.activation(vr[:], vr[:], AF.Exp,
                                                      scale=nhc[:]),
                     waits=[(sCs, sCs.n)], inc=sCs)
                P.on("vector", (lambda u13=u13, u23=u23, vrb=vrb: lambda e:
                                e.tensor_tensor(u23, u13, vrb, ALU.mult))(),
                     waits=[(sCs, sCs.n)], inc=sCs)
                P.on("gpsimd", lambda e: e.tensor_mul(u1[:], u2[:], gnwb[:]),
                     waits=[(sCs, sCs.n)], inc=sCs)
                P.on("gpsimd", lambda e: e.tensor_add(u2[:], u1[:], gnbb[:]),
                     waits=[(sCs, sCs.n)], inc=sCs)
                P.on("gpsimd", (lambda par=par, tc=tc: lambda e: e.tensor_mul(
                    gx[:], u2[:], gch[par][:]))(),
                    waits=[(sCg[par], 16 * (tc // 2 + 1)), (sCt, 32 * tc)],
                    inc=sCs)
                csM.append(sCs.n)
                for fc in range(NFC):
                    gg = 32 * tc + fc
                    tp = fc % 2
                    tw = [(sCs, csM[tc])] if fc == 0 else []
                    if gg >= 2:
                        tw.append((sCe, gg - 1))
                    P.on("tensor", (lambda fc=fc, tp=tp: lambda e: e.transpose(
                        pC[tp][:, 0:128],
                        gx[:, fc * 128:(fc + 1) * 128], ident[:]))(),
                        waits=tw, inc=sCt)
                    ew = [(sCt, gg + 1)]
                    if fc == 0:
                        ew.append((sCo[par], 16 * (tc // 2)))
                    P.on("scalar", (lambda fc=fc, tp=tp, par=par: lambda e:
                                    e.copy(gxc[par][:, fc * 128:(fc + 1) * 128],
                                           pC[tp][:, 0:128]))(),
                        waits=ew, inc=sCe)
                P.on("sync", (lambda tc=tc, par=par: lambda e: e.dma_start(
                    out=gxh[tc].rearrange("(fc p) t -> p fc t", p=128),
                    in_=gxc[par][:].rearrange("p (fc t) -> p fc t", t=128)))(),
                    waits=[(sCe, 32 * (tc + 1))], inc=sCo[par], amt=16)
            sc_done = [(sCo[0], sCo[0].n), (sCo[1], sCo[1].n)]
            P.flush(block)

        with ExitStack() as std_:
            gxf = sbuf("gxf", [128, NFC * T], BF16, std_)
            wsl = [sbuf(f"wslD{i}", [128, NFC * 512], BF16, std_)
                   for i in range(2)]
            ares = [sbuf(f"ares{i}", [128, 512], F32, std_) for i in range(2)]
            xrs = [sbuf(f"xrs{i}", [128, 512], F32, std_) for i in range(2)]
            frsd = sbuf("frsd", [128, 512], F32, std_)
            pD = [psum_t(f"pD{i}", [128, 512], F32, std_) for i in range(4)]
            sDin = Sem("sDin")
            sDe2 = sDin
            sDW = [Sem("sDW0"), Sem("sDW1")]
            sDx = [Sem("sDx0"), Sem("sDx1")]
            sDg = Sem("sDg")
            sDe = Sem("sDe")
            sDo = [Sem("sDo0"), Sem("sDo1")]

            for ci in range(NCH):
                P.on("sync", (lambda ci=ci: lambda e: e.dma_start(
                    out=gxf[:].rearrange("p (jc t) -> p jc t", t=T)
                        [:, :, ci * 128:(ci + 1) * 128],
                    in_=gxh[ci].rearrange("(jc p) t -> p jc t", p=128)))(),
                    waits=(sc_done if ci == 0 else []), inc=sDin, amt=16)
            gxf_done = sDin.n
            if gxfd is not None:
                P.on("sync", lambda e: e.dma_start(out=gxfd, in_=gxf[:]),
                     waits=[(sDin, gxf_done)], inc=sDin, amt=16)
                gxf_done = sDin.n
            gD = [0]
            for isl in range(8):
                par = isl % 2
                wldw = [(sDg, 16 * (isl - 1))] if isl >= 2 else list(sc_done)
                P.on("gpsimd", (lambda isl=isl, par=par: lambda e: e.dma_start(
                    out=wsl[par][:].rearrange("p (jc i) -> p jc i", i=512),
                    in_=wT["o"][isl]))(),
                    waits=wldw, inc=sDW[par], amt=16)
                for tc in range(NCH):
                    g = gD[0]
                    xpar = g % 2
                    P.on("sync", (lambda tc=tc, isl=isl, xpar=xpar: lambda e:
                                  e.dma_start(
                        out=xrs[xpar][:],
                        in_=xp[1 + WU + tc * 128:1 + WU + (tc + 1) * 128,
                               isl * 512:(isl + 1) * 512]))(),
                        waits=[(sDe, g - 1)] if g >= 2 else [], inc=sDx[xpar],
                        amt=16)
                    mmw = [(sDW[par], 16 * (isl // 2 + 1))]
                    if g >= 4:
                        mmw.append((sDe, g - 3))
                    if tc == 0:
                        mmw.append((sDin, gxf_done))
                    for jc in range(NFC):
                        P.on("tensor", (lambda tc=tc, isl=isl, jc=jc, par=par,
                                        g=g: lambda e: e.matmul(
                            pD[g % 4][:],
                            gxf[:, jc * T + tc * 128:jc * T + tc * 128 + 128],
                            wsl[par][:, jc * 512:(jc + 1) * 512],
                            start=(jc == 0), stop=(jc == NFC - 1)))(),
                            waits=(mmw if jc == 0 else []),
                            inc=(sDg if jc == NFC - 1 else None))
                    evw = [(sDg, sDg.n), (sDx[xpar], 16 * (g // 2 + 1))]
                    if g >= 2:
                        evw.append((sDo[g % 2], 16 * (g // 2)))
                    if g == 0 and mmd is not None:
                        base2 = sDe2.n
                        P.on("vector", lambda e: e.tensor_copy(
                            frsd[:], pD[0][:]), waits=list(evw), inc=sDe2)
                        P.on("sync", (lambda base2=base2: lambda e: e.dma_start(
                            out=mmd, in_=frsd[:]))(),
                             waits=[(sDe2, base2 + 1)], inc=sDe2, amt=16)
                        P.on("sync", lambda e: e.dma_start(out=xrd,
                                                           in_=xrs[0][:]),
                             waits=[(sDx[0], 16)], inc=sDe2, amt=16)
                    P.on("vector", (lambda g=g, xpar=xpar: lambda e:
                                    e.scalar_tensor_tensor(
                        ares[g % 2][:], xrs[xpar][:], 1.0, pD[g % 4][:],
                        ALU.bypass, ALU.add))(),
                        waits=list(evw) + ([(sDe2, sDe2.n)] if (g == 0 and mmd is not None) else []),
                        inc=sDe)
                    P.on("sync", (lambda tc=tc, isl=isl, g=g: lambda e:
                                  e.dma_start(
                        out=arh[tc * 128:(tc + 1) * 128,
                                isl * 512:(isl + 1) * 512],
                        in_=ares[g % 2][:]))(),
                        waits=[(sDe, sDe.n)], inc=sDo[g % 2], amt=16)
                    gD[0] += 1
            sd_done = [(sDo[0], sDo[0].n), (sDo[1], sDo[1].n)]
            P.flush(block)

        def ln_stage(src_h, dst_h, dst_bf_h, waits0, tag, final_out=None):
            with ExitStack() as stf_:
                art = [sbuf(f"art{tag}{i}", [128, D], F32, stf_)
                       for i in range(2)]
                sq = sbuf(f"sq{tag}", [128, D], F32, stf_)
                xlt = sbuf(f"xlt{tag}", [128, D], F32, stf_)
                xbt = sbuf(f"xbt{tag}", [128, D], BF16, stf_)
                xcl = [sbuf(f"xcl{tag}{i}", [128, D], BF16, stf_)
                       for i in range(2)]
                mcol = sbuf(f"mcol{tag}", [128, 1], F32, stf_)
                vcol = sbuf(f"vcol{tag}", [128, 1], F32, stf_)
                m2c = sbuf(f"m2c{tag}", [128, 1], F32, stf_)
                pL = [psum_t(f"pL{tag}{i}", [128, 1024], BF16, stf_)
                      for i in range(2)]
                sLin = [Sem(f"sLin{tag}0"), Sem(f"sLin{tag}1")]
                sLs = Sem(f"sLs{tag}")
                if final_out is None:
                    sLt = Sem(f"sLt{tag}")
                    sLe = Sem(f"sLe{tag}")
                    sLo = [Sem(f"sLo{tag}0"), Sem(f"sLo{tag}1"),
                           Sem(f"sLo{tag}2")]
                else:
                    sLt = sLe = sLs
                    sLo = [sLs, sLs, Sem(f"sLo{tag}2")]
                lsM = []
                for tc in range(NCH):
                    par = tc % 2
                    ldw = list(waits0) if tc < 2 else [(sLs, lsM[tc - 2])]
                    P.on("sync", (lambda tc=tc, par=par: lambda e: e.dma_start(
                        out=art[par][:],
                        in_=src_h[tc * 128:(tc + 1) * 128, :]))(),
                        waits=ldw, inc=sLin[par], amt=16)
                    P.on("vector", (lambda par=par: lambda e: e.reduce_sum(
                        mcol[:].rearrange("p (a o) -> p a o", o=1),
                        art[par][:].rearrange("p (a d) -> p a d", a=1), AX.X))(),
                        waits=[(sLin[par], 16 * (tc // 2 + 1))], inc=sLs)
                    P.on("vector", lambda e: e.tensor_scalar_mul(
                        mcol[:], mcol[:], 1.0 / D),
                        waits=[(sLs, sLs.n)], inc=sLs)
                    P.on("vector", (lambda par=par: lambda e:
                                    e.tensor_scalar_sub(
                        xlt[:], art[par][:], mcol[:]))(),
                        waits=[(sLs, sLs.n), (sLo[2], 16 * tc)], inc=sLs)
                    P.on("scalar", lambda e: e.activation(
                        sq[:], xlt[:], AF.Square),
                        waits=[(sLs, sLs.n)], inc=sLs)
                    P.on("vector", lambda e: e.reduce_sum(
                        vcol[:].rearrange("p (a o) -> p a o", o=1),
                        sq[:].rearrange("p (a d) -> p a d", a=1), AX.X),
                        waits=[(sLs, sLs.n)], inc=sLs)
                    P.on("vector", lambda e: e.tensor_scalar_mul(
                        vcol[:], vcol[:], 1.0 / D),
                        waits=[(sLs, sLs.n)], inc=sLs)
                    P.on("scalar", lambda e: e.activation(vcol[:], vcol[:],
                                                          AF.Ln, bias=epsc[:]),
                         waits=[(sLs, sLs.n)], inc=sLs)
                    P.on("scalar", lambda e: e.activation(vcol[:], vcol[:],
                                                          AF.Exp,
                                                          scale=nhc[:]),
                         waits=[(sLs, sLs.n)], inc=sLs)
                    P.on("vector", lambda e: e.tensor_scalar_mul(
                        xlt[:], xlt[:], vcol[:]),
                        waits=[(sLs, sLs.n)], inc=sLs)
                    ncount = sLs.n
                    if final_out is None:
                        P.on("sync", (lambda tc=tc: lambda e: e.dma_start(
                            out=dst_h[tc * 128:(tc + 1) * 128, :],
                            in_=xlt[:]))(),
                            waits=[(sLs, ncount), (sLo[2], 16 * tc)],
                            inc=sLo[2], amt=16)
                        P.on("vector", (lambda tc=tc: lambda e: e.tensor_copy(
                            xbt[:], xlt[:]))(),
                            waits=[(sLs, ncount), (sLt, 32 * tc)], inc=sLs)
                        bcount = sLs.n
                        for fc in range(NFC):
                            gg = 32 * tc + fc
                            tp = fc % 2
                            tw = [(sLs, bcount)] if fc == 0 else []
                            if gg >= 2:
                                tw.append((sLe, gg - 1))
                            P.on("tensor", (lambda fc=fc, tp=tp: lambda e:
                                            e.transpose(
                                pL[tp][:, 0:128],
                                xbt[:, fc * 128:(fc + 1) * 128], ident[:]))(),
                                waits=tw, inc=sLt)
                            ew = [(sLt, gg + 1)]
                            if fc == 0:
                                ew.append((sLo[par], 16 * (tc // 2)))
                            P.on("scalar", (lambda fc=fc, tp=tp, par=par:
                                            lambda e: e.copy(
                                xcl[par][:, fc * 128:(fc + 1) * 128],
                                pL[tp][:, 0:128]))(), waits=ew, inc=sLe)
                        P.on("sync", (lambda tc=tc, par=par: lambda e:
                                      e.dma_start(
                            out=dst_bf_h[tc].rearrange("(fc p) t -> p fc t",
                                                       p=128),
                            in_=xcl[par][:].rearrange("p (fc t) -> p fc t",
                                                      t=128)))(),
                            waits=[(sLe, 32 * (tc + 1))], inc=sLo[par], amt=16)
                        lsM.append(bcount)
                    else:
                        P.on("sync", (lambda tc=tc: lambda e: e.dma_start(
                            out=final_out[tc * 128:(tc + 1) * 128, :],
                            in_=xlt[:]))(),
                            waits=[(sLs, ncount), (sLo[2], 16 * tc)],
                            inc=sLo[2], amt=16)
                        lsM.append(ncount)
                done = [(sLo[0], sLo[0].n), (sLo[1], sLo[1].n),
                        (sLo[2], sLo[2].n)]
                P.on("sync", lambda e: None, waits=done)
                P.flush(block)
            return done

        if ard is not None:
            P.on("sync", lambda e: e.dma_start(out=ard, in_=arh),
                 waits=sd_done, inc=sQo, amt=16)
            P.flush(block)
        sf_done = ln_stage(arh, xlh, xlbh, sd_done, "F")
        if xld is not None:
            P.on("sync", lambda e: e.dma_start(out=xld, in_=xlh),
                 waits=sf_done, inc=sQo, amt=16)
            P.flush(block)

        with ExitStack() as stg:
            xlf = sbuf("xlf", [128, NFC * T], BF16, stg)
            wslg = [sbuf(f"wslG{i}", [128, NFC * 512], BF16, stg)
                    for i in range(2)]
            frs = [sbuf(f"frs{i}", [128, 512], F32, stg) for i in range(2)]
            xr2 = [sbuf(f"xr2{i}", [128, 512], F32, stg) for i in range(2)]
            fbc = sbuf("fbc", [128, 512], F32, stg)
            pG = [psum_t(f"pG{i}", [128, 512], F32, stg) for i in range(4)]
            sGin = Sem("sGin")
            sGW = [Sem("sGW0"), Sem("sGW1")]
            sGx = [Sem("sGx0"), Sem("sGx1")]
            sGg = Sem("sGg")
            sGe = Sem("sGe")
            sGb = Sem("sGb")
            sGo = [Sem("sGo0"), Sem("sGo1")]

            for ci in range(NCH):
                P.on("sync", (lambda ci=ci: lambda e: e.dma_start(
                    out=xlf[:].rearrange("p (jc t) -> p jc t", t=T)
                        [:, :, ci * 128:(ci + 1) * 128],
                    in_=xlbh[ci].rearrange("(jc p) t -> p jc t", p=128)))(),
                    waits=(sf_done if ci == 0 else []), inc=sGin, amt=16)
            xlf_done = sGin.n
            gG = [0]
            for isl in range(8):
                par = isl % 2
                wldw = [(sGg, 16 * (isl - 1))] if isl >= 2 else list(sf_done)
                for sp in range(8):
                    P.on("sync", (lambda isl=isl, par=par, sp=sp: lambda e:
                                  e.dma_start(
                        out=wslg[par][:].rearrange(
                            "p (jc i) -> p jc i", i=512)[:, :,
                                                         sp * WQS:(sp + 1) * WQS],
                        in_=wqh[isl * 8 + sp].rearrange(
                            "(jc p) n -> p jc n", p=128)))(),
                        waits=(wldw if sp == 0 else []), inc=sGW[par], amt=16)
                fcw = list(sf_done) if isl == 0 else [(sGe, 2 * 16 * isl)]
                P.on("sync", (lambda isl=isl: lambda e: e.dma_start(
                    out=fbc[:],
                    in_=fbi.rearrange("(o d) -> o d", o=1).broadcast_to(
                        [128, D])[:, isl * 512:(isl + 1) * 512]))(),
                    waits=fcw, inc=sGb, amt=16)
                fb_ready = sGb.n
                for tc in range(NCH):
                    g = gG[0]
                    xpar = g % 2
                    P.on("sync", (lambda tc=tc, isl=isl, xpar=xpar: lambda e:
                                  e.dma_start(
                        out=xr2[xpar][:],
                        in_=xlh[tc * 128:(tc + 1) * 128,
                                isl * 512:(isl + 1) * 512]))(),
                        waits=[(sGe, 2 * (g - 1))] if g >= 2 else [],
                        inc=sGx[xpar], amt=16)
                    mmw = [(sGW[par], 128 * (isl // 2 + 1))]
                    if g >= 4:
                        mmw.append((sGe, 2 * (g - 3)))
                    if tc == 0:
                        mmw.append((sGin, xlf_done))
                    for jc in range(NFC):
                        P.on("tensor", (lambda tc=tc, jc=jc, par=par, g=g:
                                        lambda e: e.matmul(
                            pG[g % 4][:],
                            xlf[:, jc * T + tc * 128:jc * T + tc * 128 + 128],
                            wslg[par][:, jc * 512:(jc + 1) * 512],
                            start=(jc == 0), stop=(jc == NFC - 1)))(),
                            waits=(mmw if jc == 0 else []),
                            inc=(sGg if jc == NFC - 1 else None))
                    evw = [(sGg, sGg.n), (sGx[xpar], 16 * (g // 2 + 1)),
                           (sGb, fb_ready)]
                    if g >= 2:
                        evw.append((sGo[g % 2], 16 * (g // 2)))
                    P.on("vector", (lambda g=g: lambda e:
                                    e.scalar_tensor_tensor(
                        frs[g % 2][:], fbc[:], 1.0, pG[g % 4][:],
                        ALU.bypass, ALU.add))(), waits=evw, inc=sGe)
                    P.on("vector", (lambda g=g, xpar=xpar: lambda e:
                                    e.tensor_add(frs[g % 2][:], frs[g % 2][:],
                                                 xr2[xpar][:]))(),
                        waits=[(sGe, sGe.n)], inc=sGe)
                    P.on("sync", (lambda tc=tc, isl=isl, g=g: lambda e:
                                  e.dma_start(
                        out=arh[tc * 128:(tc + 1) * 128,
                                isl * 512:(isl + 1) * 512],
                        in_=frs[g % 2][:]))(),
                        waits=[(sGe, sGe.n)], inc=sGo[g % 2], amt=16)
                    gG[0] += 1
            sg_done = [(sGo[0], sGo[0].n), (sGo[1], sGo[1].n)]
            P.flush(block)

        ln_stage(arh, None, None, sg_done, "H", final_out=out)

        for (s, v) in P.all_waits:
            assert v <= s.n, f"wait {v} > emitted {s.n} on sem (hang)"

    return nc


def kernel(**inputs):
    x = inputs["L0_attn_hidden"]
    B, S, _ = x.shape
    xpad = np.concatenate([np.zeros((B, WU + 1, D), np.float32),
                           np.asarray(x, np.float32)], axis=1)
    WSLAB = {"k": 128, "r": 128, "v": 256, "g": 256, "o": 512}
    w = {}
    for (m, n) in (("k", "key"), ("v", "value"), ("r", "receptance"),
                   ("g", "gate"), ("o", "output")):
        wi = WSLAB[m]
        A = np.asarray(inputs[f"L0_attn_w_{n}"], np.float32).T
        w[m] = np.ascontiguousarray(
            A.reshape(NFC, 128, D // wi, wi).transpose(2, 1, 0, 3))
    tmix = np.ascontiguousarray(np.stack([
        np.asarray(inputs["L0_attn_time_mix_key"]),
        np.asarray(inputs["L0_attn_time_mix_value"]),
        np.asarray(inputs["L0_attn_time_mix_receptance"]),
        np.asarray(inputs["L0_attn_time_mix_gate"])], axis=0)).astype(np.float32)
    tdec = np.ascontiguousarray(
        np.asarray(inputs["L0_attn_time_decay"]).reshape(-1)).astype(np.float32)
    tfaa = np.ascontiguousarray(
        np.asarray(inputs["L0_attn_time_faaaa"]).reshape(-1)).astype(np.float32)
    gnw = np.ascontiguousarray(np.asarray(inputs["L0_attn_gn_weight"])).astype(
        np.float32)
    gnb = np.ascontiguousarray(np.asarray(inputs["L0_attn_gn_bias"])).astype(
        np.float32)
    WQS = 64
    wf = np.asarray(inputs["L0_ffn_kernel"], np.float32)
    wfn = np.ascontiguousarray(
        wf.reshape(128, 32, D // WQS, WQS).transpose(2, 0, 1, 3))
    fbi = np.ascontiguousarray(np.asarray(inputs["L0_ffn_bias"])).astype(
        np.float32)

    nc = build_program()
    in_maps = []
    for c in range(NCORES):
        b, q = c // 4, c % 4
        xs = np.ascontiguousarray(xpad[b, q * T:q * T + TPW, :])
        m = {"xp": xs, "wfn": wfn, "tmix": tmix, "tdec": tdec, "tfaa": tfaa,
             "gnwt": gnw, "gnbt": gnb, "fbi": fbi}
        for k2 in ("k", "v", "r", "g", "o"):
            m[f"w{k2}T"] = w[k2]
        in_maps.append(m)
    res = run_bass_kernel_spmd(nc, in_maps, core_ids=list(range(NCORES)))
    outp = np.empty((B, S, D), np.float32)
    for c in range(NCORES):
        b, q = c // 4, c % 4
        outp[b, q * T:(q + 1) * T, :] = res.results[c]["out"]
    return outp


# version 12
# speedup vs baseline: 2.3558x; 1.0156x over previous
import numpy as np
import ml_dtypes
import concourse.bass as bass
import concourse.mybir as mybir
from concourse.bass_utils import run_bass_kernel_spmd
from contextlib import ExitStack

F32 = mybir.dt.float32
BF16 = mybir.dt.bfloat16
I32 = mybir.dt.int32
AF = mybir.ActivationFunctionType
ALU = mybir.AluOpType
AX = mybir.AxisListType

D = 4096
T = 2048
WU = 512
NWC = WU // 128
TPW = T + WU + 1
NFC = 32
NCH = 16
CLAMP = 80.0
NCORES = 8
ENGINES = ("sync", "vector", "scalar", "tensor", "gpsimd")

DEBUG_TAPS = []


class Prog:

    def __init__(self, nc):
        self.nc = nc
        self.q = {e: [] for e in ENGINES}
        self.all_waits = []

    def on(self, eng, fn, waits=(), inc=None, amt=1):
        if inc is not None:
            inc.n += amt
        n_after = inc.n if inc is not None else None
        self.q[eng].append((fn, tuple(waits), inc, amt))
        self.all_waits.extend(waits)
        return n_after

    def flush(self, block):
        for e in ENGINES:
            ops = self.q[e]
            if not ops:
                continue

            def body(eng, ops=ops):
                for (fn, waits, inc, amt) in ops:
                    for (s, v) in waits:
                        if v > 0:
                            eng.wait_ge(s.h, v)
                    ins = fn(eng)
                    if inc is not None and ins is not None:
                        ins.then_inc(inc.h, amt)
            getattr(block, e)(body)
        self.q = {e: [] for e in ENGINES}


def MM(t, out, lhsT, rhs, ob=0, kb=0, **kw):
    if 96 in (ob, kb):
        kw.setdefault("tile_position", (kb, ob))
    return t.matmul(out, lhsT, rhs, **kw)


def build_program():
    nc = bass.Bass("TRN2", target_bir_lowering=False)

    xp = nc.dram_tensor("xp", [TPW, D], F32, kind="ExternalInput").ap()
    WSLAB = {"k": 128, "r": 128, "v": 256, "g": 256, "o": 512}
    wT = {m: nc.dram_tensor(f"w{m}T", [D // WSLAB[m], 128, NFC, WSLAB[m]], F32,
                            kind="ExternalInput").ap()
          for m in ("k", "v", "r", "g", "o")}
    WQS = 64
    wfn = nc.dram_tensor("wfn", [D // WQS, 128, 32, WQS], F32,
                         kind="ExternalInput").ap()
    tmix = nc.dram_tensor("tmix", [4, D], F32, kind="ExternalInput").ap()
    tdec = nc.dram_tensor("tdec", [D], F32, kind="ExternalInput").ap()
    tfaa = nc.dram_tensor("tfaa", [D], F32, kind="ExternalInput").ap()
    gnwt = nc.dram_tensor("gnwt", [D], F32, kind="ExternalInput").ap()
    gnbt = nc.dram_tensor("gnbt", [D], F32, kind="ExternalInput").ap()
    fbi = nc.dram_tensor("fbi", [D], F32, kind="ExternalInput").ap()
    out = nc.dram_tensor("out", [T, D], F32, kind="ExternalOutput").ap()

    def internal(name, shape, dt):
        kind = "ExternalOutput" if name in DEBUG_TAPS else "Internal"
        return nc.dram_tensor(name, shape, dt, kind=kind).ap()

    xbf = internal("xbf", [D, TPW], BF16)
    kfh = internal("kfh", [NCH + NWC, D, 128], BF16)
    rfh = internal("rfh", [NCH, D, 128], BF16)
    vth = internal("vth", [T + WU, D], BF16)
    gth = internal("gth", [T, D], BF16)
    oh = internal("oh", [T, D], BF16)
    gxh = internal("gxh", [NCH, D, 128], BF16)
    arh = internal("arh", [T, D], F32)
    xlh = internal("xlh", [T, D], F32)
    xlbh = internal("xlbh", [NCH, D, 128], BF16)
    wqh = internal("wqh", [D // WQS, D, WQS], BF16)
    gxfd = internal("gxfd", [128, NFC * T], BF16) if "gxfd" in DEBUG_TAPS else None
    mmd = internal("mmd", [128, 512], F32) if "mmd" in DEBUG_TAPS else None
    ard = internal("ard", [T, D], F32) if "ard" in DEBUG_TAPS else None
    xld = internal("xld", [T, D], F32) if "xld" in DEBUG_TAPS else None
    xrd = internal("xrd", [128, 512], F32) if "xrd" in DEBUG_TAPS else None

    st = ExitStack()

    def sbuf(n, s, d=F32, stk=None):
        return (stk or st).enter_context(nc.sbuf_tensor(n, s, d))

    def psum_t(n, s, d=F32, stk=None):
        return (stk or st).enter_context(nc.psum_tensor(n, s, d))

    class Sem:
        def __init__(self, name):
            self.h = st.enter_context(nc.semaphore(name))
            self.n = 0

    with st:
        ident = sbuf("ident", [128, 128], BF16)
        Lall = sbuf("Lall", [128, NFC])
        tfall = sbuf("tfall", [128, NFC])
        eE = {k: sbuf(f"eE{k}", [128, NFC]) for k in (32, 64, 96, 128)}
        mixc = sbuf("mixc", [128, 4 * NFC])
        csc = sbuf("csc", [128, NFC])
        epsc = sbuf("epsc", [128, 1])
        nhc = sbuf("nhc", [128, 1])
        ii = sbuf("ii", [128, 128], I32)

        block = st.enter_context(nc.Block())
        P = Prog(nc)

        def bc(ap2d, span):
            return ap2d.rearrange("p (fc o) -> p fc o", o=1).broadcast_to(
                [128, NFC, span])

        s0 = Sem("s0")
        P.on("sync", lambda e: e.dma_start(
            out=Lall[:], in_=tdec.rearrange("(fc p) -> p fc", p=128),
            allow_slow_non_contiguous=True), inc=s0, amt=16)
        P.on("sync", lambda e: e.dma_start(
            out=tfall[:], in_=tfaa.rearrange("(fc p) -> p fc", p=128),
            allow_slow_non_contiguous=True), inc=s0, amt=16)
        P.on("sync", lambda e: e.dma_start(
            out=mixc[:].rearrange("p (m fc) -> p m fc", m=4),
            in_=tmix.rearrange("m (fc p) -> p m fc", p=128),
            allow_slow_non_contiguous=True), inc=s0, amt=16)
        P.on("gpsimd", lambda e: e.iota(ii[:], pattern=[[-1, 128]], base=0,
                                        channel_multiplier=1), inc=s0)
        P.on("vector", lambda e: e.memset(epsc[:], 1e-5), inc=s0)
        P.on("vector", lambda e: e.memset(nhc[:], -0.5), inc=s0)
        P.on("gpsimd", lambda e: e.tensor_single_scalar(
            ident[:], ii[:], 0, ALU.is_equal), waits=[(s0, s0.n)], inc=s0)
        P.on("scalar", lambda e: e.activation(csc[:], Lall[:], AF.Exp),
             waits=[(s0, 16)], inc=s0)
        P.on("vector", lambda e: e.tensor_scalar_mul(Lall[:], csc[:], -1.0),
             waits=[(s0, s0.n)], inc=s0)
        for k in (32, 64, 96, 128):
            P.on("vector", (lambda kk=k: lambda e: e.tensor_scalar_mul(
                csc[:], Lall[:], float(kk)))(), waits=[(s0, s0.n)], inc=s0)
            P.on("scalar", (lambda kk=k: lambda e: e.activation(
                eE[kk][:], csc[:], AF.Exp))(), waits=[(s0, s0.n)], inc=s0)
        s0_done = [(s0, s0.n)]

        with ExitStack() as stx:
            xt = [sbuf(f"xt{i}", [128, D], F32, stx) for i in range(2)]
            xb = sbuf("xb", [128, D], BF16, stx)
            xc = [sbuf(f"xc{i}", [128, D], BF16, stx) for i in range(2)]
            pX = [psum_t(f"pX{i}", [128, 1024], BF16, stx) for i in range(2)]
            sXin = [Sem("sXin0"), Sem("sXin1")]
            sXc = Sem("sXc")
            sXt = Sem("sXt")
            sXe = Sem("sXe")
            sXo = [Sem("sXo0"), Sem("sXo1")]
            NT = (TPW + 127) // 128

            QW = WQS
            qn = 32 * QW
            qw = [sbuf(f"qw{i}", [128, qn], F32, stx) for i in range(2)]
            tq = [sbuf(f"tq{i}", [128, qn], F32, stx) for i in range(2)]
            aq = [sbuf(f"aq{i}", [128, qn], F32, stx) for i in range(2)]
            b1 = [sbuf(f"b1{i}", [128, qn], F32, stx) for i in range(2)]
            b3 = [sbuf(f"b3{i}", [128, qn], F32, stx) for i in range(2)]
            sg = [sbuf(f"sg{i}", [128, qn], F32, stx) for i in range(2)]
            qo = [sbuf(f"qo{i}", [128, qn], BF16, stx) for i in range(2)]
            axq = [sbuf(f"axq{i}", [128, QW], F32, stx) for i in range(2)]
            eiq = [sbuf(f"eiq{i}", [128, QW], I32, stx) for i in range(2)]
            miq = [sbuf(f"miq{i}", [128, QW], I32, stx) for i in range(2)]
            scf = [sbuf(f"scf{i}", [128, QW], F32, stx) for i in range(2)]
            rsf = [sbuf(f"rsf{i}", [128, QW], F32, stx) for i in range(2)]
            sQin = [Sem("sQin0"), Sem("sQin1")]
            sQv = [Sem("sQv0"), Sem("sQv1")]
            sQp = [Sem("sQp0"), Sem("sQp1")]
            sQa = sQv
            sQo = [Sem("sQo0"), Sem("sQo1")]
            C1, C2, C3 = float(2 ** 22), float(2 ** 23), float(2 ** 24)
            NQS = D // QW

            def emit_x_tile(i):
                par = i % 2
                row0 = min(i * 128, TPW - 128)
                ldw = list(s0_done) if i < 2 else [(sXc, i - 1)]
                P.on("sync", (lambda par=par, row0=row0: lambda e: e.dma_start(
                    out=xt[par][:], in_=xp[row0:row0 + 128, :]))(),
                    waits=ldw, inc=sXin[par], amt=16)
                P.on("vector", (lambda par=par: lambda e: e.tensor_copy(
                    xb[:], xt[par][:]))(),
                    waits=[(sXin[par], 16 * (i // 2 + 1)), (sXt, 32 * i)],
                    inc=sXc)
                for fc in range(NFC):
                    g = 32 * i + fc
                    tp = fc % 2
                    tw = [(sXc, i + 1)]
                    if g >= 2:
                        tw.append((sXe, g - 1))
                    P.on("tensor", (lambda fc=fc, tp=tp: lambda e: e.transpose(
                        pX[tp][:, 0:128], xb[:, fc * 128:(fc + 1) * 128],
                        ident[:]))(), waits=tw, inc=sXt)
                    ew = [(sXt, g + 1)]
                    if fc == 0:
                        ew.append((sXo[par], 16 * (i // 2)))
                    P.on("scalar", (lambda fc=fc, tp=tp, par=par: lambda e: e.copy(
                        xc[par][:, fc * 128:(fc + 1) * 128], pX[tp][:, 0:128]))(),
                        waits=ew, inc=sXe)
                P.on("sync", (lambda par=par, row0=row0: lambda e: e.dma_start(
                    out=xbf[:, row0:row0 + 128].rearrange(
                        "(fc p) t -> p fc t", p=128),
                    in_=xc[par][:].rearrange("p (fc t) -> p fc t", t=128)))(),
                    waits=[(sXe, 32 * (i + 1))], inc=sXo[par], amt=16)

            Am_pool, Bm_act, Bm_dve, Cm_pool = {}, {}, {}, {}

            def emit_sq_A(s):
                par = s % 2
                sQ = sQv[par]
                qw3 = qw[par][:].rearrange("p (kr n) -> p kr n", kr=32)
                tq3 = tq[par][:].rearrange("p (kr n) -> p kr n", kr=32)
                rs_bc = rsf[par][:].rearrange("p (n o) -> p o n", o=1
                                              ).broadcast_to([128, 32, QW])
                ldw = [(sQv[par], Am_pool[s - 2])] if s >= 2 else list(s0_done)
                P.on("scalar", (lambda s=s, qw3=qw3: lambda e:
                                e.dma_start(out=qw3, in_=wfn[s]))(),
                    waits=ldw, inc=sQin[par], amt=16)
                P.on("vector", (lambda par=par: lambda e: e.reduce_max(
                    axq[par][:].rearrange("p (n o) -> p n o", o=1),
                    qw[par][:].rearrange("p (kr n) -> p n kr", n=QW),
                    AX.X, apply_absolute_value=True))(),
                    waits=[(sQin[par], 16 * (s // 2 + 1)), (sQv[par], sQv[par].n)],
                    inc=sQ)
                for fn in (
                    (lambda par=par: lambda e: e.tensor_scalar_max(
                        axq[par][:], axq[par][:], 1e-30))(),
                    (lambda par=par: lambda e: e.tensor_single_scalar(
                        eiq[par][:], axq[par][:].bitcast(I32), 23,
                        ALU.logical_shift_right))(),
                    (lambda par=par: lambda e: e.tensor_single_scalar(
                        miq[par][:], axq[par][:].bitcast(I32), 0x7FFFFF,
                        ALU.bitwise_and))(),
                    (lambda par=par: lambda e: e.tensor_single_scalar(
                        miq[par][:], miq[par][:], 0x400000, ALU.is_gt))(),
                    (lambda par=par: lambda e: e.tensor_tensor(
                        eiq[par][:], eiq[par][:], miq[par][:], ALU.add))(),
                    (lambda par=par: lambda e: e.tensor_single_scalar(
                        eiq[par][:], eiq[par][:], 2, ALU.subtract))(),
                ):
                    P.on("vector", fn, waits=[(sQ, sQ.n)], inc=sQ)
                scw = [(sQ, sQ.n)]
                if s >= 2:
                    scw.append((sQv[par], Cm_pool[s - 2]))
                P.on("vector", (lambda par=par: lambda e: e.tensor_single_scalar(
                    scf[par][:].bitcast(I32), eiq[par][:], 23,
                    ALU.logical_shift_left))(), waits=scw, inc=sQ)
                P.on("vector", (lambda par=par: lambda e: e.tensor_scalar(
                    miq[par][:], eiq[par][:], -1, 254, ALU.mult, ALU.add))(),
                    waits=[(sQ, sQ.n)], inc=sQ)
                P.on("vector", (lambda par=par: lambda e: e.tensor_single_scalar(
                    rsf[par][:].bitcast(I32), miq[par][:], 23,
                    ALU.logical_shift_left))(), waits=[(sQ, sQ.n)], inc=sQ)
                dve_scale = sQ.n
                tqw = [(sQv[par], dve_scale)]
                if s >= 2:
                    tqw.append((sQa[par], Bm_act[s - 2]))
                P.on("vector", (lambda tq3=tq3, qw3=qw3, rs_bc=rs_bc: lambda e:
                                e.tensor_tensor(tq3, qw3, rs_bc, ALU.mult))(),
                    waits=tqw, inc=sQv[par])
                Am_pool[s] = sQv[par].n

            def emit_sq_B(s):
                par = s % 2
                sQ = sQv[par]
                sQ = sQv[par]
                abw = [(sQv[par], Am_pool[s]), (sQa[par], sQa[par].n)]
                if s >= 2:
                    abw.append((sQv[par], Cm_pool[s - 2]))
                P.on("scalar", (lambda par=par: lambda e: e.activation(
                    aq[par][:], tq[par][:], AF.Abs))(), waits=abw, inc=sQa[par])
                P.on("scalar", (lambda par=par: lambda e: e.activation(
                    sg[par][:], tq[par][:], AF.Sign))(),
                    waits=[(sQa[par], sQa[par].n)], inc=sQa[par])
                Bm_act[s] = sQa[par].n
                P.on("vector", (lambda par=par: lambda e: e.tensor_scalar(
                    b1[par][:], aq[par][:], C1, C1, ALU.add, ALU.subtract))(),
                    waits=[(sQa[par], Bm_act[s] - 1), (sQv[par], sQv[par].n)],
                    inc=sQ)
                P.on("vector", (lambda par=par: lambda e: e.tensor_scalar(
                    b3[par][:], aq[par][:], C3, C3, ALU.add, ALU.subtract))(),
                    waits=[(sQ, sQ.n)], inc=sQ)
                P.on("vector", (lambda par=par: lambda e: e.tensor_scalar(
                    aq[par][:], aq[par][:], C2, C2, ALU.add, ALU.subtract))(),
                    waits=[(sQ, sQ.n)], inc=sQ)
                P.on("vector", (lambda par=par: lambda e: e.tensor_scalar_min(
                    b1[par][:], b1[par][:], 2.0))(), waits=[(sQ, sQ.n)], inc=sQ)
                P.on("vector", (lambda par=par: lambda e: e.tensor_scalar(
                    aq[par][:], aq[par][:], 2.0, 0.0, ALU.subtract, ALU.max))(),
                    waits=[(sQ, sQ.n)], inc=sQ)
                P.on("vector", (lambda par=par: lambda e: e.tensor_scalar(
                    b3[par][:], b3[par][:], 4.0, 0.0, ALU.subtract, ALU.max))(),
                    waits=[(sQ, sQ.n)], inc=sQ)
                Bm_dve[s] = sQ.n

            def emit_sq_C(s):
                par = s % 2
                sQ = sQv[par]
                sg3 = sg[par][:].rearrange("p (kr n) -> p kr n", kr=32)
                sc_bc = scf[par][:].rearrange("p (n o) -> p o n", o=1
                                              ).broadcast_to([128, 32, QW])
                P.on("vector", (lambda par=par: lambda e:
                                e.scalar_tensor_tensor(
                    b1[par][:], aq[par][:], 2.0, b1[par][:],
                    ALU.min, ALU.add))(), waits=[(sQ, sQ.n)], inc=sQ)
                P.on("vector", (lambda par=par: lambda e:
                                e.scalar_tensor_tensor(
                    b1[par][:], b3[par][:], 2.0, b1[par][:],
                    ALU.min, ALU.add))(), waits=[(sQ, sQ.n)], inc=sQ)
                P.on("vector", (lambda sg3=sg3, sc_bc=sc_bc, sQ=sQ: lambda e:
                                e.tensor_tensor(sg3, sg3, sc_bc, ALU.mult))(),
                    waits=[(sQ, sQ.n)], inc=sQ)
                P.on("vector", (lambda par=par, s=s: lambda e: e.tensor_mul(
                    qo[par][:], b1[par][:], sg[par][:]))(),
                    waits=[(sQ, sQ.n), (sQo[par], 16 * (s // 2))],
                    inc=sQ)
                Cm_pool[s] = sQ.n
                sQp[par].n += 0
                P.on("scalar", (lambda s=s, par=par: lambda e: e.dma_start(
                    out=wqh[s].rearrange("(kb kr) n -> kb kr n", kr=32),
                    in_=qo[par][:].rearrange("p (kr n) -> p kr n", kr=32)))(),
                    waits=[(sQv[par], Cm_pool[s])], inc=sQo[par], amt=16)

            for step in range(max(NT, NQS + 2)):
                if step < NT:
                    emit_x_tile(step)
                if 0 <= step - 2 < NQS:
                    emit_sq_C(step - 2)
                if 0 <= step - 1 < NQS:
                    emit_sq_B(step - 1)
                if step < NQS:
                    emit_sq_A(step)
            sx_done = [(sXo[0], sXo[0].n), (sXo[1], sXo[1].n)]
            sq_done = [(sQo[0], sQo[0].n), (sQo[1], sQo[1].n),
                       (sQp[0], sQp[0].n), (sQp[1], sQp[1].n)]
            P.flush(block)

        MIDX = {"k": 0, "v": 1, "r": 2, "g": 3}
        with ExitStack() as sta:
            mixf = sbuf("mixf", [128, NFC * T], BF16, sta)
            mixw = sbuf("mixw", [128, NFC * WU], BF16, sta)
            obf = [sbuf(f"obf{i}", [128, 512], BF16, sta) for i in range(4)]
            pP = [psum_t(f"pP{i}", [128, 512], F32, sta) for i in range(8)]
            sW = [Sem("sAW0"), Sem("sAW1")]
            sMMg = Sem("sAMMg")
            sEv = Sem("sAEv")
            sAOut = [Sem(f"sAOut{i}") for i in range(4)]
            sMx = Sem("sAMx")
            sXr = [Sem("sAXr0"), Sem("sAXr1")]
            ga = [0]
            wa = [0]

            def build_mix(m, waits0, tag, warm=False):
                mxM = []
                with ExitStack() as stm:
                    xr = [sbuf(f"xr{tag}{i}", [128, TPW], BF16, stm)
                          for i in range(2)]
                    for jc in range(NFC):
                        par = jc % 2
                        ldw = list(waits0) if jc < 2 else \
                            [(sMx, mxM[jc - 2])]
                        P.on("sync", (lambda jc=jc, par=par: lambda e:
                                      e.dma_start(out=xr[par][:],
                                                  in_=xbf[jc * 128:(jc + 1) * 128,
                                                          :]))(),
                            waits=ldw, inc=sXr[par], amt=16)
                        mfs = mixf[:, jc * T:(jc + 1) * T]
                        meng = "vector"
                        P.on(meng, (lambda par=par, mfs=mfs: lambda e:
                                    e.tensor_sub(mfs, xr[par][:, WU + 1:TPW],
                                                 xr[par][:, WU:WU + T]))(),
                            waits=[(sXr[par], sXr[par].n)] + list(waits0),
                            inc=sMx)
                        if meng == "vector":
                            P.on(meng, (lambda jc=jc, par=par, m=m, mfs=mfs:
                                        lambda e: e.scalar_tensor_tensor(
                                mfs, mfs,
                                mixc[:, MIDX[m] * NFC + jc:
                                     MIDX[m] * NFC + jc + 1],
                                xr[par][:, WU:WU + T], ALU.mult, ALU.add))(),
                                waits=[(sMx, sMx.n)], inc=sMx)
                        else:
                            P.on(meng, (lambda jc=jc, m=m, mfs=mfs: lambda e:
                                        e.tensor_single_scalar(
                                mfs, mfs,
                                mixc[:, MIDX[m] * NFC + jc:
                                     MIDX[m] * NFC + jc + 1], ALU.mult))(),
                                waits=[(sMx, sMx.n)], inc=sMx)
                            P.on(meng, (lambda par=par, mfs=mfs: lambda e:
                                        e.tensor_add(mfs, mfs,
                                                     xr[par][:, WU:WU + T]))(),
                                waits=[(sMx, sMx.n)], inc=sMx)
                        if warm:
                            mws = mixw[:, jc * WU:(jc + 1) * WU]
                            P.on(meng, (lambda par=par, mws=mws: lambda e:
                                        e.tensor_sub(mws,
                                                     xr[par][:, 1:WU + 1],
                                                     xr[par][:, 0:WU]))(),
                                waits=[(sMx, sMx.n)], inc=sMx)
                            if meng == "vector":
                                P.on(meng, (lambda jc=jc, par=par, m=m, mws=mws:
                                            lambda e: e.scalar_tensor_tensor(
                                    mws, mws,
                                    mixc[:, MIDX[m] * NFC + jc:
                                         MIDX[m] * NFC + jc + 1],
                                    xr[par][:, 0:WU], ALU.mult, ALU.add))(),
                                    waits=[(sMx, sMx.n)], inc=sMx)
                            else:
                                P.on(meng, (lambda jc=jc, m=m, mws=mws: lambda e:
                                            e.tensor_single_scalar(
                                    mws, mws,
                                    mixc[:, MIDX[m] * NFC + jc:
                                         MIDX[m] * NFC + jc + 1], ALU.mult))(),
                                    waits=[(sMx, sMx.n)], inc=sMx)
                                P.on(meng, (lambda par=par, mws=mws: lambda e:
                                            e.tensor_add(
                                    mws, mws, xr[par][:, 0:WU]))(),
                                    waits=[(sMx, sMx.n)], inc=sMx)
                        mxM.append(sMx.n)
                    P.flush(block)
                return [(sMx, sMx.n)]

            def proj(m, dst, fm, waits0, tag, act=None, warm=False):
                mix_done = build_mix(m, waits0, tag, warm=warm)
                with ExitStack() as stw:
                    wi = 128 if fm else 256
                    nslabs = NFC if fm else NFC // 2
                    wsl = [sbuf(f"wsl{tag}{i}", [128, NFC * wi], BF16, stw)
                           for i in range(2)]
                    if fm:
                        subs = [("w", s) for s in range(1 if warm else 0)] + \
                            [("m", s) for s in range(4)]
                    else:
                        subs = [("w", s) for s in range(NWC if warm else 0)] + \
                            [("m", s) for s in range(NCH)]
                    row_off = WU if (not fm and warm) else 0
                    for ns in range(nslabs):
                        par = wa[0] % 2
                        wldw = []
                        if wa[0] >= 2:
                            wldw = [(sMMg, slab_gdone[wa[0] - 2])]
                        if wa[0] == 0:
                            wldw += list(sq_done)
                        if ns < 2:
                            wldw += list(mix_done)
                        P.on("gpsimd", (lambda ns=ns, par=par, m=m, wi=wi:
                                        lambda e: e.dma_start(
                            out=wsl[par][:].rearrange("p (jc i) -> p jc i", i=wi),
                            in_=wT[m][ns]))(),
                            waits=wldw, inc=sW[par], amt=16)
                        wcount = sW[par].n
                        for (kind, sub) in subs:
                            g = ga[0]
                            mmw = [(sW[par], wcount)]
                            if g >= 8:
                                mmw.append((sEv, g - 7))
                            if (kind, sub) == subs[0]:
                                mmw += mix_done
                            for jc in range(NFC):
                                if fm:
                                    rhs_ap = (mixw[:, jc * WU:jc * WU + 512]
                                              if kind == "w" else
                                              mixf[:, jc * T + sub * 512:
                                                   jc * T + sub * 512 + 512])
                                    fn = (lambda jc=jc, par=par, g=g,
                                          rhs_ap=rhs_ap: lambda e: e.matmul(
                                        pP[g % 8][:],
                                        wsl[par][:, jc * 128:(jc + 1) * 128],
                                        rhs_ap,
                                        start=(jc == 0), stop=(jc == NFC - 1)))()
                                else:
                                    lhs_ap = (mixw[:, jc * WU + sub * 128:
                                                   jc * WU + sub * 128 + 128]
                                              if kind == "w" else
                                              mixf[:, jc * T + sub * 128:
                                                   jc * T + sub * 128 + 128])
                                    fn = (lambda jc=jc, par=par, g=g, wi=wi,
                                          lhs_ap=lhs_ap: lambda e: e.matmul(
                                        pP[g % 8][:, 0:wi],
                                        lhs_ap,
                                        wsl[par][:, jc * wi:(jc + 1) * wi],
                                        start=(jc == 0), stop=(jc == NFC - 1)))()
                                P.on("tensor", fn,
                                     waits=(mmw if jc == 0 else []),
                                     inc=(sMMg if jc == NFC - 1 else None))
                            evw = [(sMMg, sMMg.n)]
                            if g >= 4:
                                evw.append((sAOut[g % 4], 16 * (g // 4)))
                            wid = 512 if fm else wi
                            if act is None:
                                P.on("scalar", (lambda g=g, wid=wid: lambda e:
                                                e.copy(obf[g % 4][:, 0:wid],
                                                       pP[g % 8][:, 0:wid]))(),
                                    waits=evw, inc=sEv)
                            else:
                                P.on("scalar", (lambda g=g, act=act, wid=wid:
                                                lambda e: e.activation(
                                    obf[g % 4][:, 0:wid],
                                    pP[g % 8][:, 0:wid], act))(),
                                     waits=evw, inc=sEv)
                            if fm:
                                c0 = (sub * 4 if kind == "w" else
                                      NWC * (1 if warm else 0) + 4 * sub)
                                st_fn = (lambda ns=ns, c0=c0, g=g: lambda e:
                                         e.dma_start(
                                    out=dst[c0:c0 + 4,
                                            ns * 128:(ns + 1) * 128,
                                            :].rearrange("c p t -> p c t"),
                                    in_=obf[g % 4][:].rearrange(
                                        "p (c t) -> p c t", t=128)))()
                            else:
                                r0_ = (sub * 128 if kind == "w" else
                                       row_off + sub * 128)
                                st_fn = (lambda ns=ns, r0_=r0_, g=g, wi=wi:
                                         lambda e: e.dma_start(
                                    out=dst[r0_:r0_ + 128,
                                            ns * wi:(ns + 1) * wi],
                                    in_=obf[g % 4][:, 0:wi]))()
                            P.on("scalar", st_fn, waits=[(sEv, sEv.n)],
                                 inc=sAOut[g % 4], amt=16)
                            ga[0] += 1
                        slab_gdone.append(sMMg.n)
                        wa[0] += 1
                    P.flush(block)
                return [(sMMg, sMMg.n)]

            slab_gdone = []
            d1 = proj("k", kfh, True, sx_done + sq_done, "k", warm=True)
            d2 = proj("r", rfh, True, d1, "r")
            d3 = proj("v", vth, False, d2, "v", warm=True)
            d4 = proj("g", gth, False, d3, "g", act=AF.Silu)
            sa_done = [(sAOut[i], sAOut[i].n) for i in range(4)] + \
                [(sEv, sEv.n)]
            P.flush(block)

        with ExitStack() as stb:
            tb = {n: sbuf(f"tb_{n}", [128, NFC * 128], BF16, stb)
                  for n in ("rt16c", "kt16c", "rt16u", "kt16u", "rt32", "kt32e")}
            bRu = sbuf("bRu", [128, NFC * 128], BF16, stb)
            bRc = sbuf("bRc", [128, NFC * 128], BF16, stb)
            bR32 = sbuf("bR32", [128, NFC * 128], BF16, stb)
            bKu = sbuf("bKu", [128, NFC * 128], BF16, stb)
            bKc = sbuf("bKc", [128, NFC * 128], BF16, stb)
            bKe = sbuf("bKe", [128, NFC * 128], BF16, stb)
            kst = {G: sbuf(f"kst{G}", [128, NFC * 32 * G], BF16, stb)
                   for G in (1, 2, 3)}
            asb = sbuf("asb", [128, 64 * 128], BF16, stb)
            stf = sbuf("stf", [128, NFC * 64], F32, stb)
            stb16 = sbuf("stb16", [128, NFC * 64], BF16, stb)
            cq = sbuf("cq", [128, 64], F32, stb)

            pA = [psum_t("pA0", [128, 512], F32, stb),
                  psum_t("pA1", [128, 512], F32, stb)]
            pT = [psum_t("pT0", [128, 1024], BF16, stb),
                  psum_t("pT1", [128, 1024], BF16, stb)]
            pO = psum_t("pO", [128, 1024], F32, stb)
            pS = psum_t("pS", [128, 1024], F32, stb)

            sTab = Sem("sTab")
            sBk = [Sem("sBk0"), Sem("sBk1")]
            sBr = [Sem("sBr0"), Sem("sBr1")]
            sBv = [Sem("sBv0"), Sem("sBv1")]
            sPre = Sem("sPre")
            sAsm = Sem("sAsm")
            sEvA = Sem("sEvA")
            sMsk = Sem("sMsk")
            sDrv = Sem("sDrv")
            sTpw = Sem("sTpw")
            sTpk = Sem("sTpk")
            sTrW = Sem("sTrW")
            sTrK = Sem("sTrK")
            sCc = Sem("sCc")
            sO = Sem("sO")
            sS = Sem("sS")
            sOEv = Sem("sOEv")
            sStU = Sem("sStU")
            sStC = Sem("sStC")
            sObo = Sem("sObo")
            sWk = [Sem("sWk0"), Sem("sWk1")]
            sWv = [Sem("sWv0"), Sem("sWv1")]
            sWpre = Sem("sWpre")
            sWtpk = Sem("sWtpk")
            sWtrk = Sem("sWtrk")
            sWs = Sem("sWs")
            sWstU = Sem("sWstU")

            Lb = bc(Lall[:], 128)
            sttab = ExitStack()
            itb = sbuf("itb", [128, NFC * 128], I32, sttab)
            ftb = sbuf("ftb", [128, NFC * 128], F32, sttab)
            f3 = ftb[:].rearrange("p (fc t) -> p fc t", t=128)
            for (nm, base, pat, clamp) in (
                ("rt16c", -8, [[0, NFC], [0, 8], [1, 16]], True),
                ("kt16c", 7, [[0, NFC], [0, 8], [-1, 16]], True),
                ("rt16u", 0, [[0, NFC], [0, 8], [1, 16]], False),
                ("kt16u", 15, [[0, NFC], [0, 8], [-1, 16]], False),
                ("rt32", 0, [[0, NFC], [0, 4], [1, 32]], False),
                ("kt32e", 31, [[0, NFC], [0, 4], [-1, 32]], False),
            ):
                P.on("gpsimd", (lambda pat=pat, base=base: lambda e: e.iota(
                    itb[:], pattern=pat, base=base, channel_multiplier=0))(),
                    waits=[(sTab, sTab.n)] +
                          (list(s0_done) + sa_done if nm == "rt16c" else []),
                    inc=sTab)
                P.on("vector", lambda e: e.tensor_copy(ftb[:], itb[:]),
                     waits=[(sTab, sTab.n)], inc=sTab)
                P.on("vector", lambda e: e.tensor_tensor(f3, f3, Lb, ALU.mult),
                     waits=[(sTab, sTab.n)], inc=sTab)
                if clamp:
                    P.on("vector", lambda e: e.tensor_scalar_min(
                        ftb[:], ftb[:], CLAMP), waits=[(sTab, sTab.n)], inc=sTab)
                P.on("scalar", (lambda nm=nm: lambda e: e.activation(
                    tb[nm][:], ftb[:], AF.Exp))(), waits=[(sTab, sTab.n)],
                    inc=sTab)
            P.on("vector", lambda e: e.memset(stf[:], 0.0),
                 waits=[(sTab, sTab.n)], inc=sTab)
            P.on("vector", lambda e: e.memset(stb16[:], 0.0),
                 waits=[(sTab, sTab.n)], inc=sTab)
            tab_done = sTab.n
            P.flush(block)
            sttab.close()
            kch = [sbuf(f"kch{i}", [128, NFC * 128], BF16, stb) for i in range(2)]
            rch = [sbuf(f"rch{i}", [128, NFC * 128], BF16, stb) for i in range(2)]
            vch = [sbuf(f"vch{i}", [128, D], BF16, stb) for i in range(2)]
            obch = sbuf("obch", [128, D], BF16, stb)

            E32b = lambda span: bc(eE[32][:], span)
            E64b = lambda span: bc(eE[64][:], span)
            E96b = lambda span: bc(eE[96][:], span)
            tfb = bc(tfall[:], 128)

            def v3(buf, span=128):
                return buf[:].rearrange("p (fc t) -> p fc t", t=span)

            for wci in range(NWC):
                par = wci % 2
                ldw = []
                if wci == 0:
                    ldw = sa_done + [(sTab, tab_done)]
                elif wci >= 2:
                    ldw = [(sWs, 64 * (wci - 1)), (sWpre, 5 * (wci - 1))]
                P.on("sync", (lambda wci=wci, par=par: lambda e: e.dma_start(
                    out=v3(kch[par]),
                    in_=kfh[wci].rearrange("(fc p) t -> p fc t", p=128)))(),
                    waits=ldw, inc=sWk[par], amt=16)
                P.on("sync", (lambda wci=wci, par=par: lambda e: e.dma_start(
                    out=vch[par][:],
                    in_=vth[wci * 128:(wci + 1) * 128, :]))(),
                    inc=sWv[par], amt=16)
                w1 = [(sWk[par], 16 * (wci // 2 + 1)), (sTab, tab_done)]
                if wci > 0:
                    w1 += [(sWtpk, 32 * wci), (sWs, 64 * wci)]
                kev = v3(bKe)
                kcv = v3(bKc)
                P.on("vector", (lambda par=par: lambda e: e.tensor_mul(
                    bKe[:], kch[par][:], tb["kt32e"][:]))(), waits=w1, inc=sWpre)
                P.on("vector", lambda e: e.tensor_copy(
                    kcv[:, :, 96:128], kev[:, :, 96:128]),
                    waits=[(sWpre, sWpre.n)], inc=sWpre)
                for (g, Eb) in ((0, E96b), (1, E64b), (2, E32b)):
                    P.on("vector", (lambda g=g, Eb=Eb: lambda e: e.tensor_tensor(
                        kcv[:, :, 32 * g:32 * g + 32],
                        kev[:, :, 32 * g:32 * g + 32], Eb(32), ALU.mult))(),
                        inc=sWpre)
                wdrv = sWpre.n
                for fc in range(NFC):
                    gg = 32 * wci + fc
                    tp = fc % 2
                    tw = [(sWpre, wdrv)] if fc == 0 else []
                    if gg >= 2:
                        tw.append((sWtrk, gg - 1))
                    P.on("tensor", (lambda fc=fc, tp=tp: lambda e: e.transpose(
                        pT[tp][:, 128:256], bKc[:, fc * 128:(fc + 1) * 128],
                        ident[:]))(), waits=tw, inc=sWtpk)
                    P.on("scalar", (lambda fc=fc, tp=tp: lambda e: e.copy(
                        bKe[:, fc * 128:(fc + 1) * 128],
                        pT[tp][:, 128:256]))(),
                        waits=[(sWtpk, gg + 1)], inc=sWtrk)
                for h in range(64):
                    fc, hp = h // 2, h % 2
                    pb = 64 * hp
                    w4 = []
                    if h == 0:
                        w4 = [(sWtrk, 32 * (wci + 1)), (sWstU, 2 * wci)]
                    if h == 32:
                        w4.append((sWstU, 2 * wci + 1))
                    P.on("tensor", (lambda fc=fc, pb=pb, h=h, par=par:
                                    lambda e: MM(e,
                        pS[pb:pb + 64, (fc % 16) * 64:(fc % 16) * 64 + 64],
                        bKe[:, fc * 128 + pb:fc * 128 + pb + 64],
                        vch[par][:, h * 64:(h + 1) * 64],
                        ob=pb, kb=0, start=True, stop=True))(),
                        waits=w4, inc=sWs)
                    if h in (31, 63):
                        half = 0 if h == 31 else 1
                        hs = half * 1024
                        stv = stf[:, hs:hs + 1024].rearrange(
                            "p (fc d) -> p fc d", d=64)
                        e128h = eE[128][:, half * 16:half * 16 + 16].rearrange(
                            "p (fc o) -> p fc o", o=1).broadcast_to(
                            [128, 16, 64])
                        wv = [(sWs, 64 * wci + 32 * (half + 1)),
                              (sWv[par], 16 * (wci // 2 + 1))]
                        P.on("vector", (lambda stv=stv, e128h=e128h: lambda e:
                                        e.tensor_tensor(stv, stv, e128h,
                                                        ALU.mult))(),
                            waits=wv, inc=sStC)
                        P.on("vector", (lambda hs=hs: lambda e: e.tensor_add(
                            stf[:, hs:hs + 1024], stf[:, hs:hs + 1024],
                            pS[:]))(), waits=[(sStC, sStC.n)], inc=sStC)
                        P.on("vector", (lambda hs=hs: lambda e: e.tensor_copy(
                            stb16[:, hs:hs + 1024], stf[:, hs:hs + 1024]))(),
                            waits=[(sStC, sStC.n)], inc=sWstU)
            warm_done = [(sWstU, sWstU.n), (sWs, sWs.n), (sWpre, sWpre.n),
                         (sWtpk, sWtpk.n), (sWtrk, sWtrk.n)]

            preM, drvM, ccM = [], [], []
            for ci in range(NCH):
                par = ci % 2
                ldw = []
                if ci == 0:
                    ldw = list(warm_done)
                elif ci >= 2:
                    ldw = [(sDrv, drvM[ci - 1]), (sS, 64 * (ci - 1)),
                           (sCc, ccM[ci - 1])]
                P.on("sync", (lambda ci=ci, par=par: lambda e: e.dma_start(
                    out=v3(kch[par]),
                    in_=kfh[NWC + ci].rearrange("(fc p) t -> p fc t",
                                                p=128)))(),
                    waits=ldw, inc=sBk[par], amt=16)
                P.on("sync", (lambda ci=ci, par=par: lambda e: e.dma_start(
                    out=v3(rch[par]),
                    in_=rfh[ci].rearrange("(fc p) t -> p fc t", p=128)))(),
                    inc=sBr[par], amt=16)
                P.on("sync", (lambda ci=ci, par=par: lambda e: e.dma_start(
                    out=vch[par][:],
                    in_=vth[WU + ci * 128:WU + (ci + 1) * 128, :]))(),
                    inc=sBv[par], amt=16)

                p1w = [(sBk[par], 16 * (ci // 2 + 1)),
                       (sBr[par], 16 * (ci // 2 + 1)), (sTab, tab_done)]
                if ci == 0:
                    p1w += list(warm_done)
                if ci > 0:
                    p1w += [(sO, 64 * ci), (sS, 64 * ci), (sCc, ccM[ci - 1]),
                            (sOEv, 4 * ci)]
                P.on("vector", (lambda par=par: lambda e: e.tensor_mul(
                    bRu[:], rch[par][:], tb["rt16u"][:]))(), waits=p1w, inc=sPre)
                for (dstb, srcb, tbl) in ((bRc, rch, "rt16c"), (bR32, rch, "rt32"),
                                          (bKu, kch, "kt16u"), (bKc, kch, "kt16c"),
                                          (bKe, kch, "kt32e")):
                    P.on("vector", (lambda dstb=dstb, srcb=srcb, tbl=tbl, par=par:
                                    lambda e: e.tensor_mul(
                        dstb[:], srcb[par][:], tb[tbl][:]))(), inc=sPre)
                ke3 = v3(bKe)

                def ksl(G, lo, hi):
                    return kst[G][:].rearrange("p (fc j) -> p fc j",
                                               j=32 * G)[:, :, lo:hi]
                P.on("vector", lambda e: e.tensor_copy(ksl(1, 0, 32),
                                                       ke3[:, :, 0:32]),
                     waits=[(sPre, sPre.n)], inc=sPre)
                P.on("vector", lambda e: e.tensor_tensor(
                    ksl(2, 0, 32), ke3[:, :, 0:32], E32b(32), ALU.mult), inc=sPre)
                P.on("vector", lambda e: e.tensor_copy(ksl(2, 32, 64),
                                                       ke3[:, :, 32:64]), inc=sPre)
                P.on("vector", lambda e: e.tensor_tensor(
                    ksl(3, 0, 32), ke3[:, :, 0:32], E64b(32), ALU.mult), inc=sPre)
                P.on("vector", lambda e: e.tensor_tensor(
                    ksl(3, 32, 64), ke3[:, :, 32:64], E32b(32), ALU.mult),
                    inc=sPre)
                P.on("vector", lambda e: e.tensor_copy(ksl(3, 64, 96),
                                                       ke3[:, :, 64:96]), inc=sPre)
                preM.append(sPre.n)

                for h in range(64):
                    fc, hp = h // 2, h % 2
                    pb = 64 * hp
                    ap = h % 2
                    w2 = []
                    if h == 0:
                        w2.append((sPre, preM[ci]))
                    if 64 * ci + h >= 2:
                        w2.append((sEvA, 64 * ci + h - 1))
                    first = True
                    for G in range(4):
                        c0 = 32 * G
                        co = fc * 128 + c0
                        P.on("tensor", (lambda pb=pb, ap=ap, c0=c0, co=co:
                                        lambda e: MM(e,
                            pA[ap][c0:c0 + 32, c0:c0 + 32],
                            bKc[pb:pb + 64, co:co + 32],
                            bRc[pb:pb + 64, co:co + 32],
                            ob=c0, kb=pb, start=True, stop=True))(),
                            waits=(w2 if first else []))
                        first = False
                        P.on("tensor", (lambda pb=pb, ap=ap, c0=c0, co=co:
                                        lambda e: MM(e,
                            pA[ap][c0:c0 + 16, c0 + 16:c0 + 32],
                            bKu[pb:pb + 64, co:co + 16],
                            bRu[pb:pb + 64, co + 16:co + 32],
                            ob=c0, kb=pb, start=True, stop=True))())
                        if G > 0:
                            P.on("tensor", (lambda fc=fc, pb=pb, ap=ap, G=G,
                                            c0=c0, co=co: lambda e: MM(e,
                                pA[ap][0:32 * G, c0:c0 + 32],
                                kst[G][pb:pb + 64,
                                       fc * 32 * G:(fc + 1) * 32 * G],
                                bR32[pb:pb + 64, co:co + 32],
                                start=True, stop=True))(),
                                inc=(sAsm if G == 3 else None))
                    evw = [(sAsm, 64 * ci + h + 1)]
                    if h == 0 and ci > 0:
                        evw.append((sO, 64 * ci))
                    P.on("scalar", (lambda h=h, ap=ap: lambda e: e.copy(
                        asb[:, h * 128:(h + 1) * 128], pA[ap][:, 0:128]))(),
                        waits=evw, inc=sEvA)
                    P.on("gpsimd", (lambda h=h: lambda e: e.affine_select(
                        asb[:, h * 128:(h + 1) * 128],
                        asb[:, h * 128:(h + 1) * 128],
                        pattern=[[1, 128]], compare_op=ALU.is_ge, fill=0.0,
                        base=-1, channel_multiplier=-1))(),
                        waits=[(sEvA, 64 * ci + h + 1)], inc=sMsk)

                r32v, rcv = v3(bR32), v3(bRc)
                kev, kcv = v3(bKe), v3(bKc)
                P.on("vector", lambda e: e.tensor_copy(
                    rcv[:, :, 0:32], r32v[:, :, 0:32]),
                    waits=[(sAsm, 64 * (ci + 1))], inc=sDrv)
                for (g, Eb) in ((1, E32b), (2, E64b), (3, E96b)):
                    P.on("vector", (lambda g=g, Eb=Eb: lambda e: e.tensor_tensor(
                        rcv[:, :, 32 * g:32 * g + 32],
                        r32v[:, :, 32 * g:32 * g + 32], Eb(32), ALU.mult))(),
                        inc=sDrv)
                P.on("vector", lambda e: e.tensor_copy(
                    kcv[:, :, 96:128], kev[:, :, 96:128]), inc=sDrv)
                for (g, Eb) in ((0, E96b), (1, E64b), (2, E32b)):
                    P.on("vector", (lambda g=g, Eb=Eb: lambda e: e.tensor_tensor(
                        kcv[:, :, 32 * g:32 * g + 32],
                        kev[:, :, 32 * g:32 * g + 32], Eb(32), ALU.mult))(),
                        inc=sDrv)
                P.on("vector", (lambda par=par: lambda e: e.tensor_tensor(
                    v3(bKu), v3(rch[par]), tfb, ALU.mult))(), inc=sDrv)
                P.on("vector", (lambda par=par: lambda e: e.tensor_mul(
                    bRu[:], bKu[:], kch[par][:]))(),
                    waits=[(sDrv, sDrv.n)], inc=sDrv)
                drvM.append(sDrv.n)

                for fc in range(NFC):
                    gg = 32 * ci + fc
                    tp = fc % 2
                    tw = [(sDrv, drvM[ci])] if fc == 0 else []
                    if gg >= 2:
                        tw.append((sTrW, gg - 1))
                    if fc < 2 and ci > 0:
                        tw.append((sTrK, 32 * ci))
                    if fc < 2 and ci == 0:
                        tw.append((sWtrk, 32 * NWC))
                    P.on("tensor", (lambda fc=fc, tp=tp: lambda e: e.transpose(
                        pT[tp][:, 0:128], bRu[:, fc * 128:(fc + 1) * 128],
                        ident[:]))(), waits=tw, inc=sTpw)
                    P.on("scalar", (lambda fc=fc, tp=tp: lambda e: e.copy(
                        bKu[:, fc * 128:(fc + 1) * 128], pT[tp][:, 0:128]))(),
                        waits=[(sTpw, gg + 1)], inc=sTrW)
                for fc in range(NFC):
                    gg = 32 * ci + fc
                    tp = fc % 2
                    tw = [(sTrK, gg - 1)] if gg >= 2 else []
                    if fc < 2:
                        tw.append((sTrW, 32 * (ci + 1)))
                    P.on("tensor", (lambda fc=fc, tp=tp: lambda e: e.transpose(
                        pT[tp][:, 128:256], bKc[:, fc * 128:(fc + 1) * 128],
                        ident[:]))(), waits=tw, inc=sTpk)
                    P.on("scalar", (lambda fc=fc, tp=tp: lambda e: e.copy(
                        bKe[:, fc * 128:(fc + 1) * 128], pT[tp][:, 128:256]))(),
                        waits=[(sTpk, gg + 1)], inc=sTrK)

                P.on("vector", lambda e: e.reduce_sum(
                    cq[:].rearrange("p (h o) -> p h o", o=1),
                    bKu[:].rearrange("p (h d) -> p h d", d=64), AX.X),
                    waits=[(sTrW, 32 * (ci + 1))], inc=sCc)
                cqb = cq[:].rearrange("p (h o) -> p h o", o=1).broadcast_to(
                    [128, 64, 64])
                P.on("vector", (lambda par=par: lambda e: e.tensor_tensor(
                    bKc[:].rearrange("p (h d) -> p h d", d=64),
                    vch[par][:].rearrange("p (h d) -> p h d", d=64),
                    cqb, ALU.mult))(),
                    waits=[(sCc, sCc.n), (sTrK, 32 * (ci + 1)),
                           (sBv[par], 16 * (ci // 2 + 1))], inc=sCc)
                ccM.append(sCc.n)

                for h in range(64):
                    fc, hp = h // 2, h % 2
                    pb = 64 * hp
                    grp = h // 16
                    w4 = [(sMsk, 64 * ci + h + 1)]
                    if h == 0:
                        w4 += [(sTrK, 32 * (ci + 1)), (sStU, 2 * ci)]
                        if ci == 0:
                            w4.append((sWstU, 2 * NWC))
                    if h % 16 == 0 and (ci > 0 or grp > 0):
                        w4.append((sOEv, 4 * ci + grp))
                    if h == 32:
                        w4.append((sStU, 2 * ci + 1))
                    P.on("tensor", (lambda h=h, par=par: lambda e: e.matmul(
                        pO[:, (h % 16) * 64:(h % 16) * 64 + 64],
                        asb[:, h * 128:(h + 1) * 128],
                        vch[par][:, h * 64:(h + 1) * 64],
                        start=True, stop=False))(), waits=w4)
                    P.on("tensor", (lambda fc=fc, pb=pb, h=h: lambda e: e.matmul(
                        pO[:, (h % 16) * 64:(h % 16) * 64 + 64],
                        bRc[pb:pb + 64, fc * 128:(fc + 1) * 128],
                        stb16[pb:pb + 64, fc * 64:(fc + 1) * 64],
                        start=False, stop=True))(), inc=sO)
                    P.on("tensor", (lambda fc=fc, pb=pb, h=h, par=par:
                                    lambda e: MM(e,
                        pS[pb:pb + 64, (fc % 16) * 64:(fc % 16) * 64 + 64],
                        bKe[:, fc * 128 + pb:fc * 128 + pb + 64],
                        vch[par][:, h * 64:(h + 1) * 64],
                        ob=pb, kb=0, start=True, stop=True))(), inc=sS)
                    if h % 16 == 15:
                        evw = [(sO, 64 * ci + 16 * (grp + 1)), (sCc, ccM[ci])]
                        if grp == 0 and ci > 0:
                            evw.append((sObo, 16 * ci))
                        P.on("vector", (lambda grp=grp: lambda e:
                                        e.scalar_tensor_tensor(
                            obch[:, grp * 1024:(grp + 1) * 1024],
                            bKc[:, grp * 1024:(grp + 1) * 1024], 1.0,
                            pO[:], ALU.bypass, ALU.add))(), waits=evw, inc=sOEv)
                    if h in (31, 63):
                        half = 0 if h == 31 else 1
                        hs = half * 1024
                        stv = stf[:, hs:hs + 1024].rearrange(
                            "p (fc d) -> p fc d", d=64)
                        e128h = eE[128][:, half * 16:half * 16 + 16].rearrange(
                            "p (fc o) -> p fc o", o=1).broadcast_to([128, 16, 64])
                        P.on("vector", (lambda stv=stv, e128h=e128h: lambda e:
                                        e.tensor_tensor(stv, stv, e128h,
                                                        ALU.mult))(),
                            waits=[(sS, 64 * ci + 32 * (half + 1)),
                                   (sO, 64 * ci + 32 * (half + 1))], inc=sStC)
                        P.on("vector", (lambda hs=hs: lambda e: e.tensor_add(
                            stf[:, hs:hs + 1024], stf[:, hs:hs + 1024], pS[:]))(),
                            waits=[(sStC, sStC.n)], inc=sStC)
                        P.on("vector", (lambda hs=hs: lambda e: e.tensor_copy(
                            stb16[:, hs:hs + 1024], stf[:, hs:hs + 1024]))(),
                            waits=[(sStC, sStC.n)], inc=sStU)
                P.on("sync", (lambda ci=ci: lambda e: e.dma_start(
                    out=oh[ci * 128:(ci + 1) * 128, :], in_=obch[:]))(),
                    waits=[(sOEv, 4 * (ci + 1))], inc=sObo, amt=16)
            sb_done = [(sObo, sObo.n), (sStU, sStU.n), (sMsk, sMsk.n)]
            P.flush(block)

        with ExitStack() as stc:
            och = [sbuf(f"och{i}", [128, D], BF16, stc) for i in range(2)]
            gch = [sbuf(f"gch{i}", [128, D], BF16, stc) for i in range(2)]
            u1 = sbuf("u1", [128, D], F32, stc)
            u2 = sbuf("u2", [128, D], F32, stc)
            gx = sbuf("gx", [128, D], BF16, stc)
            gxc = [sbuf(f"gxc{i}", [128, D], BF16, stc) for i in range(2)]
            gnwb = sbuf("gnwb", [128, D], F32, stc)
            gnbb = sbuf("gnbb", [128, D], F32, stc)
            mu = sbuf("mu", [128, 64], F32, stc)
            vr = sbuf("vr", [128, 64], F32, stc)
            m2 = sbuf("m2", [128, 64], F32, stc)
            pC = [psum_t(f"pC{i}", [128, 1024], BF16, stc) for i in range(2)]
            sCin = [Sem("sCin0"), Sem("sCin1")]
            sCg = [Sem("sCg0"), Sem("sCg1")]
            sCs = Sem("sCs")
            sCt = Sem("sCt")
            sCe = Sem("sCe")
            sCo = [Sem("sCo0"), Sem("sCo1")]
            sCb = Sem("sCb")

            for (src_row, dst_bc) in ((gnwt, gnwb), (gnbt, gnbb)):
                P.on("sync", (lambda src_row=src_row, dst_bc=dst_bc: lambda e:
                              e.dma_start(
                    out=dst_bc[:],
                    in_=src_row.rearrange("(o d) -> o d", o=1).broadcast_to(
                        [128, D])))(), waits=sb_done, inc=sCb, amt=16)
            cb_done = sCb.n

            csM = []
            for tc in range(NCH):
                par = tc % 2
                ldw = [(sCb, cb_done)] if tc < 2 else [(sCs, csM[tc - 2])]
                P.on("sync", (lambda tc=tc, par=par: lambda e: e.dma_start(
                    out=och[par][:], in_=oh[tc * 128:(tc + 1) * 128, :]))(),
                    waits=ldw, inc=sCin[par], amt=16)
                P.on("sync", (lambda tc=tc, par=par: lambda e: e.dma_start(
                    out=gch[par][:], in_=gth[tc * 128:(tc + 1) * 128, :]))(),
                    inc=sCg[par], amt=16)
                ov = och[par][:].rearrange("p (h d) -> p h d", d=64)
                mub = mu[:].rearrange("p (h o) -> p h o", o=1).broadcast_to(
                    [128, 64, 64])
                vrb = vr[:].rearrange("p (h o) -> p h o", o=1).broadcast_to(
                    [128, 64, 64])
                u13 = u1[:].rearrange("p (h d) -> p h d", d=64)
                u23 = u2[:].rearrange("p (h d) -> p h d", d=64)
                P.on("vector", (lambda ov=ov: lambda e: e.reduce_sum(
                    mu[:].rearrange("p (h o) -> p h o", o=1), ov, AX.X))(),
                    waits=[(sCin[par], 16 * (tc // 2 + 1))], inc=sCs)
                P.on("vector", lambda e: e.tensor_scalar_mul(mu[:], mu[:],
                                                             1.0 / 64),
                     waits=[(sCs, sCs.n)], inc=sCs)
                P.on("vector", (lambda ov=ov, u13=u13, mub=mub: lambda e:
                                e.tensor_tensor(u13, ov, mub, ALU.subtract))(),
                    waits=[(sCs, sCs.n)], inc=sCs)
                P.on("scalar", lambda e: e.activation(m2[:] if False else
                                                      u2[:], u1[:], AF.Square),
                     waits=[(sCs, sCs.n)], inc=sCs)
                P.on("vector", lambda e: e.reduce_sum(
                    vr[:].rearrange("p (h o) -> p h o", o=1),
                    u2[:].rearrange("p (h d) -> p h d", d=64), AX.X),
                    waits=[(sCs, sCs.n)], inc=sCs)
                P.on("vector", lambda e: e.tensor_scalar_mul(vr[:], vr[:],
                                                             1.0 / 64),
                     waits=[(sCs, sCs.n)], inc=sCs)
                P.on("scalar", lambda e: e.activation(vr[:], vr[:], AF.Ln,
                                                      bias=epsc[:]),
                     waits=[(sCs, sCs.n)], inc=sCs)
                P.on("scalar", lambda e: e.activation(vr[:], vr[:], AF.Exp,
                                                      scale=nhc[:]),
                     waits=[(sCs, sCs.n)], inc=sCs)
                P.on("vector", (lambda u13=u13, u23=u23, vrb=vrb: lambda e:
                                e.tensor_tensor(u23, u13, vrb, ALU.mult))(),
                     waits=[(sCs, sCs.n)], inc=sCs)
                P.on("vector", lambda e: e.tensor_mul(u1[:], u2[:], gnwb[:]),
                     waits=[(sCs, sCs.n)], inc=sCs)
                P.on("vector", lambda e: e.tensor_add(u2[:], u1[:], gnbb[:]),
                     waits=[(sCs, sCs.n)], inc=sCs)
                P.on("vector", (lambda par=par, tc=tc: lambda e: e.tensor_mul(
                    gx[:], u2[:], gch[par][:]))(),
                    waits=[(sCg[par], 16 * (tc // 2 + 1)), (sCt, 32 * tc)],
                    inc=sCs)
                csM.append(sCs.n)
                for fc in range(NFC):
                    gg = 32 * tc + fc
                    tp = fc % 2
                    tw = [(sCs, csM[tc])] if fc == 0 else []
                    if gg >= 2:
                        tw.append((sCe, gg - 1))
                    P.on("tensor", (lambda fc=fc, tp=tp: lambda e: e.transpose(
                        pC[tp][:, 0:128],
                        gx[:, fc * 128:(fc + 1) * 128], ident[:]))(),
                        waits=tw, inc=sCt)
                    ew = [(sCt, gg + 1)]
                    if fc == 0:
                        ew.append((sCo[par], 16 * (tc // 2)))
                    P.on("scalar", (lambda fc=fc, tp=tp, par=par: lambda e:
                                    e.copy(gxc[par][:, fc * 128:(fc + 1) * 128],
                                           pC[tp][:, 0:128]))(),
                        waits=ew, inc=sCe)
                P.on("sync", (lambda tc=tc, par=par: lambda e: e.dma_start(
                    out=gxh[tc].rearrange("(fc p) t -> p fc t", p=128),
                    in_=gxc[par][:].rearrange("p (fc t) -> p fc t", t=128)))(),
                    waits=[(sCe, 32 * (tc + 1))], inc=sCo[par], amt=16)
            sc_done = [(sCo[0], sCo[0].n), (sCo[1], sCo[1].n)]
            P.flush(block)

        with ExitStack() as std_:
            gxf = sbuf("gxf", [128, NFC * T], BF16, std_)
            wsl = [sbuf(f"wslD{i}", [128, NFC * 512], BF16, std_)
                   for i in range(2)]
            ares = [sbuf(f"ares{i}", [128, 512], F32, std_) for i in range(2)]
            xrs = [sbuf(f"xrs{i}", [128, 512], F32, std_) for i in range(2)]
            frsd = sbuf("frsd", [128, 512], F32, std_)
            pD = [psum_t(f"pD{i}", [128, 512], F32, std_) for i in range(4)]
            sDin = Sem("sDin")
            sDe2 = sDin
            sDW = [Sem("sDW0"), Sem("sDW1")]
            sDx = [Sem("sDx0"), Sem("sDx1")]
            sDg = Sem("sDg")
            sDe = Sem("sDe")
            sDo = [Sem("sDo0"), Sem("sDo1")]

            for ci in range(NCH):
                P.on("sync", (lambda ci=ci: lambda e: e.dma_start(
                    out=gxf[:].rearrange("p (jc t) -> p jc t", t=T)
                        [:, :, ci * 128:(ci + 1) * 128],
                    in_=gxh[ci].rearrange("(jc p) t -> p jc t", p=128)))(),
                    waits=(sc_done if ci == 0 else []), inc=sDin, amt=16)
            gxf_done = sDin.n
            if gxfd is not None:
                P.on("sync", lambda e: e.dma_start(out=gxfd, in_=gxf[:]),
                     waits=[(sDin, gxf_done)], inc=sDin, amt=16)
                gxf_done = sDin.n
            gD = [0]
            for isl in range(8):
                par = isl % 2
                wldw = [(sDg, 16 * (isl - 1))] if isl >= 2 else list(sc_done)
                P.on("gpsimd", (lambda isl=isl, par=par: lambda e: e.dma_start(
                    out=wsl[par][:].rearrange("p (jc i) -> p jc i", i=512),
                    in_=wT["o"][isl]))(),
                    waits=wldw, inc=sDW[par], amt=16)
                for tc in range(NCH):
                    g = gD[0]
                    xpar = g % 2
                    P.on("sync", (lambda tc=tc, isl=isl, xpar=xpar: lambda e:
                                  e.dma_start(
                        out=xrs[xpar][:],
                        in_=xp[1 + WU + tc * 128:1 + WU + (tc + 1) * 128,
                               isl * 512:(isl + 1) * 512]))(),
                        waits=[(sDe, g - 1)] if g >= 2 else [], inc=sDx[xpar],
                        amt=16)
                    mmw = [(sDW[par], 16 * (isl // 2 + 1))]
                    if g >= 4:
                        mmw.append((sDe, g - 3))
                    if tc == 0:
                        mmw.append((sDin, gxf_done))
                    for jc in range(NFC):
                        P.on("tensor", (lambda tc=tc, isl=isl, jc=jc, par=par,
                                        g=g: lambda e: e.matmul(
                            pD[g % 4][:],
                            gxf[:, jc * T + tc * 128:jc * T + tc * 128 + 128],
                            wsl[par][:, jc * 512:(jc + 1) * 512],
                            start=(jc == 0), stop=(jc == NFC - 1)))(),
                            waits=(mmw if jc == 0 else []),
                            inc=(sDg if jc == NFC - 1 else None))
                    evw = [(sDg, sDg.n), (sDx[xpar], 16 * (g // 2 + 1))]
                    if g >= 2:
                        evw.append((sDo[g % 2], 16 * (g // 2)))
                    if g == 0 and mmd is not None:
                        base2 = sDe2.n
                        P.on("vector", lambda e: e.tensor_copy(
                            frsd[:], pD[0][:]), waits=list(evw), inc=sDe2)
                        P.on("sync", (lambda base2=base2: lambda e: e.dma_start(
                            out=mmd, in_=frsd[:]))(),
                             waits=[(sDe2, base2 + 1)], inc=sDe2, amt=16)
                        P.on("sync", lambda e: e.dma_start(out=xrd,
                                                           in_=xrs[0][:]),
                             waits=[(sDx[0], 16)], inc=sDe2, amt=16)
                    P.on("vector", (lambda g=g, xpar=xpar: lambda e:
                                    e.scalar_tensor_tensor(
                        ares[g % 2][:], xrs[xpar][:], 1.0, pD[g % 4][:],
                        ALU.bypass, ALU.add))(),
                        waits=list(evw) + ([(sDe2, sDe2.n)] if (g == 0 and mmd is not None) else []),
                        inc=sDe)
                    P.on("sync", (lambda tc=tc, isl=isl, g=g: lambda e:
                                  e.dma_start(
                        out=arh[tc * 128:(tc + 1) * 128,
                                isl * 512:(isl + 1) * 512],
                        in_=ares[g % 2][:]))(),
                        waits=[(sDe, sDe.n)], inc=sDo[g % 2], amt=16)
                    gD[0] += 1
            sd_done = [(sDo[0], sDo[0].n), (sDo[1], sDo[1].n)]
            P.flush(block)

        def ln_stage(src_h, dst_h, dst_bf_h, waits0, tag, final_out=None):
            with ExitStack() as stf_:
                art = [sbuf(f"art{tag}{i}", [128, D], F32, stf_)
                       for i in range(2)]
                sq = sbuf(f"sq{tag}", [128, D], F32, stf_)
                xlt = sbuf(f"xlt{tag}", [128, D], F32, stf_)
                xbt = sbuf(f"xbt{tag}", [128, D], BF16, stf_)
                xcl = [sbuf(f"xcl{tag}{i}", [128, D], BF16, stf_)
                       for i in range(2)]
                mcol = sbuf(f"mcol{tag}", [128, 1], F32, stf_)
                vcol = sbuf(f"vcol{tag}", [128, 1], F32, stf_)
                m2c = sbuf(f"m2c{tag}", [128, 1], F32, stf_)
                pL = [psum_t(f"pL{tag}{i}", [128, 1024], BF16, stf_)
                      for i in range(2)]
                sLin = [Sem(f"sLin{tag}0"), Sem(f"sLin{tag}1")]
                sLs = Sem(f"sLs{tag}")
                if final_out is None:
                    sLt = Sem(f"sLt{tag}")
                    sLe = Sem(f"sLe{tag}")
                    sLo = [Sem(f"sLo{tag}0"), Sem(f"sLo{tag}1"),
                           Sem(f"sLo{tag}2")]
                else:
                    sLt = sLe = sLs
                    sLo = [sLs, sLs, Sem(f"sLo{tag}2")]
                lsM = []
                for tc in range(NCH):
                    par = tc % 2
                    ldw = list(waits0) if tc < 2 else [(sLs, lsM[tc - 2])]
                    P.on("sync", (lambda tc=tc, par=par: lambda e: e.dma_start(
                        out=art[par][:],
                        in_=src_h[tc * 128:(tc + 1) * 128, :]))(),
                        waits=ldw, inc=sLin[par], amt=16)
                    P.on("vector", (lambda par=par: lambda e: e.reduce_sum(
                        mcol[:].rearrange("p (a o) -> p a o", o=1),
                        art[par][:].rearrange("p (a d) -> p a d", a=1), AX.X))(),
                        waits=[(sLin[par], 16 * (tc // 2 + 1))], inc=sLs)
                    P.on("vector", lambda e: e.tensor_scalar_mul(
                        mcol[:], mcol[:], 1.0 / D),
                        waits=[(sLs, sLs.n)], inc=sLs)
                    P.on("vector", (lambda par=par: lambda e:
                                    e.tensor_scalar_sub(
                        xlt[:], art[par][:], mcol[:]))(),
                        waits=[(sLs, sLs.n), (sLo[2], 16 * tc)], inc=sLs)
                    P.on("scalar", lambda e: e.activation(
                        sq[:], xlt[:], AF.Square),
                        waits=[(sLs, sLs.n)], inc=sLs)
                    P.on("vector", lambda e: e.reduce_sum(
                        vcol[:].rearrange("p (a o) -> p a o", o=1),
                        sq[:].rearrange("p (a d) -> p a d", a=1), AX.X),
                        waits=[(sLs, sLs.n)], inc=sLs)
                    P.on("vector", lambda e: e.tensor_scalar_mul(
                        vcol[:], vcol[:], 1.0 / D),
                        waits=[(sLs, sLs.n)], inc=sLs)
                    P.on("scalar", lambda e: e.activation(vcol[:], vcol[:],
                                                          AF.Ln, bias=epsc[:]),
                         waits=[(sLs, sLs.n)], inc=sLs)
                    P.on("scalar", lambda e: e.activation(vcol[:], vcol[:],
                                                          AF.Exp,
                                                          scale=nhc[:]),
                         waits=[(sLs, sLs.n)], inc=sLs)
                    P.on("vector", lambda e: e.tensor_scalar_mul(
                        xlt[:], xlt[:], vcol[:]),
                        waits=[(sLs, sLs.n)], inc=sLs)
                    ncount = sLs.n
                    if final_out is None:
                        P.on("sync", (lambda tc=tc: lambda e: e.dma_start(
                            out=dst_h[tc * 128:(tc + 1) * 128, :],
                            in_=xlt[:]))(),
                            waits=[(sLs, ncount), (sLo[2], 16 * tc)],
                            inc=sLo[2], amt=16)
                        P.on("vector", (lambda tc=tc: lambda e: e.tensor_copy(
                            xbt[:], xlt[:]))(),
                            waits=[(sLs, ncount), (sLt, 32 * tc)], inc=sLs)
                        bcount = sLs.n
                        for fc in range(NFC):
                            gg = 32 * tc + fc
                            tp = fc % 2
                            tw = [(sLs, bcount)] if fc == 0 else []
                            if gg >= 2:
                                tw.append((sLe, gg - 1))
                            P.on("tensor", (lambda fc=fc, tp=tp: lambda e:
                                            e.transpose(
                                pL[tp][:, 0:128],
                                xbt[:, fc * 128:(fc + 1) * 128], ident[:]))(),
                                waits=tw, inc=sLt)
                            ew = [(sLt, gg + 1)]
                            if fc == 0:
                                ew.append((sLo[par], 16 * (tc // 2)))
                            P.on("scalar", (lambda fc=fc, tp=tp, par=par:
                                            lambda e: e.copy(
                                xcl[par][:, fc * 128:(fc + 1) * 128],
                                pL[tp][:, 0:128]))(), waits=ew, inc=sLe)
                        P.on("sync", (lambda tc=tc, par=par: lambda e:
                                      e.dma_start(
                            out=dst_bf_h[tc].rearrange("(fc p) t -> p fc t",
                                                       p=128),
                            in_=xcl[par][:].rearrange("p (fc t) -> p fc t",
                                                      t=128)))(),
                            waits=[(sLe, 32 * (tc + 1))], inc=sLo[par], amt=16)
                        lsM.append(bcount)
                    else:
                        P.on("sync", (lambda tc=tc: lambda e: e.dma_start(
                            out=final_out[tc * 128:(tc + 1) * 128, :],
                            in_=xlt[:]))(),
                            waits=[(sLs, ncount), (sLo[2], 16 * tc)],
                            inc=sLo[2], amt=16)
                        lsM.append(ncount)
                done = [(sLo[0], sLo[0].n), (sLo[1], sLo[1].n),
                        (sLo[2], sLo[2].n)]
                P.on("sync", lambda e: None, waits=done)
                P.flush(block)
            return done

        if ard is not None:
            P.on("sync", lambda e: e.dma_start(out=ard, in_=arh),
                 waits=sd_done, inc=sQo[0], amt=16)
            P.flush(block)
        sf_done = ln_stage(arh, xlh, xlbh, sd_done, "F")
        if xld is not None:
            P.on("sync", lambda e: e.dma_start(out=xld, in_=xlh),
                 waits=sf_done, inc=sQo[0], amt=16)
            P.flush(block)

        with ExitStack() as stg:
            xlf = sbuf("xlf", [128, NFC * T], BF16, stg)
            wslg = [sbuf(f"wslG{i}", [128, NFC * 512], BF16, stg)
                    for i in range(2)]
            frs = [sbuf(f"frs{i}", [128, 512], F32, stg) for i in range(2)]
            xr2 = [sbuf(f"xr2{i}", [128, 512], F32, stg) for i in range(2)]
            fbc = sbuf("fbc", [128, 512], F32, stg)
            pG = [psum_t(f"pG{i}", [128, 512], F32, stg) for i in range(4)]
            sGin = Sem("sGin")
            sGW = [Sem("sGW0"), Sem("sGW1")]
            sGx = [Sem("sGx0"), Sem("sGx1")]
            sGg = Sem("sGg")
            sGe = Sem("sGe")
            sGb = Sem("sGb")
            sGo = [Sem("sGo0"), Sem("sGo1")]

            for ci in range(NCH):
                P.on("sync", (lambda ci=ci: lambda e: e.dma_start(
                    out=xlf[:].rearrange("p (jc t) -> p jc t", t=T)
                        [:, :, ci * 128:(ci + 1) * 128],
                    in_=xlbh[ci].rearrange("(jc p) t -> p jc t", p=128)))(),
                    waits=(sf_done if ci == 0 else []), inc=sGin, amt=16)
            xlf_done = sGin.n
            gG = [0]
            for isl in range(8):
                par = isl % 2
                wldw = [(sGg, 16 * (isl - 1))] if isl >= 2 else list(sf_done)
                for sp in range(8):
                    P.on("sync", (lambda isl=isl, par=par, sp=sp: lambda e:
                                  e.dma_start(
                        out=wslg[par][:].rearrange(
                            "p (jc i) -> p jc i", i=512)[:, :,
                                                         sp * WQS:(sp + 1) * WQS],
                        in_=wqh[isl * 8 + sp].rearrange(
                            "(jc p) n -> p jc n", p=128)))(),
                        waits=(wldw if sp == 0 else []), inc=sGW[par], amt=16)
                fcw = list(sf_done) if isl == 0 else [(sGe, 2 * 16 * isl)]
                P.on("sync", (lambda isl=isl: lambda e: e.dma_start(
                    out=fbc[:],
                    in_=fbi.rearrange("(o d) -> o d", o=1).broadcast_to(
                        [128, D])[:, isl * 512:(isl + 1) * 512]))(),
                    waits=fcw, inc=sGb, amt=16)
                fb_ready = sGb.n
                for tc in range(NCH):
                    g = gG[0]
                    xpar = g % 2
                    P.on("sync", (lambda tc=tc, isl=isl, xpar=xpar: lambda e:
                                  e.dma_start(
                        out=xr2[xpar][:],
                        in_=xlh[tc * 128:(tc + 1) * 128,
                                isl * 512:(isl + 1) * 512]))(),
                        waits=[(sGe, 2 * (g - 1))] if g >= 2 else [],
                        inc=sGx[xpar], amt=16)
                    mmw = [(sGW[par], 128 * (isl // 2 + 1))]
                    if g >= 4:
                        mmw.append((sGe, 2 * (g - 3)))
                    if tc == 0:
                        mmw.append((sGin, xlf_done))
                    for jc in range(NFC):
                        P.on("tensor", (lambda tc=tc, jc=jc, par=par, g=g:
                                        lambda e: e.matmul(
                            pG[g % 4][:],
                            xlf[:, jc * T + tc * 128:jc * T + tc * 128 + 128],
                            wslg[par][:, jc * 512:(jc + 1) * 512],
                            start=(jc == 0), stop=(jc == NFC - 1)))(),
                            waits=(mmw if jc == 0 else []),
                            inc=(sGg if jc == NFC - 1 else None))
                    evw = [(sGg, sGg.n), (sGx[xpar], 16 * (g // 2 + 1)),
                           (sGb, fb_ready)]
                    if g >= 2:
                        evw.append((sGo[g % 2], 16 * (g // 2)))
                    P.on("vector", (lambda g=g: lambda e:
                                    e.scalar_tensor_tensor(
                        frs[g % 2][:], fbc[:], 1.0, pG[g % 4][:],
                        ALU.bypass, ALU.add))(), waits=evw, inc=sGe)
                    P.on("vector", (lambda g=g, xpar=xpar: lambda e:
                                    e.tensor_add(frs[g % 2][:], frs[g % 2][:],
                                                 xr2[xpar][:]))(),
                        waits=[(sGe, sGe.n)], inc=sGe)
                    P.on("sync", (lambda tc=tc, isl=isl, g=g: lambda e:
                                  e.dma_start(
                        out=arh[tc * 128:(tc + 1) * 128,
                                isl * 512:(isl + 1) * 512],
                        in_=frs[g % 2][:]))(),
                        waits=[(sGe, sGe.n)], inc=sGo[g % 2], amt=16)
                    gG[0] += 1
            sg_done = [(sGo[0], sGo[0].n), (sGo[1], sGo[1].n)]
            P.flush(block)

        ln_stage(arh, None, None, sg_done, "H", final_out=out)

        for (s, v) in P.all_waits:
            assert v <= s.n, f"wait {v} > emitted {s.n} on sem (hang)"

    return nc


def kernel(**inputs):
    x = inputs["L0_attn_hidden"]
    B, S, _ = x.shape
    xpad = np.concatenate([np.zeros((B, WU + 1, D), np.float32),
                           np.asarray(x, np.float32)], axis=1)
    WSLAB = {"k": 128, "r": 128, "v": 256, "g": 256, "o": 512}
    w = {}
    for (m, n) in (("k", "key"), ("v", "value"), ("r", "receptance"),
                   ("g", "gate"), ("o", "output")):
        wi = WSLAB[m]
        A = np.asarray(inputs[f"L0_attn_w_{n}"], np.float32).T
        w[m] = np.ascontiguousarray(
            A.reshape(NFC, 128, D // wi, wi).transpose(2, 1, 0, 3))
    tmix = np.ascontiguousarray(np.stack([
        np.asarray(inputs["L0_attn_time_mix_key"]),
        np.asarray(inputs["L0_attn_time_mix_value"]),
        np.asarray(inputs["L0_attn_time_mix_receptance"]),
        np.asarray(inputs["L0_attn_time_mix_gate"])], axis=0)).astype(np.float32)
    tdec = np.ascontiguousarray(
        np.asarray(inputs["L0_attn_time_decay"]).reshape(-1)).astype(np.float32)
    tfaa = np.ascontiguousarray(
        np.asarray(inputs["L0_attn_time_faaaa"]).reshape(-1)).astype(np.float32)
    gnw = np.ascontiguousarray(np.asarray(inputs["L0_attn_gn_weight"])).astype(
        np.float32)
    gnb = np.ascontiguousarray(np.asarray(inputs["L0_attn_gn_bias"])).astype(
        np.float32)
    WQS = 64
    wf = np.asarray(inputs["L0_ffn_kernel"], np.float32)
    wfn = np.ascontiguousarray(
        wf.reshape(128, 32, D // WQS, WQS).transpose(2, 0, 1, 3))
    fbi = np.ascontiguousarray(np.asarray(inputs["L0_ffn_bias"])).astype(
        np.float32)

    nc = build_program()
    in_maps = []
    for c in range(NCORES):
        b, q = c // 4, c % 4
        xs = np.ascontiguousarray(xpad[b, q * T:q * T + TPW, :])
        m = {"xp": xs, "wfn": wfn, "tmix": tmix, "tdec": tdec, "tfaa": tfaa,
             "gnwt": gnw, "gnbt": gnb, "fbi": fbi}
        for k2 in ("k", "v", "r", "g", "o"):
            m[f"w{k2}T"] = w[k2]
        in_maps.append(m)
    res = run_bass_kernel_spmd(nc, in_maps, core_ids=list(range(NCORES)))
    outp = np.empty((B, S, D), np.float32)
    for c in range(NCORES):
        b, q = c // 4, c % 4
        outp[b, q * T:(q + 1) * T, :] = res.results[c]["out"]
    return outp
